# version 13 (final)
# baseline (speedup 1.0000x reference)
import os
from contextlib import ExitStack

import numpy as np
import ml_dtypes

import concourse.bass as bass
import concourse.mybir as mybir
from concourse.bass_utils import run_bass_kernel_spmd

F32 = mybir.dt.float32
BF16 = mybir.dt.bfloat16
I32 = mybir.dt.int32
U8 = mybir.dt.uint8
AF = mybir.ActivationFunctionType
ALU = mybir.AluOpType

D = 1024
DI = 2048
DTR = 64
DS = 16
DFF = 4096
H3 = 3072
NK = D // 128
NKI = DI // 128
NKF = DFF // 128
NKH = H3 // 128
CW = 1024
TCS = 4096
EPS = 1e-5

ENGS = ("sync", "vector", "scalar", "tensor", "gpsimd")

VC_CONVW = 0
VC_CONVB = 64
VC_DTB = 80
VC_D = 96
VC_LNG = 112
VC_LNB = 120
VC_FB1 = 128
VC_FB2 = 160
VC_MUR = 168
VC_MUK = 176
VC_N = 192


class Prog:

    NPHYS = 40

    def __init__(self, nc):
        self.nc = nc
        self.q = {e: [] for e in ENGS}
        self.phys = [nc.alloc_semaphore(name=f"sp{i}") for i in range(self.NPHYS)]
        self.physcnt = [0] * self.NPHYS
        self.logmap = {}
        self.next_phys = 0
        self.prod = {}
        self.waited = {e: {} for e in ENGS}
        self.nwaits = 0

    def new_scope(self):
        self.logmap = {}
        self.next_phys = 0

    def _slot(self, name):
        if name not in self.logmap:
            self.logmap[name] = self.next_phys
            self.next_phys += 1
            assert self.next_phys <= self.NPHYS, "semaphore pool exhausted"
        return self.logmap[name]

    def op(self, eng, fn, after=(), sig=None, dn=1, dma=False, force_after=()):
        for ev in list(force_after) + [("_F", e) for e in ()]:
            if ev is None:
                continue
            si, val = ev
            if val <= 0:
                continue
            if self.waited[eng].get(si, 0) >= val:
                continue
            self.q[eng].append(("w", si, val))
            self.waited[eng][si] = val
            self.nwaits += 1
        for ev in after:
            if ev is None:
                continue
            si, val = ev
            if val <= 0:
                continue
            if self.prod.get(si) == eng:
                continue
            if self.waited[eng].get(si, 0) >= val:
                continue
            self.q[eng].append(("w", si, val))
            self.waited[eng][si] = val
            self.nwaits += 1
        ev_out = None
        if sig is not None:
            si = self._slot(sig)
            kind = "dma" if dma else eng
            if si not in self.prod:
                self.prod[si] = kind
            elif self.prod[si] != kind:
                self.prod[si] = "mixed"
            self.physcnt[si] += dn
            ev_out = (si, self.physcnt[si])
        if fn is not None or sig is not None:
            self.q[eng].append(("o", fn, (self.phys[ev_out[0]], dn) if sig else None))
        return ev_out

    def dma(self, eng, out, in_, after=(), sig="dma"):
        def fn(e, _o=out, _i=in_):
            return e.dma_start(out=_o, in_=_i)
        return self.op(eng, fn, after=after, sig=sig, dn=16, dma=True)

    def val(self, sig):
        si = self._slot(sig)
        return (si, self.physcnt[si])

    def barrier(self):
        evs = []
        for e in ENGS:
            if e in ("vector", "gpsimd"):
                evs.append(self.op(e, lambda en: en.engine_nop(), sig="bar"))
            else:
                evs.append(self.op(e, lambda en: en.nop(), sig="bar"))
        last = self.val("bar")
        for e in ENGS:
            self.op(e, None, after=[last])

    def emit(self):
        with self.nc.Block() as block:
            for e in ENGS:
                q = self.q[e]

                def runner(eng, _q=q):
                    for it in _q:
                        if it[0] == "w":
                            eng.wait_ge(self.phys[it[1]], it[2])
                        else:
                            fn, inc = it[1], it[2]
                            inst = fn(eng) if fn is not None else None
                            if inc is not None:
                                inst.then_inc(inc[0], inc[1])
                getattr(block, e)(runner)

    def n_ops(self):
        return sum(len(q) for q in self.q.values())


class Mem:
    def __init__(self, nc, fused, reads=(), writes=()):
        self.nc = nc
        self.fused = fused
        self.reads = set(reads)
        self.writes = set(writes)
        self.t = {}

    def d(self, name, shape, dtype):
        if name in self.t:
            return self.t[name]
        if name in self.reads:
            kind = "ExternalInput"
        elif name in self.writes:
            kind = "ExternalOutput"
        else:
            kind = "Internal"
        self.t[name] = self.nc.dram_tensor(name, list(shape), dtype, kind=kind).ap()
        return self.t[name]


def sbuf(stack, nc, name, shape, dtype):
    return stack.enter_context(nc.sbuf_tensor(name, list(shape), dtype)).ap()


def psum(stack, nc, name, shape):
    return stack.enter_context(nc.psum_tensor(name, list(shape), F32)).ap()


def alloc_ln_state(stack, nc, pref):
    return dict(
        sq=sbuf(stack, nc, pref + "lnsq", (128, CW), F32),
        p_s1=psum(stack, nc, pref + "ps1", (1, CW)),
        p_s2=psum(stack, nc, pref + "ps2", (1, CW)),
        p_bc=psum(stack, nc, pref + "pbc", (128, 512)),
        smu=sbuf(stack, nc, pref + "smu", (1, CW), F32),
        svar=sbuf(stack, nc, pref + "svar", (1, CW), F32),
        rbc=sbuf(stack, nc, pref + "rbc", (128, CW), F32),
        mbc=sbuf(stack, nc, pref + "mbc", (128, CW), F32),
        ones_c=sbuf(stack, nc, pref + "onesc", (128, 1), F32),
        ones_r=sbuf(stack, nc, pref + "onesr", (1, 128), F32),
        cst=sbuf(stack, nc, pref + "cst", (128, 4), F32),
    )


def init_ln_state(p, st):
    p.op("vector", lambda e: e.memset(st["ones_c"][:], 1.0))
    p.op("vector", lambda e: e.memset(st["cst"][:, 0:1], EPS))
    p.op("vector", lambda e: e.memset(st["cst"][:, 1:2], -0.5))
    p.op("vector", lambda e: e.memset(st["cst"][:, 2:3], 1.0))
    p.op("vector", lambda e: e.memset(st["cst"][:, 3:4], 0.5))
    return p.op("vector", lambda e: e.memset(st["ones_r"][:], 1.0), sig="ones")


def emit_ln_stats(p, st, tag, src_tiles, nkc, ndim, after_src):
    sq, p_s1, p_s2, p_bc = st["sq"], st["p_s1"], st["p_s2"], st["p_bc"]
    smu, svar = st["smu"], st["svar"]
    srstd = svar
    rbc, mbc = st["rbc"], st["mbc"]
    ones_c, ones_r = st["ones_c"], st["ones_r"]
    Tg = tag
    ev_statd_prev = p.val(Tg + "std")
    for k in range(nkc):
        src = src_tiles(k)
        ev_sq = p.op("scalar",
                     lambda e, _s=src: e.activation(sq[:], _s, AF.Square),
                     after=list(after_src) + [p.val(Tg + "sqf")],
                     sig=Tg + "sq")
        for h in range(2):
            sl = slice(h * 512, (h + 1) * 512)
            p.op("tensor",
                 lambda e, _s=src, _sl=sl, _k=k: e.matmul(
                     p_s1[:, _sl], ones_c[:], _s[:, _sl],
                     start=(_k == 0), stop=(_k == nkc - 1)),
                 after=(list(after_src) + [ev_statd_prev]) if (k == 0 and h == 0) else [])
            p.op("tensor",
                 lambda e, _sl=sl, _k=k: e.matmul(
                     p_s2[:, _sl], ones_c[:], sq[:, _sl],
                     start=(_k == 0), stop=(_k == nkc - 1)),
                 after=[ev_sq],
                 sig=(Tg + "sqf") if h == 1 else None)
    ev_s12 = p.val(Tg + "sqf")
    inv = 1.0 / ndim
    ev_std = None
    for h in range(2):
        sl = slice(h * 512, (h + 1) * 512)
        p.op("vector", lambda e, _sl=sl: e.tensor_scalar_mul(smu[:, _sl], p_s1[:, _sl], inv),
             after=[ev_s12])
        p.op("vector", lambda e, _sl=sl: e.tensor_mul(svar[:, _sl], smu[:, _sl], smu[:, _sl]))
        ev_std = p.op("vector", lambda e, _sl=sl: e.scalar_tensor_tensor(
            svar[:, _sl], p_s2[:, _sl], inv, svar[:, _sl], ALU.mult, ALU.subtract),
            sig=(Tg + "std") if h == 1 else None)
    cst = st["cst"]
    p.op("scalar", lambda e: e.activation(srstd[:], svar[:], AF.Ln, bias=cst[0:1, 0:1]),
         after=[ev_std])
    ev_rs = p.op("scalar", lambda e: e.activation(srstd[:], srstd[:], AF.Exp,
                                                  scale=cst[0:1, 1:2]),
                 sig=Tg + "rs")
    ev_mu_ok = p.val(Tg + "std")
    ev_bcc = None
    for (vsrc, dst, evp) in ((srstd, rbc, ev_rs), (smu, mbc, ev_mu_ok)):
        for h in range(2):
            sl = slice(h * 512, (h + 1) * 512)
            ev_bm = p.op("tensor", lambda e, _v=vsrc, _sl=sl: e.matmul(
                p_bc[:], ones_r[:], _v[:, _sl], start=True, stop=True),
                after=[evp, p.val(Tg + "bcc")], sig=Tg + "bcm")
            ev_bcc = p.op("vector", lambda e, _d=dst, _sl=sl: e.tensor_copy(
                _d[:, _sl], p_bc[:]), after=[ev_bm], sig=Tg + "bcc")
    return ev_bcc


PF1_IO = dict(reads=["x_fm", "w_in"], writes=["xc_pre", "silu_z"])


def emit_pf1(p, m, T):
    nc = p.nc
    p.new_scope()
    NCH = T // CW
    x_fm = m.d("x_fm", (D, T), F32)
    w_in = m.d("w_in", (D, 2 * DI), F32)
    xc_pre = m.d("xc_pre", (DI, T), F32)
    silu_z = m.d("silu_z", (DI, T), BF16)
    x3 = x_fm.rearrange("(k p) t -> p k t", p=128)
    xcp3 = xc_pre.rearrange("(k p) t -> p k t", p=128)
    slz3 = silu_z.rearrange("(k p) t -> p k t", p=128)

    with ExitStack() as stack:
        wst = sbuf(stack, nc, "f1wst", (128, DI), F32)
        w16 = sbuf(stack, nc, "f1w16", (128, NK * 2 * DI), BF16)
        sx = [sbuf(stack, nc, f"f1sx{i}", (128, NK * CW), F32) for i in range(2)]
        ln16 = [sbuf(stack, nc, f"f1ln{i}", (128, NK * CW), BF16) for i in range(2)]
        ssz = [sbuf(stack, nc, f"f1sz{i}", (128, 512), BF16) for i in range(2)]
        sxc = [sbuf(stack, nc, f"f1xc{i}", (128, 512), F32) for i in range(2)]
        st = alloc_ln_state(stack, nc, "f1")
        p_xz = [psum(stack, nc, f"f1pxz{i}", (128, 512)) for i in range(2)]

        init_ln_state(p, st)
        for k in range(NK):
            for hh in range(2):
                ev = p.dma("sync", wst[:, :],
                           w_in[k * 128:(k + 1) * 128, hh * DI:(hh + 1) * DI],
                           after=[p.val("wcv")], sig="wld")
                p.op("vector", lambda e, _k=k, _h=hh: e.tensor_copy(
                    w16[:, _k * 2 * DI + _h * DI:_k * 2 * DI + (_h + 1) * DI], wst[:, :]),
                    after=[ev], sig="wcv")
        ev_wc_all = p.val("wcv")

        apl_evs = []
        mmz_evs = []
        for c in range(NCH):
            ev_ld = p.dma("sync", sx[c % 2].rearrange("p (k t) -> p k t", k=NK),
                          x3[:, :, c * CW:(c + 1) * CW],
                          after=[apl_evs[c - 2]] if c >= 2 else [],
                          sig="xld")
            xb = sx[c % 2]
            ev_bc = emit_ln_stats(p, st, "f1",
                                  lambda k, _x=xb: _x[:, k * CW:(k + 1) * CW],
                                  NK, D, [ev_ld])
            lnb = ln16[c % 2]
            aft0 = [ev_bc] + ([mmz_evs[c - 2]] if c >= 2 else [])
            ev_apl = None
            for k in range(NK):
                p.op("vector", lambda e, _k=k, _x=xb, _l=lnb: e.tensor_sub(
                    _l[:, _k * CW:(_k + 1) * CW], _x[:, _k * CW:(_k + 1) * CW],
                    st["mbc"][:]),
                    after=aft0 if k == 0 else [])
                ev_apl = p.op("vector", lambda e, _k=k, _l=lnb: e.tensor_mul(
                    _l[:, _k * CW:(_k + 1) * CW], _l[:, _k * CW:(_k + 1) * CW],
                    st["rbc"][:]),
                    sig="apl" if k == NK - 1 else None)
            apl_evs.append(ev_apl)
            zst_evs = []
            xst_evs = []
            ev_mm = None
            for mm in range(32):
                for h in range(2):
                    gidx = (c * 32 + mm) * 2 + h
                    bank = p_xz[gidx % 2]
                    aft = []
                    if mm == 0 and h == 0:
                        aft = [ev_apl, ev_wc_all]
                    if gidx >= 2:
                        pg = gidx - 2
                        if (pg // 2) % 32 < 16:
                            aft.append(p.val("evx"))
                        else:
                            aft.append(p.val("evz"))
                    for k in range(NK):
                        ev_mm = p.op(
                            "tensor",
                            lambda e, _b=bank, _k=k, _m=mm, _h=h, _l=lnb: e.matmul(
                                _b[:],
                                w16[:, _k * 2 * DI + _m * 128:_k * 2 * DI + (_m + 1) * 128],
                                _l[:, _k * CW + _h * 512:_k * CW + (_h + 1) * 512],
                                start=(_k == 0), stop=(_k == NK - 1)),
                            after=aft if k == 0 else [],
                            sig="mmz" if k == NK - 1 else None)
                    if mm < 16:
                        slot = sxc[gidx % 2]
                        aftx = [ev_mm]
                        if len(xst_evs) >= 2:
                            aftx.append(xst_evs[-2])
                        ev_cp = p.op("vector", lambda e, _s=slot, _b=bank:
                                     e.tensor_copy(_s[:], _b[:]),
                                     after=aftx, sig="evx")
                        xev = p.dma("gpsimd",
                                    xcp3[:, mm, c * CW + h * 512:c * CW + (h + 1) * 512],
                                    slot[:], after=[ev_cp], sig="stx")
                        xst_evs.append(xev)
                    else:
                        slot = ssz[gidx % 2]
                        aftz = [ev_mm]
                        if len(zst_evs) >= 2:
                            aftz.append(zst_evs[-2])
                        ev_sl = p.op("scalar", lambda e, _s=slot, _b=bank:
                                     e.activation(_s[:], _b[:], AF.Silu),
                                     after=aftz, sig="evz")
                        zev = p.dma("gpsimd",
                                    slz3[:, mm - 16, c * CW + h * 512:c * CW + (h + 1) * 512],
                                    slot[:], after=[ev_sl], sig="stz")
                        zst_evs.append(zev)
            mmz_evs.append(ev_mm)
        p.op("gpsimd", None, after=[p.val("stx"), p.val("stz")])
    p.barrier()


def host_prep(inputs, T):
    f32 = np.float32
    i = inputs
    out = {}
    xs = []
    for b in range(np.asarray(i["L0_attn_x"]).shape[0]):
        xs.append(np.ascontiguousarray(np.asarray(i["L0_attn_x"])[b, :T, :].T.astype(f32)))
    out["w_in"] = np.ascontiguousarray(np.asarray(i["L0_attn_in_proj_w"]).astype(f32))
    wxp = np.asarray(i["L0_attn_x_proj_w"]).astype(f32)
    wxp_p = np.zeros((DI, 128), f32)
    wxp_p[:, 0:16] = wxp[:, 64:80]
    wxp_p[:, 32:48] = wxp[:, 80:96]
    wxp_p[:, 64:128] = wxp[:, 0:64]
    out["w_xp"] = wxp_p
    out["w_dt"] = np.ascontiguousarray(np.asarray(i["L0_attn_dt_proj_w"]).astype(f32))
    out["w_out"] = np.ascontiguousarray(np.asarray(i["L0_attn_out_proj_w"]).astype(f32))
    out["w_ff1"] = np.ascontiguousarray(np.asarray(i["L0_attn_ff_w1"]).astype(f32))
    out["w_ff2"] = np.ascontiguousarray(np.asarray(i["L0_attn_ff_w2"]).astype(f32))
    out["wr_t"] = np.ascontiguousarray(np.asarray(i["L0_ffn_W_r"]).astype(f32).T)
    out["wk_t"] = np.ascontiguousarray(np.asarray(i["L0_ffn_W_k"]).astype(f32).T)
    out["wv_t"] = np.ascontiguousarray(np.asarray(i["L0_ffn_W_v"]).astype(f32).T)
    out["l1w_t"] = np.ascontiguousarray(np.asarray(i["L1_attn_W"]).astype(f32).T)
    out["wg_t"] = np.ascontiguousarray(np.asarray(i["L1_ffn_w_gate"]).astype(np.int32).T)
    out["wu_t"] = np.ascontiguousarray(np.asarray(i["L1_ffn_w_up"]).astype(np.int32).T)
    out["wd_t"] = np.ascontiguousarray(np.asarray(i["L1_ffn_w_down"]).astype(np.int32).T)

    def col(v):
        v = np.asarray(v, f32)
        n = v.shape[0] // 128
        return v.reshape(n, 128).T

    vecs = np.zeros((128, VC_N), f32)
    cw = np.asarray(i["L0_attn_conv_w"]).astype(f32)
    vecs[:, VC_CONVW:VC_CONVW + 64] = cw.reshape(16, 128, 4).transpose(1, 0, 2).reshape(128, 64)
    vecs[:, VC_CONVB:VC_CONVB + 16] = col(i["L0_attn_conv_b"])
    vecs[:, VC_DTB:VC_DTB + 16] = col(i["L0_attn_dt_proj_b"])
    vecs[:, VC_D:VC_D + 16] = col(i["L0_attn_D"])
    vecs[:, VC_LNG:VC_LNG + 8] = col(i["L0_attn_ln_g"])
    vecs[:, VC_LNB:VC_LNB + 8] = col(i["L0_attn_ln_b"])
    vecs[:, VC_FB1:VC_FB1 + 32] = col(i["L0_attn_ff_b1"])
    vecs[:, VC_FB2:VC_FB2 + 8] = col(i["L0_attn_ff_b2"])
    vecs[:, VC_MUR:VC_MUR + 8] = col(i["L0_ffn_mu_r"])
    vecs[:, VC_MUK:VC_MUK + 8] = col(i["L0_ffn_mu_k"])
    out["vecs"] = vecs
    out["eye16"] = np.eye(128, dtype=ml_dtypes.bfloat16)
    alog = np.asarray(i["L0_attn_A_log"]).astype(f32)
    out["alog_t"] = np.ascontiguousarray(
        alog.reshape(NKI, 128, DS).transpose(1, 0, 2).reshape(128, NKI * DS))
    return xs, out



SCF_IO = dict(reads=["xc_pre", "silu_z", "w_xp", "w_dt", "vecs", "alog_t", "eye16"],
              writes=["yF2"])
SCR_IO = dict(reads=["xc_pre", "silu_z", "yF2", "w_xp", "w_dt", "vecs", "alog_t",
                     "eye16"],
              writes=["ysum16"])


def emit_scan(p, m, T, rev):
    nc = p.nc
    p.new_scope()
    tcs = min(TCS, T)
    assert tcs >= DI
    NTC = T // tcs
    NS = tcs // 512
    pref = "r" if rev else "f"
    xc_pre = m.d("xc_pre", (DI, T), F32)
    silu_z = m.d("silu_z", (DI, T), BF16)
    w_xp = m.d("w_xp", (DI, 128), F32)
    w_dt = m.d("w_dt", (DTR, DI), F32)
    vecs = m.d("vecs", (128, VC_N), F32)
    alog_t = m.d("alog_t", (128, NKI * DS), F32)
    eye16d = m.d("eye16", (128, 128), BF16)
    yF2 = m.d("yF2", (DI, T), BF16)
    xcf32 = m.d(pref + "xc32", (DI, T), F32)
    xcf16 = m.d(pref + "xc16", (DI, T), BF16)
    Brow = m.d(pref + "Brow", (DS, T), BF16)
    Crow = m.d(pref + "Crow", (DS, T), BF16)
    if rev:
        ysum16 = m.d("ysum16", (DI, T), BF16)
        ys3 = ysum16.rearrange("(k p) t -> p k t", p=128)
    DBG = bool(os.environ.get("ATH_SCAN_DEBUG"))
    if DBG:
        ddump = m.d(pref + "ddump", (DI, T), F32)
        dd3 = ddump.rearrange("(k p) t -> p k t", p=128)
        ydump = m.d(pref + "ydump", (DI, T), F32)
        yd3 = ydump.rearrange("(k p) t -> p k t", p=128)

    xcp3 = xc_pre.rearrange("(k p) t -> p k t", p=128)
    slz3 = silu_z.rearrange("(k p) t -> p k t", p=128)
    yF23 = yF2.rearrange("(k p) t -> p k t", p=128)
    xc323 = xcf32.rearrange("(k p) t -> p k t", p=128)
    xc163 = xcf16.rearrange("(k p) t -> p k t", p=128)

    with ExitStack() as stack:
        S = [sbuf(stack, nc, f"{pref}S{i}", (128, tcs + 8), F32) for i in range(6)]
        xcpA, xcpB, acc, xcfA, xcfB, dblS = S
        delta, sv, xc_m, _unused1, _unused2, y_t = S
        gz = sbuf(stack, nc, pref + "gz6", (128, tcs), BF16)
        gy = sbuf(stack, nc, pref + "gy6", (128, tcs), BF16)
        B0 = sbuf(stack, nc, pref + "B0", (128, tcs), BF16)
        B1 = sbuf(stack, nc, pref + "B1", (128, tcs), BF16)
        du16 = sbuf(stack, nc, pref + "du", (128, tcs), BF16)
        a16_one = sbuf(stack, nc, pref + "a6", (128, tcs), BF16)
        a16_2 = [a16_one, a16_one]
        b16 = sbuf(stack, nc, pref + "b6", (128, tcs), BF16)
        h16 = sbuf(stack, nc, pref + "h6", (128, tcs), BF16)
        hc16_one = sbuf(stack, nc, pref + "hc6", (128, tcs), BF16)
        hc16_2 = [hc16_one, hc16_one]
        bcB = [sbuf(stack, nc, pref + f"bB{i}", (128, tcs), BF16) for i in range(2)]
        bcC = sbuf(stack, nc, pref + "bC", (128, tcs), BF16)
        wxp16 = sbuf(stack, nc, pref + "wxp", (128, NKI * 128), BF16)
        wdt16 = sbuf(stack, nc, pref + "wdt", (64, DI), BF16)
        aneg = sbuf(stack, nc, pref + "ang", (128, NKI * DS), F32)
        cvw = sbuf(stack, nc, pref + "cvw", (128, VC_N), F32)
        hcar = sbuf(stack, nc, pref + "hc", (128, NKI * DS), F32)
        cst = sbuf(stack, nc, pref + "cs", (128, 2), F32)
        eye = sbuf(stack, nc, pref + "eye", (128, 128), BF16)
        p_all = psum(stack, nc, pref + "pall", (128, NS * 512))

        p.op("vector", lambda e: e.memset(cst[:, 0:1], 1.0))
        p.op("vector", lambda e: e.memset(hcar[:], 0.0), sig="hc0")
        ev_v = p.dma("sync", cvw[:], vecs, sig="wld")
        ev_ey = p.dma("sync", eye[:], eye16d, sig="wld")
        ev_a0 = p.dma("sync", aneg[:], alog_t, sig="wld")
        ev_e = p.op("scalar", lambda e: e.activation(aneg[:], aneg[:], AF.Exp),
                    after=[ev_a0], sig="ae")
        ev_an = p.op("vector", lambda e: e.tensor_scalar_mul(aneg[:], aneg[:], -1.0),
                     after=[ev_e], sig="ang")
        ev_l = p.dma("sync", dblS[:, 0:NKI * 128].rearrange("p (k c) -> p k c", c=128),
                     w_xp.rearrange("(k p) c -> p k c", p=128), sig="wld")
        ev_wx = p.op("vector", lambda e: e.tensor_copy(
            wxp16[:], dblS[:, 0:NKI * 128]), after=[ev_l], sig="wxc")
        ev_l2 = p.dma("sync", dblS[0:DTR, 0:DI], w_dt, after=[ev_wx], sig="wld")
        ev_wd = p.op("vector", lambda e: e.tensor_copy(wdt16[:], dblS[0:DTR, 0:DI]),
                     after=[ev_l2], sig="wxc")

        dte_evs = []
        for tc in range(NTC):
            if rev:
                lo, hi = T - (tc + 1) * tcs, T - tc * tcs
            else:
                lo, hi = tc * tcs, (tc + 1) * tcs
            scol = tc * tcs
            cva_evs = []
            for mi in range(NKI):
                tile = (xcpA, xcpB)[mi % 2]
                llo, lhi = max(lo - 4, 0), min(hi + 4, T)
                if tc > 0 and mi < 2:
                    aft_ld = [p.val("aev"), p.val("spr"), p.val("dte")]
                elif mi >= 2:
                    aft_ld = [cva_evs[mi - 2]]
                else:
                    aft_ld = []
                ev_ld = p.dma("sync", tile[:, (llo - (lo - 4)):(llo - (lo - 4)) + (lhi - llo)],
                              xcp3[:, mi, llo:lhi], after=aft_ld, sig="cld")
                ev_ms = None
                if (not rev) and lo == 0:
                    ev_ms = p.op("vector", lambda e, _t=tile: e.memset(_t[:, 0:4], 0.0),
                                 sig="cms")
                if rev and hi == T:
                    ev_ms = p.op("vector", lambda e, _t=tile: e.memset(
                        _t[:, tcs + 4:tcs + 8], 0.0), sig="cms")
                aft_t0 = [ev_ld, ev_ms, ev_v]
                if tc > 0 and mi == 0:
                    aft_t0.append(p.val("gat"))
                for i in range(4):
                    if rev:
                        vw = tile[:, tcs + 6 - i: tcs + 6 - i - tcs: -1]
                    else:
                        vw = tile[:, 1 + i: 1 + i + tcs]
                    wcol = cvw[:, VC_CONVW + mi * 4 + i:VC_CONVW + mi * 4 + i + 1]
                    if i == 0:
                        p.op("vector", lambda e, _v=vw, _w=wcol: e.tensor_scalar_mul(
                            acc[:, 0:tcs], _v, _w), after=aft_t0)
                    else:
                        p.op("vector", lambda e, _v=vw, _w=wcol: e.scalar_tensor_tensor(
                            acc[:, 0:tcs], _v, _w, acc[:, 0:tcs], ALU.mult, ALU.add),
                            sig="cva" if i == 3 else None)
                ev_cv = p.val("cva")
                cva_evs.append(ev_cv)
                xcf = (xcfA, xcfB)[mi % 2]
                aft_si = [ev_cv]
                if mi >= 2:
                    aft_si.append(p.val("sxs"))
                ev_si = p.op("scalar", lambda e, _x=xcf, _b=cvw[:, VC_CONVB + mi:VC_CONVB + mi + 1]:
                             e.activation(_x[:, 0:tcs], acc[:, 0:tcs], AF.Silu, bias=_b),
                             after=aft_si, sig="csi")
                b6 = (B0, B1)[mi % 2]
                aft_cv16 = [ev_si]
                if tc > 0 and mi < 2:
                    aft_cv16.append(p.val("dtm"))
                    aft_cv16.append(p.val("yst"))
                if mi >= 2:
                    aft_cv16.append(p.val("sxs"))
                ev_c16 = p.op("vector", lambda e, _b=b6, _x=xcf: e.tensor_copy(
                    _b[:], _x[:, 0:tcs]), after=aft_cv16, sig="c16")
                p.dma("sync", xc323[:, mi, scol:scol + tcs], xcf[:, 0:tcs],
                      after=[ev_si], sig="sxs")
                p.dma("sync", xc163[:, mi, scol:scol + tcs], b6[:],
                      after=[ev_c16], sig="sxs")
            pjk_evs = []
            for k in range(NKI):
                x16 = (B0, B1)[k % 2]
                if k < 2:
                    aft = [p.val("sxs"), p.val("dtm"), p.val("yst"), p.val("bcs")]
                else:
                    aft = [pjk_evs[k - 2]]
                ev_x = p.dma("sync", x16[:], xc163[:, k, scol:scol + tcs],
                             after=aft, sig="xl6")
                for s in range(NS):
                    aftm = [ev_x, ev_wx]
                    if k == 0:
                        aftm += [p.val("yst"), p.val("dte"), p.val("gat")]
                    ev_pk = p.op("tensor", lambda e, _k=k, _s=s, _x=x16: e.matmul(
                        p_all[:, _s * 512:(_s + 1) * 512],
                        wxp16[:, _k * 128:(_k + 1) * 128],
                        _x[:, _s * 512:(_s + 1) * 512],
                        start=(_k == 0), stop=(_k == NKI - 1)),
                        after=aftm,
                        sig=("pj" if k == NKI - 1 else ("pjk" if s == NS - 1 else None)))
                    if s == NS - 1:
                        pjk_evs.append(ev_pk)
            ev_pj = p.val("pj")
            for s in range(NS):
                p.op("scalar", lambda e, _s=s: e.activation(
                    dblS[:, _s * 512:(_s + 1) * 512], p_all[:, _s * 512:(_s + 1) * 512],
                    AF.Identity), after=[ev_pj], sig="dbe")
            ev_dbe = p.val("dbe")
            ev_b48 = p.op("vector", lambda e: e.tensor_copy(
                B1[0:48, 0:tcs], dblS[0:48, 0:tcs]), after=[ev_dbe], sig="b48")
            p.dma("sync", Brow[:, scol:scol + tcs], B1[0:DS, 0:tcs],
                  after=[ev_b48], sig="bcs")
            p.dma("sync", Crow[:, scol:scol + tcs], B1[32:32 + DS, 0:tcs],
                  after=[ev_b48], sig="bcs")
            ev_bcs = p.val("bcs")
            ev_dt16 = p.op("vector", lambda e: e.tensor_copy(
                B0[0:DTR, 0:tcs], dblS[64:128, 0:tcs]),
                after=[ev_dbe, ev_pj], sig="dt6")
            for mi in range(NKI):
                for s2p in range(NS // 2):
                    gb = len(dte_evs)
                    aftm = [ev_dt16, ev_wd, p.val("gat")]
                    if gb >= 1:
                        aftm.append(dte_evs[-1])
                    else:
                        aftm.append(ev_dbe)
                    ev_dm = None
                    for s in (2 * s2p, 2 * s2p + 1):
                        ev_dm = p.op("tensor", lambda e, _mi=mi, _s=s: e.matmul(
                            p_all[:, (_s % 2) * 512:(_s % 2 + 1) * 512],
                            wdt16[:, _mi * 128:(_mi + 1) * 128],
                            B0[0:DTR, _s * 512:(_s + 1) * 512],
                            start=True, stop=True),
                            after=aftm if s % 2 == 0 else [], sig="dtm")
                    afte = [ev_dm, p.val("spr")]
                    if mi == 0 and s2p == 0:
                        afte.append(p.val("cva"))
                    ev_de = p.op("scalar", lambda e, _s=s2p,
                                 _b=cvw[:, VC_DTB + mi:VC_DTB + mi + 1]:
                                 e.activation(sv[:, _s * 1024:(_s + 1) * 1024],
                                              p_all[:, 0:1024],
                                              AF.Identity, bias=_b),
                                 after=afte, sig="dte")
                    dte_evs.append(ev_de)
                ev_v_ready = dte_evs[-1]
                aft_sp = [ev_v_ready]
                if mi == 0:
                    aft_sp.append(p.val("cva"))
                if not (tc == 0 and mi == 0):
                    aft_sp.append(p.val("aev"))
                ev_r = p.op("scalar", lambda e: e.activation(
                    delta[:, 0:tcs], sv[:, 0:tcs], AF.Relu), after=aft_sp, sig="spr")
                aft_ab = [ev_r, p.val("yst")]
                ev_abs = p.op("vector", lambda e: e.scalar_tensor_tensor(
                    y_t[:, 0:tcs], delta[:, 0:tcs], 2.0, sv[:, 0:tcs],
                    ALU.mult, ALU.subtract), after=aft_ab, sig="spa")
                p.op("scalar", lambda e: e.activation(
                    y_t[:, 0:tcs], y_t[:, 0:tcs], AF.Exp, scale=-1.0),
                    after=[ev_abs], sig="spe")
                ev_ln = p.op("scalar", lambda e: e.activation(
                    y_t[:, 0:tcs], y_t[:, 0:tcs], AF.Ln, bias=cst[:, 0:1]), sig="spe")
                ev_dl = p.op("vector", lambda e: e.tensor_add(
                    delta[:, 0:tcs], delta[:, 0:tcs], y_t[:, 0:tcs]),
                    after=[ev_ln], sig="dlt")
                if DBG:
                    p.dma("sync", dd3[:, mi, scol:scol + tcs], delta[:, 0:tcs],
                          after=[ev_dl], sig="dbgs")
                aft_xl = [p.val("csi"), p.val("gat")]
                ev_xm = p.dma("sync", xc_m[:, 0:tcs], xc323[:, mi, scol:scol + tcs],
                              after=aft_xl, sig="xml")
                ev_du = p.op("vector", lambda e: e.tensor_mul(
                    du16[:], delta[:, 0:tcs], xc_m[:, 0:tcs]),
                    after=[ev_xm] + ([p.val("yst")] if not rev else []), sig="duu")
                ev_gz = p.dma("sync", gz[:], slz3[:, mi, lo:hi],
                              after=[p.val("gml")], sig="gzl")
                if rev:
                    ev_gy = p.dma("sync", gy[:], yF23[:, mi, lo:hi],
                                  after=[p.val("ost")], sig="gyl")
                yad_evs = []
                bld_evs = []
                scn_evs = []
                for n in range(DS):
                    bb = bcB[n % 2]
                    if n < 2:
                        aftB = [p.val("bld")]
                    else:
                        aftB = [bld_evs[-2]]
                    ev_bb = p.dma("sync", bb[:],
                                  Brow[n:n + 1, scol:scol + tcs]
                                  .partition_broadcast(128).squeeze(1),
                                  after=aftB + [ev_bcs], sig="bbl")
                    aftC = [p.val("hcm")]
                    ev_cc = p.dma("scalar", bcC[:],
                                  Crow[n:n + 1, scol:scol + tcs]
                                  .partition_broadcast(128).squeeze(1),
                                  after=aftC + [ev_bcs], sig="ccl")
                    a16 = a16_2[n % 2]
                    aft_a = [ev_dl, ev_an]
                    if scn_evs:
                        aft_a.append(scn_evs[-1])
                    elif not (tc == 0 and mi == 0):
                        aft_a.append(p.val("scn"))
                    ev_a = p.op("scalar", lambda e, _mi=mi, _n=n, _a=a16: e.activation(
                        _a[:], delta[:, 0:tcs], AF.Exp,
                        scale=aneg[:, _mi * DS + _n:_mi * DS + _n + 1]),
                        after=aft_a, sig="aev")
                    ev_b = p.op("vector", lambda e, _b=bb: e.tensor_tensor(
                        b16[:], du16[:], _b[:], ALU.mult),
                        after=[ev_bb, ev_du], sig="bld")
                    bld_evs.append(ev_b)
                    col = mi * DS + n
                    ev_h = p.op("vector", lambda e, _c=col, _a=a16: e.tensor_tensor_scan(
                        h16[:], _a[:], b16[:],
                        hcar[:, _c:_c + 1], ALU.mult, ALU.add),
                        after=[ev_a], sig="scn")
                    scn_evs.append(ev_h)
                    hc16 = hc16_2[n % 2]
                    aft_hc = [ev_cc]
                    if yad_evs:
                        aft_hc.append(yad_evs[-1])
                    elif not (tc == 0 and mi == 0):
                        aft_hc.append(p.val("yad"))
                    ev_hc = p.op("vector", lambda e, _o=hc16: e.tensor_tensor(
                        _o[:], h16[:], bcC[:], ALU.mult),
                        after=aft_hc, sig="hcm")
                    aft_ya = [ev_hc, ev_ey]
                    if n == 0:
                        aft_ya += [ev_v_ready, p.val("gat")]
                    ev_ya = None
                    for s2 in range(NS):
                        ev_ya = p.op("tensor", lambda e, _s=s2, _i=hc16,
                                     _st=(n == 0), _sp=(n == DS - 1): e.matmul(
                            p_all[:, _s * 512:(_s + 1) * 512], eye[:],
                            _i[:, _s * 512:(_s + 1) * 512],
                            start=_st, stop=_sp),
                            after=aft_ya if s2 == 0 else [],
                            sig="yad" if s2 == NS - 1 else None)
                    yad_evs.append(ev_ya)
                    p.op("vector", lambda e, _c=col: e.tensor_copy(
                        hcar[:, _c:_c + 1], h16[:, tcs - 1:tcs]))

                aft_g = [yad_evs[-1], p.val("ost" if rev else "yst")]
                ev_g = None
                for s2 in range(NS):
                    ev_g = p.op("vector", lambda e, _mi=mi, _s=s2: e.scalar_tensor_tensor(
                        y_t[:, _s * 512:(_s + 1) * 512],
                        xc_m[:, _s * 512:(_s + 1) * 512],
                        cvw[:, VC_D + _mi:VC_D + _mi + 1],
                        p_all[:, _s * 512:(_s + 1) * 512],
                        ALU.mult, ALU.add),
                        after=aft_g if s2 == 0 else [], sig="gat")
                if DBG:
                    p.dma("sync", yd3[:, mi, scol:scol + tcs], y_t[:, 0:tcs],
                          after=[ev_g], sig="dbgs")
                if rev:
                    ev_gm = p.op("vector", lambda e: e.tensor_tensor(
                        y_t[:, 0:tcs], y_t[:, 0:tcs],
                        gz[:, tcs - 1::-1], ALU.mult),
                        after=[ev_gz], force_after=[ev_g], sig="gml")
                    ev_o = p.op("vector", lambda e: e.tensor_add(
                        B1[:], y_t[:, tcs - 1::-1], gy[:]),
                        after=[ev_gy, p.val("yst"), p.val("bcs")],
                        force_after=[ev_gm], sig="ost")
                    p.dma("sync", ys3[:, mi, lo:hi], B1[:], after=[ev_o], sig="yst")
                else:
                    ev_o = p.op("vector", lambda e: e.tensor_tensor(
                        du16[:], y_t[:, 0:tcs], gz[:], ALU.mult),
                        after=[ev_gz, p.val("yst")], sig="gml")
                    p.dma("sync", yF23[:, mi, lo:hi], du16[:],
                          after=[ev_o], sig="yst")
        p.op("sync", None, after=[p.val("yst"), p.val("sxs"), p.val("bcs")])
    p.barrier()


def load_w16(p, stg, dram, Mcols, kchunks, dest, scale=None, tag=None):
    evs = []
    for k in range(kchunks):
        ev = p.dma("sync", stg[:, 0:Mcols], dram[k * 128:(k + 1) * 128, :],
                   after=[p.val("lwc")], sig="lwl")
        if scale is None:
            ev2 = p.op("vector", lambda e, _k=k: e.tensor_copy(
                dest[:, _k * Mcols:(_k + 1) * Mcols], stg[:, 0:Mcols]),
                after=[ev], sig="lwc")
        else:
            ev2 = p.op("vector", lambda e, _k=k: e.tensor_scalar_mul(
                dest[:, _k * Mcols:(_k + 1) * Mcols], stg[:, 0:Mcols], scale),
                after=[ev], sig="lwc")
        evs.append(ev2)
    return evs[-1]


PO1_IO = dict(reads=["ysum16", "w_out", "w_ff1", "vecs"], writes=["g16"])


def emit_po1(p, m, T):
    nc = p.nc
    p.new_scope()
    NCH = T // CW
    ysum16 = m.d("ysum16", (DI, T), BF16)
    w_out = m.d("w_out", (DI, D), F32)
    w_ff1 = m.d("w_ff1", (D, DFF), F32)
    vecs = m.d("vecs", (128, VC_N), F32)
    g16d = m.d("g16", (DFF, T), BF16)
    ys3 = ysum16.rearrange("(k p) t -> p k t", p=128)
    g3 = g16d.rearrange("(k p) t -> p k t", p=128)

    with ExitStack() as stack:
        wo16 = sbuf(stack, nc, "o1wo", (128, NKI * D), BF16)
        wf16 = sbuf(stack, nc, "o1wf", (128, NK * DFF), BF16)
        ysl2 = [sbuf(stack, nc, f"o1ys{i}", (128, (NKI // 2) * CW), BF16) for i in range(2)]
        mot = sbuf(stack, nc, "o1mo", (128, NK * CW), F32)
        aps = sbuf(stack, nc, "o1ap", (128, CW), F32)
        h16 = sbuf(stack, nc, "o1h", (128, NK * CW), BF16)
        gsl = [sbuf(stack, nc, f"o1g{i}", (128, 512), BF16) for i in range(2)]
        cvw = sbuf(stack, nc, "o1cv", (128, VC_N), F32)
        st = alloc_ln_state(stack, nc, "o1")
        p_o = [psum(stack, nc, f"o1p{i}", (128, 512)) for i in range(2)]

        init_ln_state(p, st)
        p.dma("sync", cvw[:], vecs, sig="cvl")
        ev_cv = p.val("cvl")
        ev_w1 = load_w16(p, mot, w_out, D, NKI, wo16)
        ev_w2 = load_w16(p, mot, w_ff1, DFF, NK, wf16)

        HY = NKI // 2
        yld_evs = []
        for c in range(NCH):
            for q in range(2):
                qi = c * 2 + q
                aftq = [yld_evs[qi - 2][1]] if qi >= 2 else []
                ev_ld = p.dma("sync", ysl2[qi % 2].rearrange("p (k t) -> p k t", k=HY),
                              ys3[:, q * HY:(q + 1) * HY, c * CW:(c + 1) * CW],
                              after=aftq, sig="yld")
                yld_evs.append((ev_ld, None))
            ev_ev = None
            for mo in range(NK):
                for h in range(2):
                    g = (c * NK + mo) * 2 + h
                    bank = p_o[g % 2]
                    aft0 = [ev_w1, ev_w2, ev_cv]
                    if g >= 2:
                        aft0.append(p.val("oev"))
                    ev_mm = None
                    for k in range(NKI):
                        q = k // HY
                        qi = c * 2 + q
                        kq = k % HY
                        aft = (aft0 + [yld_evs[qi][0]]) if kq == 0 else []
                        ev_mm = p.op("tensor", lambda e, _b=bank, _q=qi % 2, _k=kq,
                                     _kk=k, _m=mo, _h=h:
                                     e.matmul(_b[:],
                                              wo16[:, _kk * D + _m * 128:_kk * D + (_m + 1) * 128],
                                              ysl2[_q][:, _k * CW + _h * 512:_k * CW + (_h + 1) * 512],
                                              start=(_kk == 0), stop=(_kk == NKI - 1)),
                                     after=aft,
                                     sig="mm1" if k == NKI - 1 else None)
                    for q in range(2):
                        yld_evs[c * 2 + q] = (yld_evs[c * 2 + q][0], ev_mm)
                    ev_ev = p.op("vector", lambda e, _b=bank, _m=mo, _h=h: e.tensor_copy(
                        mot[:, _m * CW + _h * 512:_m * CW + (_h + 1) * 512], _b[:]),
                        after=[ev_mm], sig="oev")
            ev_bc = emit_ln_stats(p, st, "o1",
                                  lambda k: mot[:, k * CW:(k + 1) * CW],
                                  NK, D, [ev_ev])
            ev_ap = None
            for k in range(NK):
                p.op("vector", lambda e, _k=k: e.tensor_sub(
                    aps[:], mot[:, _k * CW:(_k + 1) * CW], st["mbc"][:]),
                    after=[ev_bc] if k == 0 else [])
                p.op("vector", lambda e: e.tensor_mul(aps[:], aps[:], st["rbc"][:]))
                ev_ap = p.op("vector", lambda e, _k=k: e.tensor_scalar(
                    h16[:, _k * CW:(_k + 1) * CW], aps[:],
                    cvw[:, VC_LNG + _k:VC_LNG + _k + 1],
                    cvw[:, VC_LNB + _k:VC_LNB + _k + 1],
                    ALU.mult, ALU.add), sig="apl" if k == NK - 1 else None)
            gst_evs = []
            for mf in range(NKF):
                for h in range(2):
                    g = (c * NKF + mf) * 2 + h
                    bank = p_o[g % 2]
                    aft = [ev_ap]
                    if g >= 2:
                        aft.append(p.val("gev"))
                    for k in range(NK):
                        ev_mm = p.op("tensor", lambda e, _b=bank, _k=k, _m=mf, _h=h:
                                     e.matmul(_b[:],
                                              wf16[:, _k * DFF + _m * 128:_k * DFF + (_m + 1) * 128],
                                              h16[:, _k * CW + _h * 512:_k * CW + (_h + 1) * 512],
                                              start=(_k == 0), stop=(_k == NK - 1)),
                                     after=aft if k == 0 else [],
                                     sig="mm2" if k == NK - 1 else None)
                    slot = gsl[g % 2]
                    aftg = [ev_mm]
                    if len(gst_evs) >= 2:
                        aftg.append(gst_evs[-2])
                    ev_g = p.op("scalar", lambda e, _s=slot, _b=bank, _m=mf: e.activation(
                        _s[:], _b[:], AF.Gelu,
                        bias=cvw[:, VC_FB1 + _m:VC_FB1 + _m + 1]),
                        after=aftg, sig="gev")
                    gev = p.dma("gpsimd", g3[:, mf, c * CW + h * 512:c * CW + (h + 1) * 512],
                                slot[:], after=[ev_g], sig="gst")
                    gst_evs.append(gev)
        p.op("gpsimd", None, after=[p.val("gst")])
    p.barrier()


PO2_IO = dict(reads=["g16", "w_ff2", "x_fm", "vecs"], writes=["x1"])


def emit_po2(p, m, T):
    nc = p.nc
    p.new_scope()
    NCH = T // CW
    g16d = m.d("g16", (DFF, T), BF16)
    w_ff2 = m.d("w_ff2", (DFF, D), F32)
    x_fm = m.d("x_fm", (D, T), F32)
    vecs = m.d("vecs", (128, VC_N), F32)
    x1d = m.d("x1", (D, T), F32)
    g3 = g16d.rearrange("(k p) t -> p k t", p=128)
    x3 = x_fm.rearrange("(k p) t -> p k t", p=128)
    x13 = x1d.rearrange("(k p) t -> p k t", p=128)

    with ExitStack() as stack:
        wf16 = sbuf(stack, nc, "o2wf", (128, NKF * D), BF16)
        gl2 = [sbuf(stack, nc, f"o2g{i}", (128, (NKF // 4) * CW), BF16) for i in range(3)]
        xl = sbuf(stack, nc, "o2x", (128, NK * CW), F32)
        ot = sbuf(stack, nc, "o2o", (128, NK * CW), F32)
        aps = sbuf(stack, nc, "o2ap", (128, CW), F32)
        cvw = sbuf(stack, nc, "o2cv", (128, VC_N), F32)
        st = alloc_ln_state(stack, nc, "o2")
        p_o = [psum(stack, nc, f"o2p{i}", (128, 512)) for i in range(2)]

        init_ln_state(p, st)
        p.dma("sync", cvw[:], vecs, sig="cvl")
        ev_cv = p.val("cvl")
        ev_w1 = load_w16(p, ot, w_ff2, D, NKF, wf16)

        QK = NKF // 4
        gld_evs = []
        for c in range(NCH):
            ev_lx = p.dma("sync", xl.rearrange("p (k t) -> p k t", k=NK),
                          x3[:, :, c * CW:(c + 1) * CW],
                          after=[p.val("stx")], sig="xld")
            ev_ev = None
            for half in range(2):
                for q in range(2):
                    qi = c * 4 + half * 2 + q
                    aftq = [gld_evs[qi - 3][1]] if qi >= 3 else []
                    ev_ld = p.dma("sync", gl2[qi % 3].rearrange("p (k t) -> p k t", k=QK),
                                  g3[:, (half * 2 + q) * QK:(half * 2 + q + 1) * QK,
                                     c * CW:(c + 1) * CW],
                                  after=aftq, sig="gld")
                    gld_evs.append((ev_ld, None))
                for mo in range(NK):
                    for h in range(2):
                        g = ((c * 2 + half) * NK + mo) * 2 + h
                        bank = p_o[g % 2]
                        aft0 = [ev_w1, ev_cv]
                        if g >= 2:
                            aft0.append(p.val("oev"))
                        ev_mm = None
                        for k in range(2 * QK):
                            q = k // QK
                            qi = c * 4 + half * 2 + q
                            kk = half * 2 * QK + k
                            kq = k % QK
                            aft = (aft0 + [gld_evs[qi][0]]) if kq == 0 else []
                            ev_mm = p.op("tensor", lambda e, _b=bank, _q=qi % 3, _k=kq,
                                         _kk=kk, _m=mo, _h=h:
                                         e.matmul(_b[:],
                                                  wf16[:, _kk * D + _m * 128:_kk * D + (_m + 1) * 128],
                                                  gl2[_q][:, _k * CW + _h * 512:_k * CW + (_h + 1) * 512],
                                                  start=(_kk % (2 * QK) == 0),
                                                  stop=(_kk % (2 * QK) == 2 * QK - 1)),
                                         after=aft,
                                         sig="mm1" if k == 2 * QK - 1 else None)
                        for q in range(2):
                            qi = c * 4 + half * 2 + q
                            gld_evs[qi] = (gld_evs[qi][0], ev_mm)
                        if half == 0:
                            ev_ev = p.op("vector", lambda e, _b=bank, _m=mo, _h=h:
                                         e.tensor_copy(
                                             ot[:, _m * CW + _h * 512:_m * CW + (_h + 1) * 512],
                                             _b[:]),
                                         after=[ev_mm], sig="oev")
                        else:
                            sl = slice(mo * CW + h * 512, mo * CW + (h + 1) * 512)
                            ev_ev = p.op("vector", lambda e, _b=bank, _s=sl: e.scalar_tensor_tensor(
                                ot[:, _s], _b[:],
                                cvw[:, VC_FB2 + (_s.start // CW):VC_FB2 + (_s.start // CW) + 1],
                                ot[:, _s], ALU.add, ALU.add),
                                after=[ev_mm], sig="oev")
            ev_bc = emit_ln_stats(p, st, "o2",
                                  lambda k: ot[:, k * CW:(k + 1) * CW],
                                  NK, D, [ev_ev])
            for k in range(NK):
                p.op("vector", lambda e, _k=k: e.tensor_sub(
                    aps[:], ot[:, _k * CW:(_k + 1) * CW], st["mbc"][:]),
                    after=[ev_bc, ev_lx] if k == 0 else [])
                p.op("vector", lambda e: e.tensor_mul(aps[:], aps[:], st["rbc"][:]))
                ev_x1 = p.op("vector", lambda e, _k=k: e.tensor_add(
                    ot[:, _k * CW:(_k + 1) * CW], aps[:], xl[:, _k * CW:(_k + 1) * CW]),
                    sig="res")
                p.dma("gpsimd", x13[:, k, c * CW:(c + 1) * CW],
                      ot[:, k * CW:(k + 1) * CW], after=[ev_x1], sig="stx")
        p.op("gpsimd", None, after=[p.val("stx")])
    p.barrier()


SB_IO = dict(reads=["x1", "wr_t", "wk_t", "wv_t", "vecs"], writes=["x2"])


def emit_sb(p, m, T):
    nc = p.nc
    p.new_scope()
    NCH = T // CW
    x1d = m.d("x1", (D, T), F32)
    wr_t = m.d("wr_t", (D, D), F32)
    wk_t = m.d("wk_t", (D, D), F32)
    wv_t = m.d("wv_t", (D, D), F32)
    vecs = m.d("vecs", (128, VC_N), F32)
    x2d = m.d("x2", (D, T), F32)
    x13 = x1d.rearrange("(k p) t -> p k t", p=128)
    x23 = x2d.rearrange("(k p) t -> p k t", p=128)
    DBG = bool(os.environ.get("ATH_SB_DEBUG"))
    if DBG:
        dln = m.d("sb_ln", (D, T), BF16)
        dxr = m.d("sb_xr", (D, T), BF16)
        dr = m.d("sb_r", (D, T), F32)
        dka = m.d("sb_ka", (D, T), BF16)
        do = m.d("sb_o", (D, T), F32)
        dln3 = dln.rearrange("(k p) t -> p k t", p=128)
        dxr3 = dxr.rearrange("(k p) t -> p k t", p=128)
        dr3 = dr.rearrange("(k p) t -> p k t", p=128)
        dka3 = dka.rearrange("(k p) t -> p k t", p=128)
        do3 = do.rearrange("(k p) t -> p k t", p=128)

    with ExitStack() as stack:
        wr16 = sbuf(stack, nc, "sbwr", (128, NK * D), BF16)
        wk16 = sbuf(stack, nc, "sbwk", (128, NK * D), BF16)
        wv16 = sbuf(stack, nc, "sbwv", (128, NK * D), BF16)
        xl = sbuf(stack, nc, "sbx", (128, NK * CW), F32)
        lnf = sbuf(stack, nc, "sbln", (128, NK * (CW + 1)), BF16)
        xr16 = sbuf(stack, nc, "sbxr", (128, NK * CW), BF16)
        xk16 = sbuf(stack, nc, "sbxk", (128, NK * CW), BF16)
        rt = sbuf(stack, nc, "sbr", (128, NK * CW), F32)
        ka16 = sbuf(stack, nc, "sbka", (128, NK * CW), BF16)
        aps = sbuf(stack, nc, "sbap", (128, CW), F32)
        cvw = sbuf(stack, nc, "sbcv", (128, VC_N), F32)
        st = alloc_ln_state(stack, nc, "sb")
        p_o = [psum(stack, nc, f"sbp{i}", (128, 512)) for i in range(2)]

        init_ln_state(p, st)
        p.dma("sync", cvw[:], vecs, sig="cvl")
        ev_cv = p.val("cvl")
        load_w16(p, rt, wr_t, D, NK, wr16)
        load_w16(p, rt, wk_t, D, NK, wk16)
        ev_w = load_w16(p, rt, wv_t, D, NK, wv16)
        ln3 = lnf.rearrange("p (k t) -> p k t", t=CW + 1)
        p.op("vector", lambda e: e.memset(ln3[:, :, 0:1], 0.0), sig="car")

        for c in range(NCH):
            ev_lx = p.dma("sync", xl.rearrange("p (k t) -> p k t", k=NK),
                          x13[:, :, c * CW:(c + 1) * CW],
                          after=[p.val("res")], sig="xld")
            ev_bc = emit_ln_stats(p, st, "sb",
                                  lambda k: xl[:, k * CW:(k + 1) * CW],
                                  NK, D, [ev_lx])
            if c > 0:
                p.op("vector", lambda e: e.tensor_copy(
                    ln3[:, :, 0:1], ln3[:, :, CW:CW + 1]))
            ev_ap = None
            for k in range(NK):
                p.op("vector", lambda e, _k=k: e.tensor_sub(
                    aps[:], xl[:, _k * CW:(_k + 1) * CW], st["mbc"][:]),
                    after=[ev_bc, p.val("mm1")] if k == 0 else [])
                ev_ap = p.op("vector", lambda e, _k=k: e.tensor_mul(
                    ln3[:, _k, 1:CW + 1], aps[:], st["rbc"][:]), sig="apl")
            ev_mx = None
            for k in range(NK):
                p.op("vector", lambda e, _k=k: e.tensor_sub(
                    aps[:], ln3[:, _k, 1:CW + 1], ln3[:, _k, 0:CW]))
                p.op("vector", lambda e, _k=k: e.scalar_tensor_tensor(
                    xr16[:, _k * CW:(_k + 1) * CW], aps[:],
                    cvw[:, VC_MUR + _k:VC_MUR + _k + 1],
                    ln3[:, _k, 0:CW], ALU.mult, ALU.add))
                ev_mx = p.op("vector", lambda e, _k=k: e.scalar_tensor_tensor(
                    xk16[:, _k * CW:(_k + 1) * CW], aps[:],
                    cvw[:, VC_MUK + _k:VC_MUK + _k + 1],
                    ln3[:, _k, 0:CW], ALU.mult, ALU.add),
                    sig="mix" if k == NK - 1 else None)
            for mo in range(NK):
                for h in range(2):
                    g = (c * NK + mo) * 2 + h
                    bank = p_o[g % 2]
                    aft = [ev_mx, ev_w]
                    if g >= 2:
                        aft.append(p.val("rev"))
                    for k in range(NK):
                        ev_mm = p.op("tensor", lambda e, _b=bank, _k=k, _m=mo, _h=h:
                                     e.matmul(_b[:],
                                              wr16[:, _k * D + _m * 128:_k * D + (_m + 1) * 128],
                                              xr16[:, _k * CW + _h * 512:_k * CW + (_h + 1) * 512],
                                              start=(_k == 0), stop=(_k == NK - 1)),
                                     after=aft if k == 0 else [],
                                     sig="mmr" if k == NK - 1 else None)
                    p.op("scalar", lambda e, _b=bank, _m=mo, _h=h: e.activation(
                        rt[:, _m * CW + _h * 512:_m * CW + (_h + 1) * 512], _b[:],
                        AF.Sigmoid), after=[ev_mm], sig="rev")
            ev_r = p.val("rev")
            for mo in range(NK):
                for h in range(2):
                    g = (c * NK + mo) * 2 + h
                    bank = p_o[g % 2]
                    aft = [ev_r]
                    if g >= 2:
                        aft.append(p.val("kev"))
                    for k in range(NK):
                        ev_mm = p.op("tensor", lambda e, _b=bank, _k=k, _m=mo, _h=h:
                                     e.matmul(_b[:],
                                              wk16[:, _k * D + _m * 128:_k * D + (_m + 1) * 128],
                                              xk16[:, _k * CW + _h * 512:_k * CW + (_h + 1) * 512],
                                              start=(_k == 0), stop=(_k == NK - 1)),
                                     after=aft if k == 0 else [],
                                     sig="mmkk" if k == NK - 1 else None)
                    ev_rl = p.op("scalar", lambda e, _b=bank: e.activation(
                        aps[:, 0:512], _b[:], AF.Relu),
                        after=[ev_mm, p.val("kev")], sig="krl")
                    p.op("scalar", lambda e, _m=mo, _h=h: e.activation(
                        ka16[:, _m * CW + _h * 512:_m * CW + (_h + 1) * 512],
                        aps[:, 0:512], AF.Square), after=[ev_rl], sig="kev")
            ev_k = p.val("kev")
            for mo in range(NK):
                for h in range(2):
                    g = (c * NK + mo) * 2 + h
                    bank = p_o[g % 2]
                    aft = [ev_k]
                    if g >= 2:
                        aft.append(p.val("vev"))
                    for k in range(NK):
                        ev_mm = p.op("tensor", lambda e, _b=bank, _k=k, _m=mo, _h=h:
                                     e.matmul(_b[:],
                                              wv16[:, _k * D + _m * 128:_k * D + (_m + 1) * 128],
                                              ka16[:, _k * CW + _h * 512:_k * CW + (_h + 1) * 512],
                                              start=(_k == 0), stop=(_k == NK - 1)),
                                     after=aft if k == 0 else [],
                                     sig="mmk" if k == NK - 1 else None)
                    sl = lambda _m=mo, _h=h: slice(_m * CW + _h * 512, _m * CW + (_h + 1) * 512)
                    ev_vv = p.op("vector", lambda e, _b=bank, _s=sl(): e.tensor_tensor(
                        rt[:, _s], rt[:, _s], _b[:], ALU.mult),
                        after=[ev_mm], sig="vev")
            ev_o = p.val("vev")
            if DBG:
                for k in range(NK):
                    p.dma("sync", dln3[:, k, c * CW:(c + 1) * CW],
                          ln3[:, k, 1:CW + 1], after=[p.val("mix")], sig="dbg")
                    p.dma("sync", dxr3[:, k, c * CW:(c + 1) * CW],
                          xr16[:, k * CW:(k + 1) * CW], after=[p.val("mix")], sig="dbg")
                    p.dma("sync", dr3[:, k, c * CW:(c + 1) * CW],
                          rt[:, k * CW:(k + 1) * CW], after=[p.val("rev")], sig="dbg")
                    p.dma("sync", dka3[:, k, c * CW:(c + 1) * CW],
                          ka16[:, k * CW:(k + 1) * CW], after=[p.val("kev")], sig="dbg")
                    p.dma("sync", do3[:, k, c * CW:(c + 1) * CW],
                          rt[:, k * CW:(k + 1) * CW], after=[ev_o], sig="dbg")
            ev_bc2 = emit_ln_stats(p, st, "sc2" if False else "sbl",
                                   lambda k: rt[:, k * CW:(k + 1) * CW],
                                   NK, D, [ev_o])
            for k in range(NK):
                p.op("vector", lambda e, _k=k: e.tensor_sub(
                    aps[:], rt[:, _k * CW:(_k + 1) * CW], st["mbc"][:]),
                    after=[ev_bc2] if k == 0 else [])
                p.op("vector", lambda e: e.tensor_mul(aps[:], aps[:], st["rbc"][:]))
                ev_x2 = p.op("vector", lambda e, _k=k: e.tensor_add(
                    rt[:, _k * CW:(_k + 1) * CW], aps[:], xl[:, _k * CW:(_k + 1) * CW]),
                    sig="res")
                p.dma("gpsimd", x23[:, k, c * CW:(c + 1) * CW],
                      rt[:, k * CW:(k + 1) * CW], after=[ev_x2], sig="stx")
            p.op("tensor", lambda e: e.nop(), after=[p.val("vev")], sig="mm1")
        p.op("gpsimd", None, after=[p.val("stx")])
    p.barrier()



SC_IO = dict(reads=["x2", "l1w_t"], writes=["x3"])


def emit_sc(p, m, T):
    nc = p.nc
    p.new_scope()
    NCH = T // CW
    x2d = m.d("x2", (D, T), F32)
    l1w_t = m.d("l1w_t", (D, H3), F32)
    x3d = m.d("x3", (D, T), F32)
    x23 = x2d.rearrange("(k p) t -> p k t", p=128)
    x33 = x3d.rearrange("(k p) t -> p k t", p=128)

    with ExitStack() as stack:
        w16 = sbuf(stack, nc, "scw", (128, NK * H3), BF16)
        xl = sbuf(stack, nc, "scx", (128, NK * CW), F32)
        ln16 = sbuf(stack, nc, "scl", (128, NK * CW), BF16)
        hall = sbuf(stack, nc, "sch", (128, NK * CW), F32)
        fk2 = [sbuf(stack, nc, f"scf{i}", (128, CW), F32) for i in range(2)]
        ik2 = [sbuf(stack, nc, f"sci{i}", (128, CW), F32) for i in range(2)]
        hk2 = [sbuf(stack, nc, f"schd{i}", (128, CW), F32) for i in range(2)]
        t1_2 = [sbuf(stack, nc, f"sct1{i}", (128, CW), F32) for i in range(2)]
        t2_2 = [sbuf(stack, nc, f"sct2{i}", (128, CW), F32) for i in range(2)]
        ak2 = [sbuf(stack, nc, f"scak{i}", (128, CW), F32) for i in range(2)]
        bk2 = [sbuf(stack, nc, f"scbk{i}", (128, CW), F32) for i in range(2)]
        aps = bk2[1]
        ccar = sbuf(stack, nc, "sccc", (128, NK), F32)
        stg = hall[:, 0:H3]
        cvw = sbuf(stack, nc, "sccv", (128, 4), F32)
        st = alloc_ln_state(stack, nc, "sc")
        p_o = [psum(stack, nc, f"scp{i}", (128, 512)) for i in range(2)]

        init_ln_state(p, st)
        p.op("vector", lambda e: e.memset(ccar[:], 0.0))
        p.op("vector", lambda e: e.memset(cvw[:, 0:1], 1.0), sig="cv1")
        ev_w = load_w16(p, stg, l1w_t, H3, NK, w16)

        for c in range(NCH):
            ev_lx = p.dma("sync", xl.rearrange("p (k t) -> p k t", k=NK),
                          x23[:, :, c * CW:(c + 1) * CW],
                          after=[p.val("res")], sig="xld")
            ev_bc = emit_ln_stats(p, st, "sc",
                                  lambda k: xl[:, k * CW:(k + 1) * CW],
                                  NK, D, [ev_lx])
            ev_ap = None
            for k in range(NK):
                p.op("vector", lambda e, _k=k: e.tensor_sub(
                    aps[:], xl[:, _k * CW:(_k + 1) * CW], st["mbc"][:]),
                    after=[ev_bc, p.val("mml")] if k == 0 else [])
                ev_ap = p.op("vector", lambda e, _k=k: e.tensor_mul(
                    ln16[:, _k * CW:(_k + 1) * CW], ln16[:, _k * CW:(_k + 1) * CW] if False else aps[:],
                    st["rbc"][:]), sig="apl" if k == NK - 1 else None)
            def emit_gates(k):
                outs = []
                for role, mo in ((fk2[k % 2], k), (ik2[k % 2], NK + k),
                                 (hk2[k % 2], 2 * NK + k)):
                    ev_e = None
                    for h in range(2):
                        bank = p_o[(mo * 2 + h) % 2]
                        aft = [ev_ap, ev_w, p.val("cv1"), p.val("gev")]
                        for kk in range(NK):
                            ev_mm = p.op("tensor", lambda e, _b=bank, _k=kk, _m=mo, _h=h:
                                         e.matmul(_b[:],
                                                  w16[:, _k * H3 + _m * 128:_k * H3 + (_m + 1) * 128],
                                                  ln16[:, _k * CW + _h * 512:_k * CW + (_h + 1) * 512],
                                                  start=(_k == 0), stop=(_k == NK - 1)),
                                         after=aft if kk == 0 else [],
                                         sig="mml" if kk == NK - 1 else None)
                        ev_e = p.op("scalar", lambda e, _b=bank, _r=role, _h=h: e.activation(
                            _r[:, _h * 512:(_h + 1) * 512], _b[:], AF.Identity),
                            after=[ev_mm, p.val("sgv")], sig="gev")
                    outs.append(ev_e)
                return outs

            def chain_steps(k, outs):
                fk = fk2[k % 2]; ik = ik2[k % 2]; hk = hk2[k % 2]
                t1 = t1_2[k % 2]; t2 = t2_2[k % 2]
                ak = ak2[k % 2]; bk = bk2[k % 2]
                ev_rf = p.op("scalar", lambda e: e.activation(t1[:], fk[:], AF.Relu,
                                                              scale=-1.0),
                             after=[outs[0]], sig="c" + str(k % 2))
                yield
                ev_af = p.op("vector", lambda e: e.scalar_tensor_tensor(
                    t2[:], fk[:], -1.0, fk[:], ALU.mult, ALU.max),
                    after=[outs[0]], sig="cv" + str(k % 2))
                yield
                ev_ef = p.op("scalar", lambda e: e.activation(t2[:], t2[:], AF.Exp,
                                                              scale=-1.0),
                             after=[ev_af], sig="c" + str(k % 2))
                ev_lf = p.op("scalar", lambda e: e.activation(t2[:], t2[:], AF.Ln,
                                                              bias=cvw[:, 0:1]),
                             sig="c" + str(k % 2))
                yield
                ev_sf = p.op("vector", lambda e: e.tensor_add(t1[:], t1[:], t2[:]),
                             after=[ev_lf], sig="cv" + str(k % 2))
                yield
                ev_ri = p.op("scalar", lambda e: e.activation(bk[:], ik[:], AF.Relu,
                                                              scale=-1.0),
                             after=[outs[1]], sig="c" + str(k % 2))
                yield
                ev_ai = p.op("vector", lambda e: e.scalar_tensor_tensor(
                    t2[:], ik[:], -1.0, ik[:], ALU.mult, ALU.max),
                    after=[outs[1], ev_sf], sig="cv" + str(k % 2))
                yield
                ev_ei = p.op("scalar", lambda e: e.activation(t2[:], t2[:], AF.Exp,
                                                              scale=-1.0),
                             after=[ev_ai], sig="c" + str(k % 2))
                ev_li = p.op("scalar", lambda e: e.activation(t2[:], t2[:], AF.Ln,
                                                              bias=cvw[:, 0:1]),
                             sig="c" + str(k % 2))
                yield
                ev_si = p.op("vector", lambda e: e.tensor_add(bk[:], bk[:], t2[:]),
                             after=[ev_li, ev_ri], sig="cv" + str(k % 2))
                yield
                ev_df = p.op("vector", lambda e: e.tensor_sub(fk[:], t1[:], bk[:]),
                             sig="dif")
                yield
                ev_a = p.op("scalar", lambda e, _a=ak: e.activation(_a[:], fk[:],
                                                                    AF.Sigmoid,
                                                                    scale=-1.0),
                            after=[ev_df], sig="aev")
                ev_sg = p.op("scalar", lambda e, _b=bk: e.activation(_b[:], hk[:],
                                                                     AF.Sigmoid),
                             after=[ev_df, outs[2]], sig="sgv")
                yield
                ev_s = p.op("vector", lambda e, _a=ak: e.tensor_scalar(
                    t1[:], _a[:], -1.0, 1.0, ALU.mult, ALU.add),
                    after=[ev_a], sig="sev")
                ev_sel = p.op("vector", lambda e, _b=bk: e.scalar_tensor_tensor(
                    t2[:], hk[:], 0.5, _b[:], ALU.add, ALU.max),
                    after=[ev_sg], sig="sel")
                ev_b = p.op("vector", lambda e, _b=bk: e.tensor_mul(_b[:], t1[:], t2[:]),
                            after=[ev_sel, ev_s], sig="bev")
                yield
                p.op("vector", lambda e, _k=k, _a=ak, _b=bk: e.tensor_tensor_scan(
                    hall[:, _k * CW:(_k + 1) * CW], _a[:], _b[:],
                    ccar[:, _k:_k + 1], ALU.mult, ALU.add),
                    after=[ev_a, ev_b], sig="scn")
                yield

            outs_k = emit_gates(0)
            gens = {}
            for k in range(NK):
                gens[k] = chain_steps(k, outs_k)
                if k + 1 < NK:
                    outs_k = emit_gates(k + 1)
                gprev = gens.get(k - 1)
                gcur = gens[k]
                done_p = gprev is None
                done_c = False
                while not (done_p and done_c):
                    if not done_p:
                        done_p = next(gprev, "END") == "END"
                    if not done_c:
                        done_c = next(gcur, "END") == "END"
            for k in range(NK):
                p.op("vector", lambda e, _k=k: e.tensor_copy(
                    ccar[:, _k:_k + 1], hall[:, (_k + 1) * CW - 1:(_k + 1) * CW]),
                    sig="car" if k == NK - 1 else None)
            ev_bc2 = emit_ln_stats(p, st, "scl",
                                   lambda k: hall[:, k * CW:(k + 1) * CW],
                                   NK, D, [p.val("scn")])
            for k in range(NK):
                p.op("vector", lambda e, _k=k: e.tensor_sub(
                    aps[:], hall[:, _k * CW:(_k + 1) * CW], st["mbc"][:]),
                    after=[ev_bc2] if k == 0 else [])
                p.op("vector", lambda e: e.tensor_mul(aps[:], aps[:], st["rbc"][:]))
                ev_x3 = p.op("vector", lambda e, _k=k: e.tensor_add(
                    hall[:, _k * CW:(_k + 1) * CW], aps[:],
                    xl[:, _k * CW:(_k + 1) * CW]), sig="res")
                p.dma("gpsimd", x33[:, k, c * CW:(c + 1) * CW],
                      hall[:, k * CW:(k + 1) * CW], after=[ev_x3], sig="stx")
        p.op("gpsimd", None, after=[p.val("stx")])
    p.barrier()


SD1_IO = dict(reads=["x3", "wg_t", "wu_t"], writes=["hh16"])


def emit_sd1(p, m, T):
    nc = p.nc
    p.new_scope()
    NCH = T // CW
    x3d = m.d("x3", (D, T), F32)
    wg_t = m.d("wg_t", (D, H3), I32)
    wu_t = m.d("wu_t", (D, H3), I32)
    hh16d = m.d("hh16", (H3, T), BF16)
    x33 = x3d.rearrange("(k p) t -> p k t", p=128)
    hh3 = hh16d.rearrange("(k p) t -> p k t", p=128)

    with ExitStack() as stack:
        wg16 = sbuf(stack, nc, "d1wg", (128, NK * H3), BF16)
        wu16 = sbuf(stack, nc, "d1wu", (128, NK * H3), BF16)
        stg = sbuf(stack, nc, "d1st", (128, H3), I32)
        xl = sbuf(stack, nc, "d1x", (128, NK * CW), F32)
        ln16 = sbuf(stack, nc, "d1l", (128, NK * CW), BF16)
        sg = [sbuf(stack, nc, f"d1s{i}", (128, 512), F32) for i in range(2)]
        hhs = [sbuf(stack, nc, f"d1h{i}", (128, 512), BF16) for i in range(2)]
        aps = sbuf(stack, nc, "d1ap", (128, CW), F32)
        st = alloc_ln_state(stack, nc, "d1")
        p_g = psum(stack, nc, "d1pg", (128, 512))
        p_u = psum(stack, nc, "d1pu", (128, 512))

        init_ln_state(p, st)
        load_w16(p, stg, wg_t, H3, NK, wg16, scale=1.0 / 127.0)
        ev_w = load_w16(p, stg, wu_t, H3, NK, wu16, scale=1.0 / 127.0)

        hst_evs = []
        for c in range(NCH):
            ev_lx = p.dma("sync", xl.rearrange("p (k t) -> p k t", k=NK),
                          x33[:, :, c * CW:(c + 1) * CW],
                          after=[p.val("mml")], sig="xld")
            ev_bc = emit_ln_stats(p, st, "d1",
                                  lambda k: xl[:, k * CW:(k + 1) * CW],
                                  NK, D, [ev_lx])
            ev_ap = None
            for k in range(NK):
                p.op("vector", lambda e, _k=k: e.tensor_sub(
                    aps[:], xl[:, _k * CW:(_k + 1) * CW], st["mbc"][:]),
                    after=[ev_bc, p.val("mml")] if k == 0 else [])
                ev_ap = p.op("vector", lambda e, _k=k: e.tensor_mul(
                    ln16[:, _k * CW:(_k + 1) * CW], aps[:], st["rbc"][:]),
                    sig="apl" if k == NK - 1 else None)
            for mo in range(NKH):
                for h in range(2):
                    g = (c * NKH + mo) * 2 + h
                    aft = [ev_ap, ev_w]
                    if g >= 1:
                        aft.append(p.val("sev"))
                    ev_mg = None
                    for kk in range(NK):
                        ev_mg = p.op("tensor", lambda e, _k=kk, _m=mo, _h=h: e.matmul(
                            p_g[:],
                            wg16[:, _k * H3 + _m * 128:_k * H3 + (_m + 1) * 128],
                            ln16[:, _k * CW + _h * 512:_k * CW + (_h + 1) * 512],
                            start=(_k == 0), stop=(_k == NK - 1)),
                            after=aft if kk == 0 else [],
                            sig="mmg" if kk == NK - 1 else None)
                    aft2 = [p.val("hev")] if g >= 1 else []
                    for kk in range(NK):
                        ev_mu = p.op("tensor", lambda e, _k=kk, _m=mo, _h=h: e.matmul(
                            p_u[:],
                            wu16[:, _k * H3 + _m * 128:_k * H3 + (_m + 1) * 128],
                            ln16[:, _k * CW + _h * 512:_k * CW + (_h + 1) * 512],
                            start=(_k == 0), stop=(_k == NK - 1)),
                            after=aft2 if kk == 0 else [],
                            sig="mml" if kk == NK - 1 else None)
                    slot = sg[g % 2]
                    aft3 = [ev_mg] + ([p.val("hev")] if g >= 2 else [])
                    ev_sl = p.op("scalar", lambda e, _s=slot: e.activation(
                        _s[:], p_g[:], AF.Silu), after=aft3, sig="sev")
                    hslot = hhs[g % 2]
                    afth = [ev_sl, ev_mu]
                    if len(hst_evs) >= 2:
                        afth.append(hst_evs[-2])
                    ev_hh = p.op("vector", lambda e, _s=slot, _o=hslot: e.tensor_tensor(
                        _o[:], _s[:], p_u[:], ALU.mult), after=afth, sig="hev")
                    hev = p.dma("gpsimd", hh3[:, mo, c * CW + h * 512:c * CW + (h + 1) * 512],
                                hslot[:], after=[ev_hh], sig="hst")
                    hst_evs.append(hev)
        p.op("gpsimd", None, after=[p.val("hst")])
    p.barrier()


SD2_IO = dict(reads=["hh16", "wd_t", "x3"], writes=["y_out"])


def emit_sd2(p, m, T):
    nc = p.nc
    p.new_scope()
    NCH = T // CW
    hh16d = m.d("hh16", (H3, T), BF16)
    wd_t = m.d("wd_t", (H3, D), I32)
    x3d = m.d("x3", (D, T), F32)
    yod = m.d("y_out", (D, T), F32)
    hh3 = hh16d.rearrange("(k p) t -> p k t", p=128)
    x33 = x3d.rearrange("(k p) t -> p k t", p=128)
    yo3 = yod.rearrange("(k p) t -> p k t", p=128)

    with ExitStack() as stack:
        wd16 = sbuf(stack, nc, "d2wd", (128, NKH * D), BF16)
        stg = sbuf(stack, nc, "d2st", (128, D), I32)
        hl2 = [sbuf(stack, nc, f"d2h{i}", (128, (NKH // 3) * CW), BF16) for i in range(3)]
        xl = sbuf(stack, nc, "d2x", (128, NK * CW), F32)
        ot = sbuf(stack, nc, "d2o", (128, NK * CW), F32)
        aps = sbuf(stack, nc, "d2ap", (128, CW), F32)
        st = alloc_ln_state(stack, nc, "d2")
        p_o = [psum(stack, nc, f"d2p{i}", (128, 512)) for i in range(2)]

        init_ln_state(p, st)
        ev_w = load_w16(p, stg, wd_t, D, NKH, wd16, scale=1.0 / 127.0)

        QH = NKH // 3
        hld_evs = []
        for c in range(NCH):
            for q in range(3):
                qi = c * 3 + q
                aftq = [hld_evs[qi - 3][1]] if qi >= 3 else []
                ev_ld = p.dma("sync", hl2[qi % 3].rearrange("p (k t) -> p k t", k=QH),
                              hh3[:, q * QH:(q + 1) * QH, c * CW:(c + 1) * CW],
                              after=aftq, sig="hld")
                hld_evs.append((ev_ld, None))
            ev_lx = p.dma("sync", xl.rearrange("p (k t) -> p k t", k=NK),
                          x33[:, :, c * CW:(c + 1) * CW],
                          after=[p.val("sty")], sig="xld")
            ev_ev = None
            for mo in range(NK):
                for h in range(2):
                    g = (c * NK + mo) * 2 + h
                    bank = p_o[g % 2]
                    aft0 = [ev_w]
                    if g >= 2:
                        aft0.append(p.val("oev"))
                    ev_mm = None
                    for k in range(NKH):
                        q = k // QH
                        qi = c * 3 + q
                        kq = k % QH
                        aft = (aft0 + [hld_evs[qi][0]]) if kq == 0 else []
                        ev_mm = p.op("tensor", lambda e, _b=bank, _q=qi % 3, _k=kq,
                                     _kk=k, _m=mo, _h=h:
                                     e.matmul(_b[:],
                                              wd16[:, _kk * D + _m * 128:_kk * D + (_m + 1) * 128],
                                              hl2[_q][:, _k * CW + _h * 512:_k * CW + (_h + 1) * 512],
                                              start=(_kk == 0), stop=(_kk == NKH - 1)),
                                     after=aft,
                                     sig="mm1" if k == NKH - 1 else None)
                    for q in range(3):
                        hld_evs[c * 3 + q] = (hld_evs[c * 3 + q][0], ev_mm)
                    ev_ev = p.op("vector", lambda e, _b=bank, _m=mo, _h=h: e.tensor_copy(
                        ot[:, _m * CW + _h * 512:_m * CW + (_h + 1) * 512], _b[:]),
                        after=[ev_mm], sig="oev")
            ev_bc = emit_ln_stats(p, st, "d2",
                                  lambda k: ot[:, k * CW:(k + 1) * CW],
                                  NK, D, [ev_ev])
            for k in range(NK):
                p.op("vector", lambda e, _k=k: e.tensor_sub(
                    aps[:], ot[:, _k * CW:(_k + 1) * CW], st["mbc"][:]),
                    after=[ev_bc, ev_lx] if k == 0 else [])
                p.op("vector", lambda e: e.tensor_mul(aps[:], aps[:], st["rbc"][:]))
                ev_y = p.op("vector", lambda e, _k=k: e.tensor_add(
                    ot[:, _k * CW:(_k + 1) * CW], aps[:],
                    xl[:, _k * CW:(_k + 1) * CW]), sig="res")
                p.dma("gpsimd", yo3[:, k, c * CW:(c + 1) * CW],
                      ot[:, k * CW:(k + 1) * CW], after=[ev_y], sig="sty")
        p.op("gpsimd", None, after=[p.val("sty")])
    p.barrier()


STAGES = {"pf1": (emit_pf1, PF1_IO),
          "scf": (lambda p, m, T: emit_scan(p, m, T, False), SCF_IO),
          "scr": (lambda p, m, T: emit_scan(p, m, T, True), SCR_IO),
          "po1": (emit_po1, PO1_IO),
          "po2": (emit_po2, PO2_IO),
          "sb": (emit_sb, SB_IO),
          "sc": (emit_sc, SC_IO),
          "sd1": (emit_sd1, SD1_IO),
          "sd2": (emit_sd2, SD2_IO)}


def build_stage(name, T):
    emit, io = STAGES[name]
    io = dict(io)
    if os.environ.get("ATH_SB_DEBUG") and name == "sb":
        io["writes"] = list(io["writes"]) + ["sb_ln", "sb_xr", "sb_r", "sb_ka", "sb_o"]
    if os.environ.get("ATH_SCAN_DEBUG") and name in ("scf", "scr"):
        pref = "r" if name == "scr" else "f"
        io["writes"] = list(io["writes"]) + [pref + "ddump", pref + "ydump",
                                             pref + "xc32", pref + "Brow", pref + "Crow"]
    nc = bass.Bass("TRN2", target_bir_lowering=False)
    m = Mem(nc, fused=False, reads=io["reads"], writes=io["writes"])
    p = Prog(nc)
    emit(p, m, T)
    p.emit()
    return nc


def run_stage(name, T, in_maps, core_ids=None):
    nc = build_stage(name, T)
    if core_ids is None:
        core_ids = list(range(len(in_maps)))
    res = run_bass_kernel_spmd(nc, in_maps, core_ids=core_ids)
    return res.results


CHAIN = [
    ("pf1", ["x_fm", "w_in"]),
    ("scf", ["xc_pre", "silu_z", "w_xp", "w_dt", "vecs", "alog_t", "eye16"]),
    ("scr", ["xc_pre", "silu_z", "yF2", "w_xp", "w_dt", "vecs", "alog_t", "eye16"]),
    ("po1", ["ysum16", "w_out", "w_ff1", "vecs"]),
    ("po2", ["g16", "w_ff2", "x_fm", "vecs"]),
    ("sb", ["x1", "wr_t", "wk_t", "wv_t", "vecs"]),
    ("sc", ["x2", "l1w_t"]),
    ("sd1", ["x3", "wg_t", "wu_t"]),
    ("sd2", ["hh16", "wd_t", "x3"]),
]


MODULE_INPUTS = ["x_fm", "w_in", "w_xp", "w_dt", "vecs", "alog_t", "eye16",
                 "w_out", "w_ff1", "w_ff2", "wr_t", "wk_t", "wv_t", "l1w_t",
                 "wg_t", "wu_t", "wd_t"]


def build_fused(T):
    nc = bass.Bass("TRN2", target_bir_lowering=False)
    m = Mem(nc, fused=True, reads=MODULE_INPUTS, writes=["y_out"])
    p = Prog(nc)
    for name, _ in CHAIN:
        STAGES[name][0](p, m, T)
    p.emit()
    return nc


def kernel(**inputs):
    x = np.asarray(inputs["L0_attn_x"])
    B, T, _ = x.shape
    xs, shared = host_prep(inputs, T)
    if os.environ.get("ATH_UNFUSED"):
        state = [dict(shared, x_fm=xs[b]) for b in range(B)]
        for name, reads in CHAIN:
            in_maps = [{r: state[b][r] for r in reads} for b in range(B)]
            res = run_stage(name, T, in_maps, core_ids=list(range(B)))
            for b in range(B):
                for k2, v in res[b].items():
                    state[b][k2] = np.asarray(v)
        return np.stack([state[b]["y_out"].T for b in range(B)]).astype(np.float32)
    nc = build_fused(T)
    in_maps = [dict(shared, x_fm=xs[b]) for b in range(B)]
    res = run_bass_kernel_spmd(nc, in_maps, core_ids=list(range(B)))
    return np.stack([np.asarray(r["y_out"]).T for r in res.results]).astype(np.float32)


# version 14
# speedup vs baseline: 1.1756x; 1.1756x over previous
import os
from contextlib import ExitStack

import numpy as np
import ml_dtypes

import concourse.bass as bass
import concourse.mybir as mybir
from concourse.bass_utils import run_bass_kernel_spmd

F32 = mybir.dt.float32
BF16 = mybir.dt.bfloat16
I32 = mybir.dt.int32
U8 = mybir.dt.uint8
AF = mybir.ActivationFunctionType
ALU = mybir.AluOpType

D = 1024
DI = 2048
DTR = 64
DS = 16
DFF = 4096
H3 = 3072
NK = D // 128
NKI = DI // 128
NKF = DFF // 128
NKH = H3 // 128
CW = 1024
TCS = 4096
EPS = 1e-5

ENGS = ("sync", "vector", "scalar", "tensor", "gpsimd")

VC_CONVW = 0
VC_CONVB = 64
VC_DTB = 80
VC_D = 96
VC_LNG = 112
VC_LNB = 120
VC_FB1 = 128
VC_FB2 = 160
VC_MUR = 168
VC_MUK = 176
VC_N = 192


class Prog:

    NPHYS = 40

    def __init__(self, nc):
        self.nc = nc
        self.q = {e: [] for e in ENGS}
        self.phys = [nc.alloc_semaphore(name=f"sp{i}") for i in range(self.NPHYS)]
        self.physcnt = [0] * self.NPHYS
        self.logmap = {}
        self.next_phys = 0
        self.prod = {}
        self.waited = {e: {} for e in ENGS}
        self.nwaits = 0

    def new_scope(self):
        self.logmap = {}
        self.next_phys = 0

    def _slot(self, name):
        if name not in self.logmap:
            self.logmap[name] = self.next_phys
            self.next_phys += 1
            assert self.next_phys <= self.NPHYS, "semaphore pool exhausted"
        return self.logmap[name]

    def op(self, eng, fn, after=(), sig=None, dn=1, dma=False, force_after=()):
        for ev in list(force_after) + [("_F", e) for e in ()]:
            if ev is None:
                continue
            si, val = ev
            if val <= 0:
                continue
            if self.waited[eng].get(si, 0) >= val:
                continue
            self.q[eng].append(("w", si, val))
            self.waited[eng][si] = val
            self.nwaits += 1
        for ev in after:
            if ev is None:
                continue
            si, val = ev
            if val <= 0:
                continue
            if self.prod.get(si) == eng:
                continue
            if self.waited[eng].get(si, 0) >= val:
                continue
            self.q[eng].append(("w", si, val))
            self.waited[eng][si] = val
            self.nwaits += 1
        ev_out = None
        if sig is not None:
            si = self._slot(sig)
            kind = "dma" if dma else eng
            if si not in self.prod:
                self.prod[si] = kind
            elif self.prod[si] != kind:
                self.prod[si] = "mixed"
            self.physcnt[si] += dn
            ev_out = (si, self.physcnt[si])
        if fn is not None or sig is not None:
            self.q[eng].append(("o", fn, (self.phys[ev_out[0]], dn) if sig else None))
        return ev_out

    def dma(self, eng, out, in_, after=(), sig="dma"):
        def fn(e, _o=out, _i=in_):
            return e.dma_start(out=_o, in_=_i)
        return self.op(eng, fn, after=after, sig=sig, dn=16, dma=True)

    def val(self, sig):
        si = self._slot(sig)
        return (si, self.physcnt[si])

    def barrier(self):
        evs = []
        for e in ENGS:
            if e in ("vector", "gpsimd"):
                evs.append(self.op(e, lambda en: en.engine_nop(), sig="bar"))
            else:
                evs.append(self.op(e, lambda en: en.nop(), sig="bar"))
        last = self.val("bar")
        for e in ENGS:
            self.op(e, None, after=[last])

    def emit(self):
        with self.nc.Block() as block:
            for e in ENGS:
                q = self.q[e]

                def runner(eng, _q=q):
                    for it in _q:
                        if it[0] == "w":
                            eng.wait_ge(self.phys[it[1]], it[2])
                        else:
                            fn, inc = it[1], it[2]
                            inst = fn(eng) if fn is not None else None
                            if inc is not None:
                                inst.then_inc(inc[0], inc[1])
                getattr(block, e)(runner)

    def n_ops(self):
        return sum(len(q) for q in self.q.values())


class Mem:
    def __init__(self, nc, fused, reads=(), writes=()):
        self.nc = nc
        self.fused = fused
        self.reads = set(reads)
        self.writes = set(writes)
        self.t = {}

    def d(self, name, shape, dtype):
        if name in self.t:
            return self.t[name]
        if name in self.reads:
            kind = "ExternalInput"
        elif name in self.writes:
            kind = "ExternalOutput"
        else:
            kind = "Internal"
        self.t[name] = self.nc.dram_tensor(name, list(shape), dtype, kind=kind).ap()
        return self.t[name]


def sbuf(stack, nc, name, shape, dtype):
    return stack.enter_context(nc.sbuf_tensor(name, list(shape), dtype)).ap()


def psum(stack, nc, name, shape):
    return stack.enter_context(nc.psum_tensor(name, list(shape), F32)).ap()


def alloc_ln_state(stack, nc, pref):
    return dict(
        sq=sbuf(stack, nc, pref + "lnsq", (128, CW), F32),
        p_s1=psum(stack, nc, pref + "ps1", (1, CW)),
        p_s2=psum(stack, nc, pref + "ps2", (1, CW)),
        p_bc=psum(stack, nc, pref + "pbc", (128, 512)),
        smu=sbuf(stack, nc, pref + "smu", (1, CW), F32),
        svar=sbuf(stack, nc, pref + "svar", (1, CW), F32),
        rbc=sbuf(stack, nc, pref + "rbc", (128, CW), F32),
        mbc=sbuf(stack, nc, pref + "mbc", (128, CW), F32),
        ones_c=sbuf(stack, nc, pref + "onesc", (128, 1), F32),
        ones_r=sbuf(stack, nc, pref + "onesr", (1, 128), F32),
        cst=sbuf(stack, nc, pref + "cst", (128, 4), F32),
    )


def init_ln_state(p, st):
    p.op("vector", lambda e: e.memset(st["ones_c"][:], 1.0))
    p.op("vector", lambda e: e.memset(st["cst"][:, 0:1], EPS))
    p.op("vector", lambda e: e.memset(st["cst"][:, 1:2], -0.5))
    p.op("vector", lambda e: e.memset(st["cst"][:, 2:3], 1.0))
    p.op("vector", lambda e: e.memset(st["cst"][:, 3:4], 0.5))
    return p.op("vector", lambda e: e.memset(st["ones_r"][:], 1.0), sig="ones")


def emit_ln_stats(p, st, tag, src_tiles, nkc, ndim, after_src):
    sq, p_s1, p_s2, p_bc = st["sq"], st["p_s1"], st["p_s2"], st["p_bc"]
    smu, svar = st["smu"], st["svar"]
    srstd = svar
    rbc, mbc = st["rbc"], st["mbc"]
    ones_c, ones_r = st["ones_c"], st["ones_r"]
    Tg = tag
    ev_statd_prev = p.val(Tg + "std")
    for k in range(nkc):
        src = src_tiles(k)
        ev_sq = p.op("scalar",
                     lambda e, _s=src: e.activation(sq[:], _s, AF.Square),
                     after=list(after_src) + [p.val(Tg + "sqf")],
                     sig=Tg + "sq")
        for h in range(2):
            sl = slice(h * 512, (h + 1) * 512)
            p.op("tensor",
                 lambda e, _s=src, _sl=sl, _k=k: e.matmul(
                     p_s1[:, _sl], ones_c[:], _s[:, _sl],
                     start=(_k == 0), stop=(_k == nkc - 1)),
                 after=(list(after_src) + [ev_statd_prev]) if (k == 0 and h == 0) else [])
            p.op("tensor",
                 lambda e, _sl=sl, _k=k: e.matmul(
                     p_s2[:, _sl], ones_c[:], sq[:, _sl],
                     start=(_k == 0), stop=(_k == nkc - 1)),
                 after=[ev_sq],
                 sig=(Tg + "sqf") if h == 1 else None)
    ev_s12 = p.val(Tg + "sqf")
    inv = 1.0 / ndim
    ev_std = None
    for h in range(2):
        sl = slice(h * 512, (h + 1) * 512)
        p.op("vector", lambda e, _sl=sl: e.tensor_scalar_mul(smu[:, _sl], p_s1[:, _sl], inv),
             after=[ev_s12])
        p.op("vector", lambda e, _sl=sl: e.tensor_mul(svar[:, _sl], smu[:, _sl], smu[:, _sl]))
        ev_std = p.op("vector", lambda e, _sl=sl: e.scalar_tensor_tensor(
            svar[:, _sl], p_s2[:, _sl], inv, svar[:, _sl], ALU.mult, ALU.subtract),
            sig=(Tg + "std") if h == 1 else None)
    cst = st["cst"]
    p.op("scalar", lambda e: e.activation(srstd[:], svar[:], AF.Ln, bias=cst[0:1, 0:1]),
         after=[ev_std])
    ev_rs = p.op("scalar", lambda e: e.activation(srstd[:], srstd[:], AF.Exp,
                                                  scale=cst[0:1, 1:2]),
                 sig=Tg + "rs")
    ev_mu_ok = p.val(Tg + "std")
    ev_bcc = None
    for (vsrc, dst, evp) in ((srstd, rbc, ev_rs), (smu, mbc, ev_mu_ok)):
        for h in range(2):
            sl = slice(h * 512, (h + 1) * 512)
            ev_bm = p.op("tensor", lambda e, _v=vsrc, _sl=sl: e.matmul(
                p_bc[:], ones_r[:], _v[:, _sl], start=True, stop=True),
                after=[evp, p.val(Tg + "bcc")], sig=Tg + "bcm")
            ev_bcc = p.op("vector", lambda e, _d=dst, _sl=sl: e.tensor_copy(
                _d[:, _sl], p_bc[:]), after=[ev_bm], sig=Tg + "bcc")
    return ev_bcc


PF1_IO = dict(reads=["x_fm", "w_in"], writes=["xc_pre", "silu_z"])


def emit_pf1(p, m, T):
    nc = p.nc
    p.new_scope()
    NCH = T // CW
    x_fm = m.d("x_fm", (D, T), F32)
    w_in = m.d("w_in", (D, 2 * DI), F32)
    xc_pre = m.d("xc_pre", (DI, T), F32)
    silu_z = m.d("silu_z", (DI, T), BF16)
    x3 = x_fm.rearrange("(k p) t -> p k t", p=128)
    xcp3 = xc_pre.rearrange("(k p) t -> p k t", p=128)
    slz3 = silu_z.rearrange("(k p) t -> p k t", p=128)

    with ExitStack() as stack:
        wst = sbuf(stack, nc, "f1wst", (128, DI), F32)
        w16 = sbuf(stack, nc, "f1w16", (128, NK * 2 * DI), BF16)
        sx = [sbuf(stack, nc, f"f1sx{i}", (128, NK * CW), F32) for i in range(2)]
        ln16 = [sbuf(stack, nc, f"f1ln{i}", (128, NK * CW), BF16) for i in range(2)]
        ssz = [sbuf(stack, nc, f"f1sz{i}", (128, 512), BF16) for i in range(2)]
        sxc = [sbuf(stack, nc, f"f1xc{i}", (128, 512), F32) for i in range(2)]
        st = alloc_ln_state(stack, nc, "f1")
        p_xz = [psum(stack, nc, f"f1pxz{i}", (128, 512)) for i in range(2)]

        init_ln_state(p, st)
        for k in range(NK):
            for hh in range(2):
                ev = p.dma("sync", wst[:, :],
                           w_in[k * 128:(k + 1) * 128, hh * DI:(hh + 1) * DI],
                           after=[p.val("wcv")], sig="wld")
                p.op("vector", lambda e, _k=k, _h=hh: e.tensor_copy(
                    w16[:, _k * 2 * DI + _h * DI:_k * 2 * DI + (_h + 1) * DI], wst[:, :]),
                    after=[ev], sig="wcv")
        ev_wc_all = p.val("wcv")

        apl_evs = []
        mmz_evs = []
        for c in range(NCH):
            ev_ld = p.dma("sync", sx[c % 2].rearrange("p (k t) -> p k t", k=NK),
                          x3[:, :, c * CW:(c + 1) * CW],
                          after=[apl_evs[c - 2]] if c >= 2 else [],
                          sig="xld")
            xb = sx[c % 2]
            ev_bc = emit_ln_stats(p, st, "f1",
                                  lambda k, _x=xb: _x[:, k * CW:(k + 1) * CW],
                                  NK, D, [ev_ld])
            lnb = ln16[c % 2]
            aft0 = [ev_bc] + ([mmz_evs[c - 2]] if c >= 2 else [])
            ev_apl = None
            for k in range(NK):
                p.op("vector", lambda e, _k=k, _x=xb, _l=lnb: e.tensor_sub(
                    _l[:, _k * CW:(_k + 1) * CW], _x[:, _k * CW:(_k + 1) * CW],
                    st["mbc"][:]),
                    after=aft0 if k == 0 else [])
                ev_apl = p.op("vector", lambda e, _k=k, _l=lnb: e.tensor_mul(
                    _l[:, _k * CW:(_k + 1) * CW], _l[:, _k * CW:(_k + 1) * CW],
                    st["rbc"][:]),
                    sig="apl" if k == NK - 1 else None)
            apl_evs.append(ev_apl)
            zst_evs = []
            xst_evs = []
            ev_mm = None
            for mm in range(32):
                for h in range(2):
                    gidx = (c * 32 + mm) * 2 + h
                    bank = p_xz[gidx % 2]
                    aft = []
                    if mm == 0 and h == 0:
                        aft = [ev_apl, ev_wc_all]
                    if gidx >= 2:
                        pg = gidx - 2
                        if (pg // 2) % 32 < 16:
                            aft.append(p.val("evx"))
                        else:
                            aft.append(p.val("evz"))
                    for k in range(NK):
                        ev_mm = p.op(
                            "tensor",
                            lambda e, _b=bank, _k=k, _m=mm, _h=h, _l=lnb: e.matmul(
                                _b[:],
                                w16[:, _k * 2 * DI + _m * 128:_k * 2 * DI + (_m + 1) * 128],
                                _l[:, _k * CW + _h * 512:_k * CW + (_h + 1) * 512],
                                start=(_k == 0), stop=(_k == NK - 1)),
                            after=aft if k == 0 else [],
                            sig="mmz" if k == NK - 1 else None)
                    if mm < 16:
                        slot = sxc[gidx % 2]
                        aftx = [ev_mm]
                        if len(xst_evs) >= 2:
                            aftx.append(xst_evs[-2])
                        ev_cp = p.op("vector", lambda e, _s=slot, _b=bank:
                                     e.tensor_copy(_s[:], _b[:]),
                                     after=aftx, sig="evx")
                        xev = p.dma("gpsimd",
                                    xcp3[:, mm, c * CW + h * 512:c * CW + (h + 1) * 512],
                                    slot[:], after=[ev_cp], sig="stx")
                        xst_evs.append(xev)
                    else:
                        slot = ssz[gidx % 2]
                        aftz = [ev_mm]
                        if len(zst_evs) >= 2:
                            aftz.append(zst_evs[-2])
                        ev_sl = p.op("scalar", lambda e, _s=slot, _b=bank:
                                     e.activation(_s[:], _b[:], AF.Silu),
                                     after=aftz, sig="evz")
                        zev = p.dma("gpsimd",
                                    slz3[:, mm - 16, c * CW + h * 512:c * CW + (h + 1) * 512],
                                    slot[:], after=[ev_sl], sig="stz")
                        zst_evs.append(zev)
            mmz_evs.append(ev_mm)
        p.op("gpsimd", None, after=[p.val("stx"), p.val("stz")])
    p.barrier()


def host_prep(inputs, T):
    f32 = np.float32
    i = inputs
    out = {}
    xs = []
    for b in range(np.asarray(i["L0_attn_x"]).shape[0]):
        xs.append(np.ascontiguousarray(np.asarray(i["L0_attn_x"])[b, :T, :].T.astype(f32)))
    out["w_in"] = np.ascontiguousarray(np.asarray(i["L0_attn_in_proj_w"]).astype(f32))
    wxp = np.asarray(i["L0_attn_x_proj_w"]).astype(f32)
    wxp_p = np.zeros((DI, 128), f32)
    wxp_p[:, 0:16] = wxp[:, 64:80]
    wxp_p[:, 32:48] = wxp[:, 80:96]
    wxp_p[:, 64:128] = wxp[:, 0:64]
    out["w_xp"] = wxp_p
    out["w_dt"] = np.ascontiguousarray(np.asarray(i["L0_attn_dt_proj_w"]).astype(f32))
    out["w_out"] = np.ascontiguousarray(np.asarray(i["L0_attn_out_proj_w"]).astype(f32))
    out["w_ff1"] = np.ascontiguousarray(np.asarray(i["L0_attn_ff_w1"]).astype(f32))
    out["w_ff2"] = np.ascontiguousarray(np.asarray(i["L0_attn_ff_w2"]).astype(f32))
    out["wr_t"] = np.ascontiguousarray(np.asarray(i["L0_ffn_W_r"]).astype(f32).T)
    out["wk_t"] = np.ascontiguousarray(np.asarray(i["L0_ffn_W_k"]).astype(f32).T)
    out["wv_t"] = np.ascontiguousarray(np.asarray(i["L0_ffn_W_v"]).astype(f32).T)
    out["l1w_t"] = np.ascontiguousarray(np.asarray(i["L1_attn_W"]).astype(f32).T)
    out["wg_t"] = np.ascontiguousarray(np.asarray(i["L1_ffn_w_gate"]).astype(np.int32).T)
    out["wu_t"] = np.ascontiguousarray(np.asarray(i["L1_ffn_w_up"]).astype(np.int32).T)
    out["wd_t"] = np.ascontiguousarray(np.asarray(i["L1_ffn_w_down"]).astype(np.int32).T)

    def col(v):
        v = np.asarray(v, f32)
        n = v.shape[0] // 128
        return v.reshape(n, 128).T

    vecs = np.zeros((128, VC_N), f32)
    cw = np.asarray(i["L0_attn_conv_w"]).astype(f32)
    vecs[:, VC_CONVW:VC_CONVW + 64] = cw.reshape(16, 128, 4).transpose(1, 0, 2).reshape(128, 64)
    vecs[:, VC_CONVB:VC_CONVB + 16] = col(i["L0_attn_conv_b"])
    vecs[:, VC_DTB:VC_DTB + 16] = col(i["L0_attn_dt_proj_b"])
    vecs[:, VC_D:VC_D + 16] = col(i["L0_attn_D"])
    vecs[:, VC_LNG:VC_LNG + 8] = col(i["L0_attn_ln_g"])
    vecs[:, VC_LNB:VC_LNB + 8] = col(i["L0_attn_ln_b"])
    vecs[:, VC_FB1:VC_FB1 + 32] = col(i["L0_attn_ff_b1"])
    vecs[:, VC_FB2:VC_FB2 + 8] = col(i["L0_attn_ff_b2"])
    vecs[:, VC_MUR:VC_MUR + 8] = col(i["L0_ffn_mu_r"])
    vecs[:, VC_MUK:VC_MUK + 8] = col(i["L0_ffn_mu_k"])
    out["vecs"] = vecs
    out["eye16"] = np.eye(128, dtype=ml_dtypes.bfloat16)
    alog = np.asarray(i["L0_attn_A_log"]).astype(f32)
    out["alog_t"] = np.ascontiguousarray(
        alog.reshape(NKI, 128, DS).transpose(1, 0, 2).reshape(128, NKI * DS))
    return xs, out



SCF_IO = dict(reads=["xc_pre", "silu_z", "w_xp", "w_dt", "vecs", "alog_t", "eye16"],
              writes=["yF2"])
SCR_IO = dict(reads=["xc_pre", "silu_z", "yF2", "w_xp", "w_dt", "vecs", "alog_t",
                     "eye16"],
              writes=["ysum16"])


def emit_scan(p, m, T, rev):
    nc = p.nc
    p.new_scope()
    tcs = min(TCS, T)
    assert tcs >= DI
    NTC = T // tcs
    NS = tcs // 512
    pref = "r" if rev else "f"
    xc_pre = m.d("xc_pre", (DI, T), F32)
    silu_z = m.d("silu_z", (DI, T), BF16)
    w_xp = m.d("w_xp", (DI, 128), F32)
    w_dt = m.d("w_dt", (DTR, DI), F32)
    vecs = m.d("vecs", (128, VC_N), F32)
    alog_t = m.d("alog_t", (128, NKI * DS), F32)
    eye16d = m.d("eye16", (128, 128), BF16)
    yF2 = m.d("yF2", (DI, T), BF16)
    xcf32 = m.d(pref + "xc32", (DI, T), F32)
    xcf16 = m.d(pref + "xc16", (DI, T), BF16)
    Brow = m.d(pref + "Brow", (DS, T), BF16)
    Crow = m.d(pref + "Crow", (DS, T), BF16)
    if rev:
        ysum16 = m.d("ysum16", (DI, T), BF16)
        ys3 = ysum16.rearrange("(k p) t -> p k t", p=128)
    DBG = bool(os.environ.get("ATH_SCAN_DEBUG"))
    if DBG:
        ddump = m.d(pref + "ddump", (DI, T), F32)
        dd3 = ddump.rearrange("(k p) t -> p k t", p=128)
        ydump = m.d(pref + "ydump", (DI, T), F32)
        yd3 = ydump.rearrange("(k p) t -> p k t", p=128)

    xcp3 = xc_pre.rearrange("(k p) t -> p k t", p=128)
    slz3 = silu_z.rearrange("(k p) t -> p k t", p=128)
    yF23 = yF2.rearrange("(k p) t -> p k t", p=128)
    xc323 = xcf32.rearrange("(k p) t -> p k t", p=128)
    xc163 = xcf16.rearrange("(k p) t -> p k t", p=128)

    with ExitStack() as stack:
        S = [sbuf(stack, nc, f"{pref}S{i}", (128, tcs + 8), F32) for i in range(6)]
        xcpA, xcpB, acc, xcfA, xcfB, dblS = S
        delta, sv, xc_m, _unused1, _unused2, y_t = S
        gz = sbuf(stack, nc, pref + "gz6", (128, tcs), BF16)
        gy = sbuf(stack, nc, pref + "gy6", (128, tcs), BF16)
        B0 = sbuf(stack, nc, pref + "B0", (128, tcs), BF16)
        B1 = sbuf(stack, nc, pref + "B1", (128, tcs), BF16)
        du16 = sbuf(stack, nc, pref + "du", (128, tcs), BF16)
        a16_one = sbuf(stack, nc, pref + "a6", (128, tcs), BF16)
        a16_2 = [a16_one, a16_one]
        b16 = sbuf(stack, nc, pref + "b6", (128, tcs), BF16)
        h16 = sbuf(stack, nc, pref + "h6", (128, tcs), BF16)
        hc16_one = sbuf(stack, nc, pref + "hc6", (128, tcs), BF16)
        hc16_2 = [hc16_one, hc16_one]
        bcB = [sbuf(stack, nc, pref + f"bB{i}", (128, tcs), BF16) for i in range(2)]
        bcC = sbuf(stack, nc, pref + "bC", (128, tcs), BF16)
        wxp16 = sbuf(stack, nc, pref + "wxp", (128, NKI * 128), BF16)
        wdt16 = sbuf(stack, nc, pref + "wdt", (64, DI), BF16)
        aneg = sbuf(stack, nc, pref + "ang", (128, NKI * DS), F32)
        cvw = sbuf(stack, nc, pref + "cvw", (128, VC_N), F32)
        hcar = sbuf(stack, nc, pref + "hc", (128, NKI * DS), F32)
        cst = sbuf(stack, nc, pref + "cs", (128, 2), F32)
        eye = sbuf(stack, nc, pref + "eye", (128, 128), BF16)
        p_all = psum(stack, nc, pref + "pall", (128, NS * 512))

        p.op("vector", lambda e: e.memset(cst[:, 0:1], 1.0))
        p.op("vector", lambda e: e.memset(hcar[:], 0.0), sig="hc0")
        ev_v = p.dma("sync", cvw[:], vecs, sig="wld")
        ev_ey = p.dma("sync", eye[:], eye16d, sig="wld")
        ev_a0 = p.dma("sync", aneg[:], alog_t, sig="wld")
        ev_e = p.op("scalar", lambda e: e.activation(aneg[:], aneg[:], AF.Exp),
                    after=[ev_a0], sig="ae")
        ev_an = p.op("vector", lambda e: e.tensor_scalar_mul(aneg[:], aneg[:], -1.0),
                     after=[ev_e], sig="ang")
        ev_l = p.dma("sync", dblS[:, 0:NKI * 128].rearrange("p (k c) -> p k c", c=128),
                     w_xp.rearrange("(k p) c -> p k c", p=128), sig="wld")
        ev_wx = p.op("vector", lambda e: e.tensor_copy(
            wxp16[:], dblS[:, 0:NKI * 128]), after=[ev_l], sig="wxc")
        ev_l2 = p.dma("sync", dblS[0:DTR, 0:DI], w_dt, after=[ev_wx], sig="wld")
        ev_wd = p.op("vector", lambda e: e.tensor_copy(wdt16[:], dblS[0:DTR, 0:DI]),
                     after=[ev_l2], sig="wxc")

        dte_evs = []
        for tc in range(NTC):
            if rev:
                lo, hi = T - (tc + 1) * tcs, T - tc * tcs
            else:
                lo, hi = tc * tcs, (tc + 1) * tcs
            scol = tc * tcs
            cva_evs = []
            for mi in range(NKI):
                tile = (xcpA, xcpB)[mi % 2]
                llo, lhi = max(lo - 4, 0), min(hi + 4, T)
                if tc > 0 and mi < 2:
                    aft_ld = [p.val("aev"), p.val("spr"), p.val("dte")]
                elif mi >= 2:
                    aft_ld = [cva_evs[mi - 2]]
                else:
                    aft_ld = []
                ev_ld = p.dma("sync", tile[:, (llo - (lo - 4)):(llo - (lo - 4)) + (lhi - llo)],
                              xcp3[:, mi, llo:lhi], after=aft_ld, sig="cld")
                ev_ms = None
                if (not rev) and lo == 0:
                    ev_ms = p.op("vector", lambda e, _t=tile: e.memset(_t[:, 0:4], 0.0),
                                 sig="cms")
                if rev and hi == T:
                    ev_ms = p.op("vector", lambda e, _t=tile: e.memset(
                        _t[:, tcs + 4:tcs + 8], 0.0), sig="cms")
                aft_t0 = [ev_ld, ev_ms, ev_v]
                if tc > 0 and mi == 0:
                    aft_t0.append(p.val("gat"))
                for i in range(4):
                    if rev:
                        vw = tile[:, tcs + 6 - i: tcs + 6 - i - tcs: -1]
                    else:
                        vw = tile[:, 1 + i: 1 + i + tcs]
                    wcol = cvw[:, VC_CONVW + mi * 4 + i:VC_CONVW + mi * 4 + i + 1]
                    if i == 0:
                        p.op("vector", lambda e, _v=vw, _w=wcol: e.tensor_scalar_mul(
                            acc[:, 0:tcs], _v, _w), after=aft_t0)
                    else:
                        p.op("vector", lambda e, _v=vw, _w=wcol: e.scalar_tensor_tensor(
                            acc[:, 0:tcs], _v, _w, acc[:, 0:tcs], ALU.mult, ALU.add),
                            sig="cva" if i == 3 else None)
                ev_cv = p.val("cva")
                cva_evs.append(ev_cv)
                xcf = (xcfA, xcfB)[mi % 2]
                aft_si = [ev_cv]
                if mi >= 2:
                    aft_si.append(p.val("sxs"))
                ev_si = p.op("scalar", lambda e, _x=xcf, _b=cvw[:, VC_CONVB + mi:VC_CONVB + mi + 1]:
                             e.activation(_x[:, 0:tcs], acc[:, 0:tcs], AF.Silu, bias=_b),
                             after=aft_si, sig="csi")
                b6 = (B0, B1)[mi % 2]
                aft_cv16 = [ev_si]
                if tc > 0 and mi < 2:
                    aft_cv16.append(p.val("dtm"))
                    aft_cv16.append(p.val("yst"))
                if mi >= 2:
                    aft_cv16.append(p.val("sxs"))
                ev_c16 = p.op("vector", lambda e, _b=b6, _x=xcf: e.tensor_copy(
                    _b[:], _x[:, 0:tcs]), after=aft_cv16, sig="c16")
                p.dma("sync", xc323[:, mi, scol:scol + tcs], xcf[:, 0:tcs],
                      after=[ev_si], sig="sxs")
                p.dma("sync", xc163[:, mi, scol:scol + tcs], b6[:],
                      after=[ev_c16], sig="sxs")
            pjk_evs = []
            for k in range(NKI):
                x16 = (B0, B1)[k % 2]
                if k < 2:
                    aft = [p.val("sxs"), p.val("dtm"), p.val("yst"), p.val("bcs")]
                else:
                    aft = [pjk_evs[k - 2]]
                ev_x = p.dma("sync", x16[:], xc163[:, k, scol:scol + tcs],
                             after=aft, sig="xl6")
                for s in range(NS):
                    aftm = [ev_x, ev_wx]
                    if k == 0:
                        aftm += [p.val("yst"), p.val("dte"), p.val("gat")]
                    ev_pk = p.op("tensor", lambda e, _k=k, _s=s, _x=x16: e.matmul(
                        p_all[:, _s * 512:(_s + 1) * 512],
                        wxp16[:, _k * 128:(_k + 1) * 128],
                        _x[:, _s * 512:(_s + 1) * 512],
                        start=(_k == 0), stop=(_k == NKI - 1)),
                        after=aftm,
                        sig=("pj" if k == NKI - 1 else ("pjk" if s == NS - 1 else None)))
                    if s == NS - 1:
                        pjk_evs.append(ev_pk)
            ev_pj = p.val("pj")
            for s in range(NS):
                p.op("scalar", lambda e, _s=s: e.activation(
                    dblS[:, _s * 512:(_s + 1) * 512], p_all[:, _s * 512:(_s + 1) * 512],
                    AF.Identity), after=[ev_pj], sig="dbe")
            ev_dbe = p.val("dbe")
            ev_b48 = p.op("vector", lambda e: e.tensor_copy(
                B1[0:48, 0:tcs], dblS[0:48, 0:tcs]), after=[ev_dbe], sig="b48")
            p.dma("sync", Brow[:, scol:scol + tcs], B1[0:DS, 0:tcs],
                  after=[ev_b48], sig="bcs")
            p.dma("sync", Crow[:, scol:scol + tcs], B1[32:32 + DS, 0:tcs],
                  after=[ev_b48], sig="bcs")
            ev_bcs = p.val("bcs")
            ev_dt16 = p.op("vector", lambda e: e.tensor_copy(
                B0[0:DTR, 0:tcs], dblS[64:128, 0:tcs]),
                after=[ev_dbe, ev_pj], sig="dt6")
            for mi in range(NKI):
                for s2p in range(NS // 2):
                    gb = len(dte_evs)
                    aftm = [ev_dt16, ev_wd, p.val("gat")]
                    if gb >= 1:
                        aftm.append(dte_evs[-1])
                    else:
                        aftm.append(ev_dbe)
                    ev_dm = None
                    for s in (2 * s2p, 2 * s2p + 1):
                        ev_dm = p.op("tensor", lambda e, _mi=mi, _s=s: e.matmul(
                            p_all[:, (_s % 2) * 512:(_s % 2 + 1) * 512],
                            wdt16[:, _mi * 128:(_mi + 1) * 128],
                            B0[0:DTR, _s * 512:(_s + 1) * 512],
                            start=True, stop=True),
                            after=aftm if s % 2 == 0 else [], sig="dtm")
                    afte = [ev_dm, p.val("spr")]
                    if mi == 0 and s2p == 0:
                        afte.append(p.val("cva"))
                    ev_de = p.op("scalar", lambda e, _s=s2p,
                                 _b=cvw[:, VC_DTB + mi:VC_DTB + mi + 1]:
                                 e.activation(sv[:, _s * 1024:(_s + 1) * 1024],
                                              p_all[:, 0:1024],
                                              AF.Identity, bias=_b),
                                 after=afte, sig="dte")
                    dte_evs.append(ev_de)
                ev_v_ready = dte_evs[-1]
                aft_sp = [ev_v_ready]
                if mi == 0:
                    aft_sp.append(p.val("cva"))
                if not (tc == 0 and mi == 0):
                    aft_sp.append(p.val("aev"))
                ev_r = p.op("scalar", lambda e: e.activation(
                    delta[:, 0:tcs], sv[:, 0:tcs], AF.Relu), after=aft_sp, sig="spr")
                aft_ab = [ev_r, p.val("yst")]
                ev_abs = p.op("vector", lambda e: e.scalar_tensor_tensor(
                    y_t[:, 0:tcs], delta[:, 0:tcs], 2.0, sv[:, 0:tcs],
                    ALU.mult, ALU.subtract), after=aft_ab, sig="spa")
                p.op("scalar", lambda e: e.activation(
                    y_t[:, 0:tcs], y_t[:, 0:tcs], AF.Exp, scale=-1.0),
                    after=[ev_abs], sig="spe")
                ev_ln = p.op("scalar", lambda e: e.activation(
                    y_t[:, 0:tcs], y_t[:, 0:tcs], AF.Ln, bias=cst[:, 0:1]), sig="spe")
                ev_dl = p.op("vector", lambda e: e.tensor_add(
                    delta[:, 0:tcs], delta[:, 0:tcs], y_t[:, 0:tcs]),
                    after=[ev_ln], sig="dlt")
                if DBG:
                    p.dma("sync", dd3[:, mi, scol:scol + tcs], delta[:, 0:tcs],
                          after=[ev_dl], sig="dbgs")
                aft_xl = [p.val("csi"), p.val("gat")]
                ev_xm = p.dma("sync", xc_m[:, 0:tcs], xc323[:, mi, scol:scol + tcs],
                              after=aft_xl, sig="xml")
                ev_du = p.op("vector", lambda e: e.tensor_mul(
                    du16[:], delta[:, 0:tcs], xc_m[:, 0:tcs]),
                    after=[ev_xm] + ([p.val("yst")] if not rev else []), sig="duu")
                ev_gz = p.dma("sync", gz[:], slz3[:, mi, lo:hi],
                              after=[p.val("gml")], sig="gzl")
                if rev:
                    ev_gy = p.dma("sync", gy[:], yF23[:, mi, lo:hi],
                                  after=[p.val("ost")], sig="gyl")
                yad_evs = []
                bld_evs = []
                scn_evs = []
                for n in range(DS):
                    bb = bcB[n % 2]
                    if n < 2:
                        aftB = [p.val("bld")]
                    else:
                        aftB = [bld_evs[-2]]
                    ev_bb = p.dma("sync", bb[:],
                                  Brow[n:n + 1, scol:scol + tcs]
                                  .partition_broadcast(128).squeeze(1),
                                  after=aftB + [ev_bcs], sig="bbl")
                    aftC = [p.val("hcm")]
                    ev_cc = p.dma("scalar", bcC[:],
                                  Crow[n:n + 1, scol:scol + tcs]
                                  .partition_broadcast(128).squeeze(1),
                                  after=aftC + [ev_bcs], sig="ccl")
                    a16 = a16_2[n % 2]
                    aft_a = [ev_dl, ev_an]
                    if scn_evs:
                        aft_a.append(scn_evs[-1])
                    elif not (tc == 0 and mi == 0):
                        aft_a.append(p.val("scn"))
                    ev_a = p.op("scalar", lambda e, _mi=mi, _n=n, _a=a16: e.activation(
                        _a[:], delta[:, 0:tcs], AF.Exp,
                        scale=aneg[:, _mi * DS + _n:_mi * DS + _n + 1]),
                        after=aft_a, sig="aev")
                    ev_b = p.op("vector", lambda e, _b=bb: e.tensor_tensor(
                        b16[:], du16[:], _b[:], ALU.mult),
                        after=[ev_bb, ev_du], sig="bld")
                    bld_evs.append(ev_b)
                    col = mi * DS + n
                    ev_h = p.op("vector", lambda e, _c=col, _a=a16: e.tensor_tensor_scan(
                        h16[:], _a[:], b16[:],
                        hcar[:, _c:_c + 1], ALU.mult, ALU.add),
                        after=[ev_a], sig="scn")
                    scn_evs.append(ev_h)
                    hc16 = hc16_2[n % 2]
                    aft_hc = [ev_cc]
                    if yad_evs:
                        aft_hc.append(yad_evs[-1])
                    elif not (tc == 0 and mi == 0):
                        aft_hc.append(p.val("yad"))
                    ev_hc = p.op("vector", lambda e, _o=hc16: e.tensor_tensor(
                        _o[:], h16[:], bcC[:], ALU.mult),
                        after=aft_hc, sig="hcm")
                    aft_ya = [ev_hc, ev_ey]
                    if n == 0:
                        aft_ya += [ev_v_ready, p.val("gat")]
                    ev_ya = None
                    for s2 in range(NS):
                        ev_ya = p.op("tensor", lambda e, _s=s2, _i=hc16,
                                     _st=(n == 0), _sp=(n == DS - 1): e.matmul(
                            p_all[:, _s * 512:(_s + 1) * 512], eye[:],
                            _i[:, _s * 512:(_s + 1) * 512],
                            start=_st, stop=_sp),
                            after=aft_ya if s2 == 0 else [],
                            sig="yad" if s2 == NS - 1 else None)
                    yad_evs.append(ev_ya)
                    p.op("vector", lambda e, _c=col: e.tensor_copy(
                        hcar[:, _c:_c + 1], h16[:, tcs - 1:tcs]))

                aft_g = [yad_evs[-1], p.val("ost" if rev else "yst")]
                ev_g = None
                for s2 in range(NS):
                    ev_g = p.op("vector", lambda e, _mi=mi, _s=s2: e.scalar_tensor_tensor(
                        y_t[:, _s * 512:(_s + 1) * 512],
                        xc_m[:, _s * 512:(_s + 1) * 512],
                        cvw[:, VC_D + _mi:VC_D + _mi + 1],
                        p_all[:, _s * 512:(_s + 1) * 512],
                        ALU.mult, ALU.add),
                        after=aft_g if s2 == 0 else [], sig="gat")
                if DBG:
                    p.dma("sync", yd3[:, mi, scol:scol + tcs], y_t[:, 0:tcs],
                          after=[ev_g], sig="dbgs")
                if rev:
                    ev_gm = p.op("vector", lambda e: e.tensor_tensor(
                        y_t[:, 0:tcs], y_t[:, 0:tcs],
                        gz[:, tcs - 1::-1], ALU.mult),
                        after=[ev_gz], force_after=[ev_g], sig="gml")
                    ev_o = p.op("vector", lambda e: e.tensor_add(
                        B1[:], y_t[:, tcs - 1::-1], gy[:]),
                        after=[ev_gy, p.val("yst"), p.val("bcs")],
                        force_after=[ev_gm], sig="ost")
                    p.dma("sync", ys3[:, mi, lo:hi], B1[:], after=[ev_o], sig="yst")
                else:
                    ev_o = p.op("vector", lambda e: e.tensor_tensor(
                        du16[:], y_t[:, 0:tcs], gz[:], ALU.mult),
                        after=[ev_gz, p.val("yst")], sig="gml")
                    p.dma("sync", yF23[:, mi, lo:hi], du16[:],
                          after=[ev_o], sig="yst")
        p.op("sync", None, after=[p.val("yst"), p.val("sxs"), p.val("bcs")])
    p.barrier()


def load_w16(p, stg, dram, Mcols, kchunks, dest, scale=None, tag=None):
    evs = []
    for k in range(kchunks):
        ev = p.dma("sync", stg[:, 0:Mcols], dram[k * 128:(k + 1) * 128, :],
                   after=[p.val("lwc")], sig="lwl")
        if scale is None:
            ev2 = p.op("vector", lambda e, _k=k: e.tensor_copy(
                dest[:, _k * Mcols:(_k + 1) * Mcols], stg[:, 0:Mcols]),
                after=[ev], sig="lwc")
        else:
            ev2 = p.op("vector", lambda e, _k=k: e.tensor_scalar_mul(
                dest[:, _k * Mcols:(_k + 1) * Mcols], stg[:, 0:Mcols], scale),
                after=[ev], sig="lwc")
        evs.append(ev2)
    return evs[-1]


PO1_IO = dict(reads=["ysum16", "w_out", "w_ff1", "vecs"], writes=["g16"])


def emit_po1(p, m, T):
    nc = p.nc
    p.new_scope()
    NCH = T // CW
    ysum16 = m.d("ysum16", (DI, T), BF16)
    w_out = m.d("w_out", (DI, D), F32)
    w_ff1 = m.d("w_ff1", (D, DFF), F32)
    vecs = m.d("vecs", (128, VC_N), F32)
    g16d = m.d("g16", (DFF, T), BF16)
    ys3 = ysum16.rearrange("(k p) t -> p k t", p=128)
    g3 = g16d.rearrange("(k p) t -> p k t", p=128)

    with ExitStack() as stack:
        wo16 = sbuf(stack, nc, "o1wo", (128, NKI * D), BF16)
        wf16 = sbuf(stack, nc, "o1wf", (128, NK * DFF), BF16)
        ysl2 = [sbuf(stack, nc, f"o1ys{i}", (128, (NKI // 2) * CW), BF16) for i in range(2)]
        mot = sbuf(stack, nc, "o1mo", (128, NK * CW), F32)
        aps = sbuf(stack, nc, "o1ap", (128, CW), F32)
        h16 = sbuf(stack, nc, "o1h", (128, NK * CW), BF16)
        gsl = [sbuf(stack, nc, f"o1g{i}", (128, 512), BF16) for i in range(2)]
        cvw = sbuf(stack, nc, "o1cv", (128, VC_N), F32)
        st = alloc_ln_state(stack, nc, "o1")
        p_o = [psum(stack, nc, f"o1p{i}", (128, 512)) for i in range(2)]

        init_ln_state(p, st)
        p.dma("sync", cvw[:], vecs, sig="cvl")
        ev_cv = p.val("cvl")
        ev_w1 = load_w16(p, mot, w_out, D, NKI, wo16)
        ev_w2 = load_w16(p, mot, w_ff1, DFF, NK, wf16)

        HY = NKI // 2
        yld_evs = []
        for c in range(NCH):
            for q in range(2):
                qi = c * 2 + q
                aftq = [yld_evs[qi - 2][1]] if qi >= 2 else []
                ev_ld = p.dma("sync", ysl2[qi % 2].rearrange("p (k t) -> p k t", k=HY),
                              ys3[:, q * HY:(q + 1) * HY, c * CW:(c + 1) * CW],
                              after=aftq, sig="yld")
                yld_evs.append((ev_ld, None))
            ev_ev = None
            for mo in range(NK):
                for h in range(2):
                    g = (c * NK + mo) * 2 + h
                    bank = p_o[g % 2]
                    aft0 = [ev_w1, ev_w2, ev_cv]
                    if g >= 2:
                        aft0.append(p.val("oev"))
                    ev_mm = None
                    for k in range(NKI):
                        q = k // HY
                        qi = c * 2 + q
                        kq = k % HY
                        aft = (aft0 + [yld_evs[qi][0]]) if kq == 0 else []
                        ev_mm = p.op("tensor", lambda e, _b=bank, _q=qi % 2, _k=kq,
                                     _kk=k, _m=mo, _h=h:
                                     e.matmul(_b[:],
                                              wo16[:, _kk * D + _m * 128:_kk * D + (_m + 1) * 128],
                                              ysl2[_q][:, _k * CW + _h * 512:_k * CW + (_h + 1) * 512],
                                              start=(_kk == 0), stop=(_kk == NKI - 1)),
                                     after=aft,
                                     sig="mm1" if k == NKI - 1 else None)
                    for q in range(2):
                        yld_evs[c * 2 + q] = (yld_evs[c * 2 + q][0], ev_mm)
                    ev_ev = p.op("vector", lambda e, _b=bank, _m=mo, _h=h: e.tensor_copy(
                        mot[:, _m * CW + _h * 512:_m * CW + (_h + 1) * 512], _b[:]),
                        after=[ev_mm], sig="oev")
            ev_bc = emit_ln_stats(p, st, "o1",
                                  lambda k: mot[:, k * CW:(k + 1) * CW],
                                  NK, D, [ev_ev])
            ev_ap = None
            for k in range(NK):
                p.op("vector", lambda e, _k=k: e.tensor_sub(
                    aps[:], mot[:, _k * CW:(_k + 1) * CW], st["mbc"][:]),
                    after=[ev_bc] if k == 0 else [])
                p.op("vector", lambda e: e.tensor_mul(aps[:], aps[:], st["rbc"][:]))
                ev_ap = p.op("vector", lambda e, _k=k: e.tensor_scalar(
                    h16[:, _k * CW:(_k + 1) * CW], aps[:],
                    cvw[:, VC_LNG + _k:VC_LNG + _k + 1],
                    cvw[:, VC_LNB + _k:VC_LNB + _k + 1],
                    ALU.mult, ALU.add), sig="apl" if k == NK - 1 else None)
            gst_evs = []
            for mf in range(NKF):
                for h in range(2):
                    g = (c * NKF + mf) * 2 + h
                    bank = p_o[g % 2]
                    aft = [ev_ap]
                    if g >= 2:
                        aft.append(p.val("gev"))
                    for k in range(NK):
                        ev_mm = p.op("tensor", lambda e, _b=bank, _k=k, _m=mf, _h=h:
                                     e.matmul(_b[:],
                                              wf16[:, _k * DFF + _m * 128:_k * DFF + (_m + 1) * 128],
                                              h16[:, _k * CW + _h * 512:_k * CW + (_h + 1) * 512],
                                              start=(_k == 0), stop=(_k == NK - 1)),
                                     after=aft if k == 0 else [],
                                     sig="mm2" if k == NK - 1 else None)
                    slot = gsl[g % 2]
                    aftg = [ev_mm]
                    if len(gst_evs) >= 2:
                        aftg.append(gst_evs[-2])
                    ev_g = p.op("scalar", lambda e, _s=slot, _b=bank, _m=mf: e.activation(
                        _s[:], _b[:], AF.Gelu,
                        bias=cvw[:, VC_FB1 + _m:VC_FB1 + _m + 1]),
                        after=aftg, sig="gev")
                    gev = p.dma("gpsimd", g3[:, mf, c * CW + h * 512:c * CW + (h + 1) * 512],
                                slot[:], after=[ev_g], sig="gst")
                    gst_evs.append(gev)
        p.op("gpsimd", None, after=[p.val("gst")])
    p.barrier()


PO2_IO = dict(reads=["g16", "w_ff2", "x_fm", "vecs"], writes=["x1"])


def emit_po2(p, m, T):
    nc = p.nc
    p.new_scope()
    NCH = T // CW
    g16d = m.d("g16", (DFF, T), BF16)
    w_ff2 = m.d("w_ff2", (DFF, D), F32)
    x_fm = m.d("x_fm", (D, T), F32)
    vecs = m.d("vecs", (128, VC_N), F32)
    x1d = m.d("x1", (D, T), F32)
    g3 = g16d.rearrange("(k p) t -> p k t", p=128)
    x3 = x_fm.rearrange("(k p) t -> p k t", p=128)
    x13 = x1d.rearrange("(k p) t -> p k t", p=128)

    with ExitStack() as stack:
        wf16 = sbuf(stack, nc, "o2wf", (128, NKF * D), BF16)
        gl2 = [sbuf(stack, nc, f"o2g{i}", (128, (NKF // 4) * CW), BF16) for i in range(3)]
        xl = sbuf(stack, nc, "o2x", (128, NK * CW), F32)
        ot = sbuf(stack, nc, "o2o", (128, NK * CW), F32)
        aps = sbuf(stack, nc, "o2ap", (128, CW), F32)
        cvw = sbuf(stack, nc, "o2cv", (128, VC_N), F32)
        st = alloc_ln_state(stack, nc, "o2")
        p_o = [psum(stack, nc, f"o2p{i}", (128, 512)) for i in range(2)]

        init_ln_state(p, st)
        p.dma("sync", cvw[:], vecs, sig="cvl")
        ev_cv = p.val("cvl")
        ev_w1 = load_w16(p, ot, w_ff2, D, NKF, wf16)

        QK = NKF // 4
        gld_evs = []
        for c in range(NCH):
            ev_lx = p.dma("sync", xl.rearrange("p (k t) -> p k t", k=NK),
                          x3[:, :, c * CW:(c + 1) * CW],
                          after=[p.val("stx")], sig="xld")
            ev_ev = None
            for half in range(2):
                for q in range(2):
                    qi = c * 4 + half * 2 + q
                    aftq = [gld_evs[qi - 3][1]] if qi >= 3 else []
                    ev_ld = p.dma(("sync", "gpsimd")[qi % 2],
                                  gl2[qi % 3].rearrange("p (k t) -> p k t", k=QK),
                                  g3[:, (half * 2 + q) * QK:(half * 2 + q + 1) * QK,
                                     c * CW:(c + 1) * CW],
                                  after=aftq, sig="gld" + str(qi % 2))
                    gld_evs.append((ev_ld, None))
                for mo in range(NK):
                    for h in range(2):
                        g = ((c * 2 + half) * NK + mo) * 2 + h
                        bank = p_o[g % 2]
                        aft0 = [ev_w1, ev_cv]
                        if g >= 2:
                            aft0.append(p.val("oev"))
                        ev_mm = None
                        for k in range(2 * QK):
                            q = k // QK
                            qi = c * 4 + half * 2 + q
                            kk = half * 2 * QK + k
                            kq = k % QK
                            aft = (aft0 + [gld_evs[qi][0]]) if kq == 0 else []
                            ev_mm = p.op("tensor", lambda e, _b=bank, _q=qi % 3, _k=kq,
                                         _kk=kk, _m=mo, _h=h:
                                         e.matmul(_b[:],
                                                  wf16[:, _kk * D + _m * 128:_kk * D + (_m + 1) * 128],
                                                  gl2[_q][:, _k * CW + _h * 512:_k * CW + (_h + 1) * 512],
                                                  start=(_kk % (2 * QK) == 0),
                                                  stop=(_kk % (2 * QK) == 2 * QK - 1)),
                                         after=aft,
                                         sig="mm1" if k == 2 * QK - 1 else None)
                        for q in range(2):
                            qi = c * 4 + half * 2 + q
                            gld_evs[qi] = (gld_evs[qi][0], ev_mm)
                        if half == 0:
                            ev_ev = p.op("vector", lambda e, _b=bank, _m=mo, _h=h:
                                         e.tensor_copy(
                                             ot[:, _m * CW + _h * 512:_m * CW + (_h + 1) * 512],
                                             _b[:]),
                                         after=[ev_mm], sig="oev")
                        else:
                            sl = slice(mo * CW + h * 512, mo * CW + (h + 1) * 512)
                            ev_ev = p.op("vector", lambda e, _b=bank, _s=sl: e.scalar_tensor_tensor(
                                ot[:, _s], _b[:],
                                cvw[:, VC_FB2 + (_s.start // CW):VC_FB2 + (_s.start // CW) + 1],
                                ot[:, _s], ALU.add, ALU.add),
                                after=[ev_mm], sig="oev")
            ev_bc = emit_ln_stats(p, st, "o2",
                                  lambda k: ot[:, k * CW:(k + 1) * CW],
                                  NK, D, [ev_ev])
            for k in range(NK):
                p.op("vector", lambda e, _k=k: e.tensor_sub(
                    aps[:], ot[:, _k * CW:(_k + 1) * CW], st["mbc"][:]),
                    after=[ev_bc, ev_lx] if k == 0 else [])
                p.op("vector", lambda e: e.tensor_mul(aps[:], aps[:], st["rbc"][:]))
                ev_x1 = p.op("vector", lambda e, _k=k: e.tensor_add(
                    ot[:, _k * CW:(_k + 1) * CW], aps[:], xl[:, _k * CW:(_k + 1) * CW]),
                    sig="res")
                p.dma("gpsimd", x13[:, k, c * CW:(c + 1) * CW],
                      ot[:, k * CW:(k + 1) * CW], after=[ev_x1], sig="stx")
        p.op("gpsimd", None, after=[p.val("stx")])
    p.barrier()


SB_IO = dict(reads=["x1", "wr_t", "wk_t", "wv_t", "vecs"], writes=["x2"])


def emit_sb(p, m, T):
    nc = p.nc
    p.new_scope()
    NCH = T // CW
    x1d = m.d("x1", (D, T), F32)
    wr_t = m.d("wr_t", (D, D), F32)
    wk_t = m.d("wk_t", (D, D), F32)
    wv_t = m.d("wv_t", (D, D), F32)
    vecs = m.d("vecs", (128, VC_N), F32)
    x2d = m.d("x2", (D, T), F32)
    x13 = x1d.rearrange("(k p) t -> p k t", p=128)
    x23 = x2d.rearrange("(k p) t -> p k t", p=128)
    DBG = bool(os.environ.get("ATH_SB_DEBUG"))
    if DBG:
        dln = m.d("sb_ln", (D, T), BF16)
        dxr = m.d("sb_xr", (D, T), BF16)
        dr = m.d("sb_r", (D, T), F32)
        dka = m.d("sb_ka", (D, T), BF16)
        do = m.d("sb_o", (D, T), F32)
        dln3 = dln.rearrange("(k p) t -> p k t", p=128)
        dxr3 = dxr.rearrange("(k p) t -> p k t", p=128)
        dr3 = dr.rearrange("(k p) t -> p k t", p=128)
        dka3 = dka.rearrange("(k p) t -> p k t", p=128)
        do3 = do.rearrange("(k p) t -> p k t", p=128)

    with ExitStack() as stack:
        wr16 = sbuf(stack, nc, "sbwr", (128, NK * D), BF16)
        wk16 = sbuf(stack, nc, "sbwk", (128, NK * D), BF16)
        wv16 = sbuf(stack, nc, "sbwv", (128, NK * D), BF16)
        xl = sbuf(stack, nc, "sbx", (128, NK * CW), F32)
        lnf = sbuf(stack, nc, "sbln", (128, NK * (CW + 1)), BF16)
        xr16 = sbuf(stack, nc, "sbxr", (128, NK * CW), BF16)
        xk16 = sbuf(stack, nc, "sbxk", (128, NK * CW), BF16)
        rt = sbuf(stack, nc, "sbr", (128, NK * CW), F32)
        ka16 = sbuf(stack, nc, "sbka", (128, NK * CW), BF16)
        aps = sbuf(stack, nc, "sbap", (128, CW), F32)
        cvw = sbuf(stack, nc, "sbcv", (128, VC_N), F32)
        st = alloc_ln_state(stack, nc, "sb")
        p_o = [psum(stack, nc, f"sbp{i}", (128, 512)) for i in range(2)]

        init_ln_state(p, st)
        p.dma("sync", cvw[:], vecs, sig="cvl")
        ev_cv = p.val("cvl")
        load_w16(p, rt, wr_t, D, NK, wr16)
        load_w16(p, rt, wk_t, D, NK, wk16)
        ev_w = load_w16(p, rt, wv_t, D, NK, wv16)
        ln3 = lnf.rearrange("p (k t) -> p k t", t=CW + 1)
        p.op("vector", lambda e: e.memset(ln3[:, :, 0:1], 0.0), sig="car")

        for c in range(NCH):
            ev_lx = p.dma("sync", xl.rearrange("p (k t) -> p k t", k=NK),
                          x13[:, :, c * CW:(c + 1) * CW],
                          after=[p.val("res")], sig="xld")
            ev_bc = emit_ln_stats(p, st, "sb",
                                  lambda k: xl[:, k * CW:(k + 1) * CW],
                                  NK, D, [ev_lx])
            if c > 0:
                p.op("vector", lambda e: e.tensor_copy(
                    ln3[:, :, 0:1], ln3[:, :, CW:CW + 1]))
            ev_ap = None
            for k in range(NK):
                p.op("vector", lambda e, _k=k: e.tensor_sub(
                    aps[:], xl[:, _k * CW:(_k + 1) * CW], st["mbc"][:]),
                    after=[ev_bc, p.val("mm1")] if k == 0 else [])
                ev_ap = p.op("vector", lambda e, _k=k: e.tensor_mul(
                    ln3[:, _k, 1:CW + 1], aps[:], st["rbc"][:]), sig="apl")
            ev_mx = None
            for k in range(NK):
                p.op("vector", lambda e, _k=k: e.tensor_sub(
                    aps[:], ln3[:, _k, 1:CW + 1], ln3[:, _k, 0:CW]))
                p.op("vector", lambda e, _k=k: e.scalar_tensor_tensor(
                    xr16[:, _k * CW:(_k + 1) * CW], aps[:],
                    cvw[:, VC_MUR + _k:VC_MUR + _k + 1],
                    ln3[:, _k, 0:CW], ALU.mult, ALU.add))
                ev_mx = p.op("vector", lambda e, _k=k: e.scalar_tensor_tensor(
                    xk16[:, _k * CW:(_k + 1) * CW], aps[:],
                    cvw[:, VC_MUK + _k:VC_MUK + _k + 1],
                    ln3[:, _k, 0:CW], ALU.mult, ALU.add),
                    sig="mix" if k == NK - 1 else None)
            for mo in range(NK):
                for h in range(2):
                    g = (c * NK + mo) * 2 + h
                    bank = p_o[g % 2]
                    aft = [ev_mx, ev_w]
                    if g >= 2:
                        aft.append(p.val("rev"))
                    for k in range(NK):
                        ev_mm = p.op("tensor", lambda e, _b=bank, _k=k, _m=mo, _h=h:
                                     e.matmul(_b[:],
                                              wr16[:, _k * D + _m * 128:_k * D + (_m + 1) * 128],
                                              xr16[:, _k * CW + _h * 512:_k * CW + (_h + 1) * 512],
                                              start=(_k == 0), stop=(_k == NK - 1)),
                                     after=aft if k == 0 else [],
                                     sig="mmr" if k == NK - 1 else None)
                    p.op("scalar", lambda e, _b=bank, _m=mo, _h=h: e.activation(
                        rt[:, _m * CW + _h * 512:_m * CW + (_h + 1) * 512], _b[:],
                        AF.Sigmoid), after=[ev_mm], sig="rev")
            ev_r = p.val("rev")
            for mo in range(NK):
                for h in range(2):
                    g = (c * NK + mo) * 2 + h
                    bank = p_o[g % 2]
                    aft = [ev_r]
                    if g >= 2:
                        aft.append(p.val("kev"))
                    for k in range(NK):
                        ev_mm = p.op("tensor", lambda e, _b=bank, _k=k, _m=mo, _h=h:
                                     e.matmul(_b[:],
                                              wk16[:, _k * D + _m * 128:_k * D + (_m + 1) * 128],
                                              xk16[:, _k * CW + _h * 512:_k * CW + (_h + 1) * 512],
                                              start=(_k == 0), stop=(_k == NK - 1)),
                                     after=aft if k == 0 else [],
                                     sig="mmkk" if k == NK - 1 else None)
                    ev_rl = p.op("scalar", lambda e, _b=bank: e.activation(
                        aps[:, 0:512], _b[:], AF.Relu),
                        after=[ev_mm, p.val("kev")], sig="krl")
                    p.op("scalar", lambda e, _m=mo, _h=h: e.activation(
                        ka16[:, _m * CW + _h * 512:_m * CW + (_h + 1) * 512],
                        aps[:, 0:512], AF.Square), after=[ev_rl], sig="kev")
            ev_k = p.val("kev")
            for mo in range(NK):
                for h in range(2):
                    g = (c * NK + mo) * 2 + h
                    bank = p_o[g % 2]
                    aft = [ev_k]
                    if g >= 2:
                        aft.append(p.val("vev"))
                    for k in range(NK):
                        ev_mm = p.op("tensor", lambda e, _b=bank, _k=k, _m=mo, _h=h:
                                     e.matmul(_b[:],
                                              wv16[:, _k * D + _m * 128:_k * D + (_m + 1) * 128],
                                              ka16[:, _k * CW + _h * 512:_k * CW + (_h + 1) * 512],
                                              start=(_k == 0), stop=(_k == NK - 1)),
                                     after=aft if k == 0 else [],
                                     sig="mmk" if k == NK - 1 else None)
                    sl = lambda _m=mo, _h=h: slice(_m * CW + _h * 512, _m * CW + (_h + 1) * 512)
                    ev_vv = p.op("vector", lambda e, _b=bank, _s=sl(): e.tensor_tensor(
                        rt[:, _s], rt[:, _s], _b[:], ALU.mult),
                        after=[ev_mm], sig="vev")
            ev_o = p.val("vev")
            if DBG:
                for k in range(NK):
                    p.dma("sync", dln3[:, k, c * CW:(c + 1) * CW],
                          ln3[:, k, 1:CW + 1], after=[p.val("mix")], sig="dbg")
                    p.dma("sync", dxr3[:, k, c * CW:(c + 1) * CW],
                          xr16[:, k * CW:(k + 1) * CW], after=[p.val("mix")], sig="dbg")
                    p.dma("sync", dr3[:, k, c * CW:(c + 1) * CW],
                          rt[:, k * CW:(k + 1) * CW], after=[p.val("rev")], sig="dbg")
                    p.dma("sync", dka3[:, k, c * CW:(c + 1) * CW],
                          ka16[:, k * CW:(k + 1) * CW], after=[p.val("kev")], sig="dbg")
                    p.dma("sync", do3[:, k, c * CW:(c + 1) * CW],
                          rt[:, k * CW:(k + 1) * CW], after=[ev_o], sig="dbg")
            ev_bc2 = emit_ln_stats(p, st, "sc2" if False else "sbl",
                                   lambda k: rt[:, k * CW:(k + 1) * CW],
                                   NK, D, [ev_o])
            for k in range(NK):
                p.op("vector", lambda e, _k=k: e.tensor_sub(
                    aps[:], rt[:, _k * CW:(_k + 1) * CW], st["mbc"][:]),
                    after=[ev_bc2] if k == 0 else [])
                p.op("vector", lambda e: e.tensor_mul(aps[:], aps[:], st["rbc"][:]))
                ev_x2 = p.op("vector", lambda e, _k=k: e.tensor_add(
                    rt[:, _k * CW:(_k + 1) * CW], aps[:], xl[:, _k * CW:(_k + 1) * CW]),
                    sig="res")
                p.dma("gpsimd", x23[:, k, c * CW:(c + 1) * CW],
                      rt[:, k * CW:(k + 1) * CW], after=[ev_x2], sig="stx")
            p.op("tensor", lambda e: e.nop(), after=[p.val("vev")], sig="mm1")
        p.op("gpsimd", None, after=[p.val("stx")])
    p.barrier()



SC_IO = dict(reads=["x2", "l1w_t"], writes=["x3"])


def emit_sc(p, m, T):
    nc = p.nc
    p.new_scope()
    NCH = T // CW
    x2d = m.d("x2", (D, T), F32)
    l1w_t = m.d("l1w_t", (D, H3), F32)
    x3d = m.d("x3", (D, T), F32)
    x23 = x2d.rearrange("(k p) t -> p k t", p=128)
    x33 = x3d.rearrange("(k p) t -> p k t", p=128)

    with ExitStack() as stack:
        w16 = sbuf(stack, nc, "scw", (128, NK * H3), BF16)
        xl = sbuf(stack, nc, "scx", (128, NK * CW), F32)
        ln16 = sbuf(stack, nc, "scl", (128, NK * CW), BF16)
        hall = sbuf(stack, nc, "sch", (128, NK * CW), F32)
        fk2 = [sbuf(stack, nc, f"scf{i}", (128, CW), F32) for i in range(2)]
        ik2 = [sbuf(stack, nc, f"sci{i}", (128, CW), F32) for i in range(2)]
        hk2 = [sbuf(stack, nc, f"schd{i}", (128, CW), F32) for i in range(2)]
        t1_2 = [sbuf(stack, nc, f"sct1{i}", (128, CW), F32) for i in range(2)]
        t2_2 = [sbuf(stack, nc, f"sct2{i}", (128, CW), F32) for i in range(2)]
        ak2 = [sbuf(stack, nc, f"scak{i}", (128, CW), F32) for i in range(2)]
        bk2 = [sbuf(stack, nc, f"scbk{i}", (128, CW), F32) for i in range(2)]
        aps = bk2[1]
        ccar = sbuf(stack, nc, "sccc", (128, NK), F32)
        stg = hall[:, 0:H3]
        cvw = sbuf(stack, nc, "sccv", (128, 4), F32)
        st = alloc_ln_state(stack, nc, "sc")
        p_o = [psum(stack, nc, f"scp{i}", (128, 512)) for i in range(2)]

        init_ln_state(p, st)
        p.op("vector", lambda e: e.memset(ccar[:], 0.0))
        p.op("vector", lambda e: e.memset(cvw[:, 0:1], 1.0), sig="cv1")
        ev_w = load_w16(p, stg, l1w_t, H3, NK, w16)

        for c in range(NCH):
            ev_lx = p.dma("sync", xl.rearrange("p (k t) -> p k t", k=NK),
                          x23[:, :, c * CW:(c + 1) * CW],
                          after=[p.val("res")], sig="xld")
            ev_bc = emit_ln_stats(p, st, "sc",
                                  lambda k: xl[:, k * CW:(k + 1) * CW],
                                  NK, D, [ev_lx])
            ev_ap = None
            for k in range(NK):
                p.op("vector", lambda e, _k=k: e.tensor_sub(
                    aps[:], xl[:, _k * CW:(_k + 1) * CW], st["mbc"][:]),
                    after=[ev_bc, p.val("mml")] if k == 0 else [])
                ev_ap = p.op("vector", lambda e, _k=k: e.tensor_mul(
                    ln16[:, _k * CW:(_k + 1) * CW], ln16[:, _k * CW:(_k + 1) * CW] if False else aps[:],
                    st["rbc"][:]), sig="apl" if k == NK - 1 else None)
            def emit_gates(k):
                outs = []
                for role, mo in ((fk2[k % 2], k), (ik2[k % 2], NK + k),
                                 (hk2[k % 2], 2 * NK + k)):
                    ev_e = None
                    for h in range(2):
                        bank = p_o[(mo * 2 + h) % 2]
                        aft = [ev_ap, ev_w, p.val("cv1"), p.val("gev")]
                        for kk in range(NK):
                            ev_mm = p.op("tensor", lambda e, _b=bank, _k=kk, _m=mo, _h=h:
                                         e.matmul(_b[:],
                                                  w16[:, _k * H3 + _m * 128:_k * H3 + (_m + 1) * 128],
                                                  ln16[:, _k * CW + _h * 512:_k * CW + (_h + 1) * 512],
                                                  start=(_k == 0), stop=(_k == NK - 1)),
                                         after=aft if kk == 0 else [],
                                         sig="mml" if kk == NK - 1 else None)
                        ev_e = p.op("scalar", lambda e, _b=bank, _r=role, _h=h: e.activation(
                            _r[:, _h * 512:(_h + 1) * 512], _b[:], AF.Identity),
                            after=[ev_mm, p.val("sgv")], sig="gev")
                    outs.append(ev_e)
                return outs

            def chain_steps(k, outs):
                fk = fk2[k % 2]; ik = ik2[k % 2]; hk = hk2[k % 2]
                t1 = t1_2[k % 2]; t2 = t2_2[k % 2]
                ak = ak2[k % 2]; bk = bk2[k % 2]
                ev_rf = p.op("scalar", lambda e: e.activation(t1[:], fk[:], AF.Relu,
                                                              scale=-1.0),
                             after=[outs[0]], sig="c" + str(k % 2))
                yield
                ev_af = p.op("vector", lambda e: e.scalar_tensor_tensor(
                    t2[:], fk[:], -1.0, fk[:], ALU.mult, ALU.max),
                    after=[outs[0]], sig="cv" + str(k % 2))
                yield
                ev_ef = p.op("scalar", lambda e: e.activation(t2[:], t2[:], AF.Exp,
                                                              scale=-1.0),
                             after=[ev_af], sig="c" + str(k % 2))
                ev_lf = p.op("scalar", lambda e: e.activation(t2[:], t2[:], AF.Ln,
                                                              bias=cvw[:, 0:1]),
                             sig="c" + str(k % 2))
                yield
                ev_sf = p.op("vector", lambda e: e.tensor_add(t1[:], t1[:], t2[:]),
                             after=[ev_lf], sig="cv" + str(k % 2))
                yield
                ev_ri = p.op("scalar", lambda e: e.activation(bk[:], ik[:], AF.Relu,
                                                              scale=-1.0),
                             after=[outs[1]], sig="c" + str(k % 2))
                yield
                ev_ai = p.op("vector", lambda e: e.scalar_tensor_tensor(
                    t2[:], ik[:], -1.0, ik[:], ALU.mult, ALU.max),
                    after=[outs[1], ev_sf], sig="cv" + str(k % 2))
                yield
                ev_ei = p.op("scalar", lambda e: e.activation(t2[:], t2[:], AF.Exp,
                                                              scale=-1.0),
                             after=[ev_ai], sig="c" + str(k % 2))
                ev_li = p.op("scalar", lambda e: e.activation(t2[:], t2[:], AF.Ln,
                                                              bias=cvw[:, 0:1]),
                             sig="c" + str(k % 2))
                yield
                ev_si = p.op("vector", lambda e: e.tensor_add(bk[:], bk[:], t2[:]),
                             after=[ev_li, ev_ri], sig="cv" + str(k % 2))
                yield
                ev_df = p.op("vector", lambda e: e.tensor_sub(fk[:], t1[:], bk[:]),
                             sig="dif")
                yield
                ev_a = p.op("scalar", lambda e, _a=ak: e.activation(_a[:], fk[:],
                                                                    AF.Sigmoid,
                                                                    scale=-1.0),
                            after=[ev_df], sig="aev")
                ev_sg = p.op("scalar", lambda e, _b=bk: e.activation(_b[:], hk[:],
                                                                     AF.Sigmoid),
                             after=[ev_df, outs[2]], sig="sgv")
                yield
                ev_s = p.op("vector", lambda e, _a=ak: e.tensor_scalar(
                    t1[:], _a[:], -1.0, 1.0, ALU.mult, ALU.add),
                    after=[ev_a], sig="sev")
                ev_sel = p.op("vector", lambda e, _b=bk: e.scalar_tensor_tensor(
                    t2[:], hk[:], 0.5, _b[:], ALU.add, ALU.max),
                    after=[ev_sg], sig="sel")
                ev_b = p.op("vector", lambda e, _b=bk: e.tensor_mul(_b[:], t1[:], t2[:]),
                            after=[ev_sel, ev_s], sig="bev")
                yield
                p.op("vector", lambda e, _k=k, _a=ak, _b=bk: e.tensor_tensor_scan(
                    hall[:, _k * CW:(_k + 1) * CW], _a[:], _b[:],
                    ccar[:, _k:_k + 1], ALU.mult, ALU.add),
                    after=[ev_a, ev_b], sig="scn")
                yield

            outs_k = emit_gates(0)
            gens = {}
            for k in range(NK):
                gens[k] = chain_steps(k, outs_k)
                if k + 1 < NK:
                    outs_k = emit_gates(k + 1)
                gprev = gens.get(k - 1)
                gcur = gens[k]
                done_p = gprev is None
                done_c = False
                while not (done_p and done_c):
                    if not done_p:
                        done_p = next(gprev, "END") == "END"
                    if not done_c:
                        done_c = next(gcur, "END") == "END"
            for k in range(NK):
                p.op("vector", lambda e, _k=k: e.tensor_copy(
                    ccar[:, _k:_k + 1], hall[:, (_k + 1) * CW - 1:(_k + 1) * CW]),
                    sig="car" if k == NK - 1 else None)
            ev_bc2 = emit_ln_stats(p, st, "scl",
                                   lambda k: hall[:, k * CW:(k + 1) * CW],
                                   NK, D, [p.val("scn")])
            for k in range(NK):
                p.op("vector", lambda e, _k=k: e.tensor_sub(
                    aps[:], hall[:, _k * CW:(_k + 1) * CW], st["mbc"][:]),
                    after=[ev_bc2] if k == 0 else [])
                p.op("vector", lambda e: e.tensor_mul(aps[:], aps[:], st["rbc"][:]))
                ev_x3 = p.op("vector", lambda e, _k=k: e.tensor_add(
                    hall[:, _k * CW:(_k + 1) * CW], aps[:],
                    xl[:, _k * CW:(_k + 1) * CW]), sig="res")
                p.dma("gpsimd", x33[:, k, c * CW:(c + 1) * CW],
                      hall[:, k * CW:(k + 1) * CW], after=[ev_x3], sig="stx")
        p.op("gpsimd", None, after=[p.val("stx")])
    p.barrier()


SD1_IO = dict(reads=["x3", "wg_t", "wu_t"], writes=["hh16"])


def emit_sd1(p, m, T):
    nc = p.nc
    p.new_scope()
    NCH = T // CW
    x3d = m.d("x3", (D, T), F32)
    wg_t = m.d("wg_t", (D, H3), I32)
    wu_t = m.d("wu_t", (D, H3), I32)
    hh16d = m.d("hh16", (H3, T), BF16)
    x33 = x3d.rearrange("(k p) t -> p k t", p=128)
    hh3 = hh16d.rearrange("(k p) t -> p k t", p=128)

    with ExitStack() as stack:
        wg16 = sbuf(stack, nc, "d1wg", (128, NK * H3), BF16)
        wu16 = sbuf(stack, nc, "d1wu", (128, NK * H3), BF16)
        stg = sbuf(stack, nc, "d1st", (128, H3), I32)
        xl = sbuf(stack, nc, "d1x", (128, NK * CW), F32)
        ln16 = sbuf(stack, nc, "d1l", (128, NK * CW), BF16)
        sg = [sbuf(stack, nc, f"d1s{i}", (128, 512), F32) for i in range(2)]
        hhs = [sbuf(stack, nc, f"d1h{i}", (128, 512), BF16) for i in range(2)]
        aps = sbuf(stack, nc, "d1ap", (128, CW), F32)
        st = alloc_ln_state(stack, nc, "d1")
        p_g = psum(stack, nc, "d1pg", (128, 512))
        p_u = psum(stack, nc, "d1pu", (128, 512))

        init_ln_state(p, st)
        load_w16(p, stg, wg_t, H3, NK, wg16, scale=1.0 / 127.0)
        ev_w = load_w16(p, stg, wu_t, H3, NK, wu16, scale=1.0 / 127.0)

        hst_evs = []
        for c in range(NCH):
            ev_lx = p.dma("sync", xl.rearrange("p (k t) -> p k t", k=NK),
                          x33[:, :, c * CW:(c + 1) * CW],
                          after=[p.val("mml")], sig="xld")
            ev_bc = emit_ln_stats(p, st, "d1",
                                  lambda k: xl[:, k * CW:(k + 1) * CW],
                                  NK, D, [ev_lx])
            ev_ap = None
            for k in range(NK):
                p.op("vector", lambda e, _k=k: e.tensor_sub(
                    aps[:], xl[:, _k * CW:(_k + 1) * CW], st["mbc"][:]),
                    after=[ev_bc, p.val("mml")] if k == 0 else [])
                ev_ap = p.op("vector", lambda e, _k=k: e.tensor_mul(
                    ln16[:, _k * CW:(_k + 1) * CW], aps[:], st["rbc"][:]),
                    sig="apl" if k == NK - 1 else None)
            for mo in range(NKH):
                for h in range(2):
                    g = (c * NKH + mo) * 2 + h
                    aft = [ev_ap, ev_w]
                    if g >= 1:
                        aft.append(p.val("sev"))
                    ev_mg = None
                    for kk in range(NK):
                        ev_mg = p.op("tensor", lambda e, _k=kk, _m=mo, _h=h: e.matmul(
                            p_g[:],
                            wg16[:, _k * H3 + _m * 128:_k * H3 + (_m + 1) * 128],
                            ln16[:, _k * CW + _h * 512:_k * CW + (_h + 1) * 512],
                            start=(_k == 0), stop=(_k == NK - 1)),
                            after=aft if kk == 0 else [],
                            sig="mmg" if kk == NK - 1 else None)
                    aft2 = [p.val("hev")] if g >= 1 else []
                    for kk in range(NK):
                        ev_mu = p.op("tensor", lambda e, _k=kk, _m=mo, _h=h: e.matmul(
                            p_u[:],
                            wu16[:, _k * H3 + _m * 128:_k * H3 + (_m + 1) * 128],
                            ln16[:, _k * CW + _h * 512:_k * CW + (_h + 1) * 512],
                            start=(_k == 0), stop=(_k == NK - 1)),
                            after=aft2 if kk == 0 else [],
                            sig="mml" if kk == NK - 1 else None)
                    slot = sg[g % 2]
                    aft3 = [ev_mg] + ([p.val("hev")] if g >= 2 else [])
                    ev_sl = p.op("scalar", lambda e, _s=slot: e.activation(
                        _s[:], p_g[:], AF.Silu), after=aft3, sig="sev")
                    hslot = hhs[g % 2]
                    afth = [ev_sl, ev_mu]
                    if len(hst_evs) >= 2:
                        afth.append(hst_evs[-2])
                    ev_hh = p.op("vector", lambda e, _s=slot, _o=hslot: e.tensor_tensor(
                        _o[:], _s[:], p_u[:], ALU.mult), after=afth, sig="hev")
                    hev = p.dma("gpsimd", hh3[:, mo, c * CW + h * 512:c * CW + (h + 1) * 512],
                                hslot[:], after=[ev_hh], sig="hst")
                    hst_evs.append(hev)
        p.op("gpsimd", None, after=[p.val("hst")])
    p.barrier()


SD2_IO = dict(reads=["hh16", "wd_t", "x3"], writes=["y_out"])


def emit_sd2(p, m, T):
    nc = p.nc
    p.new_scope()
    NCH = T // CW
    hh16d = m.d("hh16", (H3, T), BF16)
    wd_t = m.d("wd_t", (H3, D), I32)
    x3d = m.d("x3", (D, T), F32)
    yod = m.d("y_out", (D, T), F32)
    hh3 = hh16d.rearrange("(k p) t -> p k t", p=128)
    x33 = x3d.rearrange("(k p) t -> p k t", p=128)
    yo3 = yod.rearrange("(k p) t -> p k t", p=128)

    with ExitStack() as stack:
        wd16 = sbuf(stack, nc, "d2wd", (128, NKH * D), BF16)
        stg = sbuf(stack, nc, "d2st", (128, D), I32)
        hl2 = [sbuf(stack, nc, f"d2h{i}", (128, (NKH // 3) * CW), BF16) for i in range(3)]
        xl = sbuf(stack, nc, "d2x", (128, NK * CW), F32)
        ot = sbuf(stack, nc, "d2o", (128, NK * CW), F32)
        aps = sbuf(stack, nc, "d2ap", (128, CW), F32)
        st = alloc_ln_state(stack, nc, "d2")
        p_o = [psum(stack, nc, f"d2p{i}", (128, 512)) for i in range(2)]

        init_ln_state(p, st)
        ev_w = load_w16(p, stg, wd_t, D, NKH, wd16, scale=1.0 / 127.0)

        QH = NKH // 3
        hld_evs = []
        for c in range(NCH):
            for q in range(3):
                qi = c * 3 + q
                aftq = [hld_evs[qi - 3][1]] if qi >= 3 else []
                ev_ld = p.dma(("sync", "gpsimd")[qi % 2],
                              hl2[qi % 3].rearrange("p (k t) -> p k t", k=QH),
                              hh3[:, q * QH:(q + 1) * QH, c * CW:(c + 1) * CW],
                              after=aftq, sig="hld" + str(qi % 2))
                hld_evs.append((ev_ld, None))
            ev_lx = p.dma("sync", xl.rearrange("p (k t) -> p k t", k=NK),
                          x33[:, :, c * CW:(c + 1) * CW],
                          after=[p.val("sty")], sig="xld")
            ev_ev = None
            for mo in range(NK):
                for h in range(2):
                    g = (c * NK + mo) * 2 + h
                    bank = p_o[g % 2]
                    aft0 = [ev_w]
                    if g >= 2:
                        aft0.append(p.val("oev"))
                    ev_mm = None
                    for k in range(NKH):
                        q = k // QH
                        qi = c * 3 + q
                        kq = k % QH
                        aft = (aft0 + [hld_evs[qi][0]]) if kq == 0 else []
                        ev_mm = p.op("tensor", lambda e, _b=bank, _q=qi % 3, _k=kq,
                                     _kk=k, _m=mo, _h=h:
                                     e.matmul(_b[:],
                                              wd16[:, _kk * D + _m * 128:_kk * D + (_m + 1) * 128],
                                              hl2[_q][:, _k * CW + _h * 512:_k * CW + (_h + 1) * 512],
                                              start=(_kk == 0), stop=(_kk == NKH - 1)),
                                     after=aft,
                                     sig="mm1" if k == NKH - 1 else None)
                    for q in range(3):
                        hld_evs[c * 3 + q] = (hld_evs[c * 3 + q][0], ev_mm)
                    ev_ev = p.op("vector", lambda e, _b=bank, _m=mo, _h=h: e.tensor_copy(
                        ot[:, _m * CW + _h * 512:_m * CW + (_h + 1) * 512], _b[:]),
                        after=[ev_mm], sig="oev")
            ev_bc = emit_ln_stats(p, st, "d2",
                                  lambda k: ot[:, k * CW:(k + 1) * CW],
                                  NK, D, [ev_ev])
            for k in range(NK):
                p.op("vector", lambda e, _k=k: e.tensor_sub(
                    aps[:], ot[:, _k * CW:(_k + 1) * CW], st["mbc"][:]),
                    after=[ev_bc, ev_lx] if k == 0 else [])
                p.op("vector", lambda e: e.tensor_mul(aps[:], aps[:], st["rbc"][:]))
                ev_y = p.op("vector", lambda e, _k=k: e.tensor_add(
                    ot[:, _k * CW:(_k + 1) * CW], aps[:],
                    xl[:, _k * CW:(_k + 1) * CW]), sig="res")
                p.dma("gpsimd", yo3[:, k, c * CW:(c + 1) * CW],
                      ot[:, k * CW:(k + 1) * CW], after=[ev_y], sig="sty")
        p.op("gpsimd", None, after=[p.val("sty")])
    p.barrier()


STAGES = {"pf1": (emit_pf1, PF1_IO),
          "scf": (lambda p, m, T: emit_scan(p, m, T, False), SCF_IO),
          "scr": (lambda p, m, T: emit_scan(p, m, T, True), SCR_IO),
          "po1": (emit_po1, PO1_IO),
          "po2": (emit_po2, PO2_IO),
          "sb": (emit_sb, SB_IO),
          "sc": (emit_sc, SC_IO),
          "sd1": (emit_sd1, SD1_IO),
          "sd2": (emit_sd2, SD2_IO)}


def build_stage(name, T):
    emit, io = STAGES[name]
    io = dict(io)
    if os.environ.get("ATH_SB_DEBUG") and name == "sb":
        io["writes"] = list(io["writes"]) + ["sb_ln", "sb_xr", "sb_r", "sb_ka", "sb_o"]
    if os.environ.get("ATH_SCAN_DEBUG") and name in ("scf", "scr"):
        pref = "r" if name == "scr" else "f"
        io["writes"] = list(io["writes"]) + [pref + "ddump", pref + "ydump",
                                             pref + "xc32", pref + "Brow", pref + "Crow"]
    nc = bass.Bass("TRN2", target_bir_lowering=False)
    m = Mem(nc, fused=False, reads=io["reads"], writes=io["writes"])
    p = Prog(nc)
    emit(p, m, T)
    p.emit()
    return nc


def run_stage(name, T, in_maps, core_ids=None):
    nc = build_stage(name, T)
    if core_ids is None:
        core_ids = list(range(len(in_maps)))
    res = run_bass_kernel_spmd(nc, in_maps, core_ids=core_ids)
    return res.results


CHAIN = [
    ("pf1", ["x_fm", "w_in"]),
    ("scf", ["xc_pre", "silu_z", "w_xp", "w_dt", "vecs", "alog_t", "eye16"]),
    ("scr", ["xc_pre", "silu_z", "yF2", "w_xp", "w_dt", "vecs", "alog_t", "eye16"]),
    ("po1", ["ysum16", "w_out", "w_ff1", "vecs"]),
    ("po2", ["g16", "w_ff2", "x_fm", "vecs"]),
    ("sb", ["x1", "wr_t", "wk_t", "wv_t", "vecs"]),
    ("sc", ["x2", "l1w_t"]),
    ("sd1", ["x3", "wg_t", "wu_t"]),
    ("sd2", ["hh16", "wd_t", "x3"]),
]


MODULE_INPUTS = ["x_fm", "w_in", "w_xp", "w_dt", "vecs", "alog_t", "eye16",
                 "w_out", "w_ff1", "w_ff2", "wr_t", "wk_t", "wv_t", "l1w_t",
                 "wg_t", "wu_t", "wd_t"]


def build_fused(T):
    nc = bass.Bass("TRN2", target_bir_lowering=False)
    m = Mem(nc, fused=True, reads=MODULE_INPUTS, writes=["y_out"])
    p = Prog(nc)
    for name, _ in CHAIN:
        STAGES[name][0](p, m, T)
    p.emit()
    return nc


def kernel(**inputs):
    x = np.asarray(inputs["L0_attn_x"])
    B, T, _ = x.shape
    xs, shared = host_prep(inputs, T)
    if os.environ.get("ATH_UNFUSED"):
        state = [dict(shared, x_fm=xs[b]) for b in range(B)]
        for name, reads in CHAIN:
            in_maps = [{r: state[b][r] for r in reads} for b in range(B)]
            res = run_stage(name, T, in_maps, core_ids=list(range(B)))
            for b in range(B):
                for k2, v in res[b].items():
                    state[b][k2] = np.asarray(v)
        return np.stack([state[b]["y_out"].T for b in range(B)]).astype(np.float32)
    nc = build_fused(T)
    in_maps = [dict(shared, x_fm=xs[b]) for b in range(B)]
    res = run_bass_kernel_spmd(nc, in_maps, core_ids=list(range(B)))
    return np.stack([np.asarray(r["y_out"]).T for r in res.results]).astype(np.float32)


# version 16
# speedup vs baseline: 1.1838x; 1.0071x over previous
import os
from contextlib import ExitStack

import numpy as np
import ml_dtypes

import concourse.bass as bass
import concourse.mybir as mybir
from concourse.bass_utils import run_bass_kernel_spmd

F32 = mybir.dt.float32
BF16 = mybir.dt.bfloat16
I32 = mybir.dt.int32
U8 = mybir.dt.uint8
AF = mybir.ActivationFunctionType
ALU = mybir.AluOpType

D = 1024
DI = 2048
DTR = 64
DS = 16
DFF = 4096
H3 = 3072
NK = D // 128
NKI = DI // 128
NKF = DFF // 128
NKH = H3 // 128
CW = 1024
TCS = 4096
EPS = 1e-5

ENGS = ("sync", "vector", "scalar", "tensor", "gpsimd")

VC_CONVW = 0
VC_CONVB = 64
VC_DTB = 80
VC_D = 96
VC_LNG = 112
VC_LNB = 120
VC_FB1 = 128
VC_FB2 = 160
VC_MUR = 168
VC_MUK = 176
VC_N = 192


class Prog:

    NPHYS = 40

    def __init__(self, nc):
        self.nc = nc
        self.q = {e: [] for e in ENGS}
        self.phys = [nc.alloc_semaphore(name=f"sp{i}") for i in range(self.NPHYS)]
        self.physcnt = [0] * self.NPHYS
        self.logmap = {}
        self.next_phys = 0
        self.prod = {}
        self.waited = {e: {} for e in ENGS}
        self.nwaits = 0

    def new_scope(self):
        self.logmap = {}
        self.next_phys = 0

    def _slot(self, name):
        if name not in self.logmap:
            self.logmap[name] = self.next_phys
            self.next_phys += 1
            assert self.next_phys <= self.NPHYS, "semaphore pool exhausted"
        return self.logmap[name]

    def op(self, eng, fn, after=(), sig=None, dn=1, dma=False, force_after=()):
        for ev in list(force_after) + [("_F", e) for e in ()]:
            if ev is None:
                continue
            si, val = ev
            if val <= 0:
                continue
            if self.waited[eng].get(si, 0) >= val:
                continue
            self.q[eng].append(("w", si, val))
            self.waited[eng][si] = val
            self.nwaits += 1
        for ev in after:
            if ev is None:
                continue
            si, val = ev
            if val <= 0:
                continue
            if self.prod.get(si) == eng:
                continue
            if self.waited[eng].get(si, 0) >= val:
                continue
            self.q[eng].append(("w", si, val))
            self.waited[eng][si] = val
            self.nwaits += 1
        ev_out = None
        if sig is not None:
            si = self._slot(sig)
            kind = "dma" if dma else eng
            if si not in self.prod:
                self.prod[si] = kind
            elif self.prod[si] != kind:
                self.prod[si] = "mixed"
            self.physcnt[si] += dn
            ev_out = (si, self.physcnt[si])
        if fn is not None or sig is not None:
            self.q[eng].append(("o", fn, (self.phys[ev_out[0]], dn) if sig else None))
        return ev_out

    def dma(self, eng, out, in_, after=(), sig="dma"):
        def fn(e, _o=out, _i=in_):
            return e.dma_start(out=_o, in_=_i)
        return self.op(eng, fn, after=after, sig=sig, dn=16, dma=True)

    def val(self, sig):
        si = self._slot(sig)
        return (si, self.physcnt[si])

    def barrier(self):
        evs = []
        for e in ENGS:
            if e in ("vector", "gpsimd"):
                evs.append(self.op(e, lambda en: en.engine_nop(), sig="bar"))
            else:
                evs.append(self.op(e, lambda en: en.nop(), sig="bar"))
        last = self.val("bar")
        for e in ENGS:
            self.op(e, None, after=[last])

    def emit(self):
        with self.nc.Block() as block:
            for e in ENGS:
                q = self.q[e]

                def runner(eng, _q=q):
                    for it in _q:
                        if it[0] == "w":
                            eng.wait_ge(self.phys[it[1]], it[2])
                        else:
                            fn, inc = it[1], it[2]
                            inst = fn(eng) if fn is not None else None
                            if inc is not None:
                                inst.then_inc(inc[0], inc[1])
                getattr(block, e)(runner)

    def n_ops(self):
        return sum(len(q) for q in self.q.values())


class Mem:
    def __init__(self, nc, fused, reads=(), writes=()):
        self.nc = nc
        self.fused = fused
        self.reads = set(reads)
        self.writes = set(writes)
        self.t = {}

    def d(self, name, shape, dtype):
        if name in self.t:
            return self.t[name]
        if name in self.reads:
            kind = "ExternalInput"
        elif name in self.writes:
            kind = "ExternalOutput"
        else:
            kind = "Internal"
        self.t[name] = self.nc.dram_tensor(name, list(shape), dtype, kind=kind).ap()
        return self.t[name]


def sbuf(stack, nc, name, shape, dtype):
    return stack.enter_context(nc.sbuf_tensor(name, list(shape), dtype)).ap()


def psum(stack, nc, name, shape):
    return stack.enter_context(nc.psum_tensor(name, list(shape), F32)).ap()


def alloc_ln_state(stack, nc, pref):
    return dict(
        sq=sbuf(stack, nc, pref + "lnsq", (128, CW), F32),
        p_s1=psum(stack, nc, pref + "ps1", (1, CW)),
        p_s2=psum(stack, nc, pref + "ps2", (1, CW)),
        p_bc=psum(stack, nc, pref + "pbc", (128, 512)),
        smu=sbuf(stack, nc, pref + "smu", (1, CW), F32),
        svar=sbuf(stack, nc, pref + "svar", (1, CW), F32),
        rbc=sbuf(stack, nc, pref + "rbc", (128, CW), F32),
        mbc=sbuf(stack, nc, pref + "mbc", (128, CW), F32),
        ones_c=sbuf(stack, nc, pref + "onesc", (128, 1), F32),
        ones_r=sbuf(stack, nc, pref + "onesr", (1, 128), F32),
        cst=sbuf(stack, nc, pref + "cst", (128, 4), F32),
    )


def init_ln_state(p, st):
    p.op("vector", lambda e: e.memset(st["ones_c"][:], 1.0))
    p.op("vector", lambda e: e.memset(st["cst"][:, 0:1], EPS))
    p.op("vector", lambda e: e.memset(st["cst"][:, 1:2], -0.5))
    p.op("vector", lambda e: e.memset(st["cst"][:, 2:3], 1.0))
    p.op("vector", lambda e: e.memset(st["cst"][:, 3:4], 0.5))
    return p.op("vector", lambda e: e.memset(st["ones_r"][:], 1.0), sig="ones")


def emit_ln_stats(p, st, tag, src_tiles, nkc, ndim, after_src):
    sq, p_s1, p_s2, p_bc = st["sq"], st["p_s1"], st["p_s2"], st["p_bc"]
    smu, svar = st["smu"], st["svar"]
    srstd = svar
    rbc, mbc = st["rbc"], st["mbc"]
    ones_c, ones_r = st["ones_c"], st["ones_r"]
    Tg = tag
    perk = after_src if callable(after_src) else None
    ev_statd_prev = p.val(Tg + "std")
    for k in range(nkc):
        src = src_tiles(k)
        ak = list(perk(k)) if perk else list(after_src)
        ev_sq = p.op("scalar",
                     lambda e, _s=src: e.activation(sq[:], _s, AF.Square),
                     after=ak + [p.val(Tg + "sqf")],
                     sig=Tg + "sq")
        for h in range(2):
            sl = slice(h * 512, (h + 1) * 512)
            if perk:
                a1 = (ak + [ev_statd_prev] if k == 0 else ak) if h == 0 else []
            else:
                a1 = (ak + [ev_statd_prev]) if (k == 0 and h == 0) else []
            p.op("tensor",
                 lambda e, _s=src, _sl=sl, _k=k: e.matmul(
                     p_s1[:, _sl], ones_c[:], _s[:, _sl],
                     start=(_k == 0), stop=(_k == nkc - 1)),
                 after=a1)
            p.op("tensor",
                 lambda e, _sl=sl, _k=k: e.matmul(
                     p_s2[:, _sl], ones_c[:], sq[:, _sl],
                     start=(_k == 0), stop=(_k == nkc - 1)),
                 after=[ev_sq],
                 sig=(Tg + "sqf") if h == 1 else None)
    ev_s12 = p.val(Tg + "sqf")
    inv = 1.0 / ndim
    ev_std = None
    for h in range(2):
        sl = slice(h * 512, (h + 1) * 512)
        p.op("vector", lambda e, _sl=sl: e.tensor_scalar_mul(smu[:, _sl], p_s1[:, _sl], inv),
             after=[ev_s12])
        p.op("vector", lambda e, _sl=sl: e.tensor_mul(svar[:, _sl], smu[:, _sl], smu[:, _sl]))
        ev_std = p.op("vector", lambda e, _sl=sl: e.scalar_tensor_tensor(
            svar[:, _sl], p_s2[:, _sl], inv, svar[:, _sl], ALU.mult, ALU.subtract),
            sig=(Tg + "std") if h == 1 else None)
    cst = st["cst"]
    p.op("scalar", lambda e: e.activation(srstd[:], svar[:], AF.Ln, bias=cst[0:1, 0:1]),
         after=[ev_std])
    ev_rs = p.op("scalar", lambda e: e.activation(srstd[:], srstd[:], AF.Exp,
                                                  scale=cst[0:1, 1:2]),
                 sig=Tg + "rs")
    ev_mu_ok = p.val(Tg + "std")
    ev_bcc = None
    for (vsrc, dst, evp) in ((srstd, rbc, ev_rs), (smu, mbc, ev_mu_ok)):
        for h in range(2):
            sl = slice(h * 512, (h + 1) * 512)
            ev_bm = p.op("tensor", lambda e, _v=vsrc, _sl=sl: e.matmul(
                p_bc[:], ones_r[:], _v[:, _sl], start=True, stop=True),
                after=[evp, p.val(Tg + "bcc")], sig=Tg + "bcm")
            ev_bcc = p.op("vector", lambda e, _d=dst, _sl=sl: e.tensor_copy(
                _d[:, _sl], p_bc[:]), after=[ev_bm], sig=Tg + "bcc")
    return ev_bcc


PF1_IO = dict(reads=["x_fm", "w_in"], writes=["xc_pre", "silu_z"])


def emit_pf1(p, m, T):
    nc = p.nc
    p.new_scope()
    NCH = T // CW
    x_fm = m.d("x_fm", (D, T), F32)
    w_in = m.d("w_in", (D, 2 * DI), F32)
    xc_pre = m.d("xc_pre", (DI, T), F32)
    silu_z = m.d("silu_z", (DI, T), BF16)
    x3 = x_fm.rearrange("(k p) t -> p k t", p=128)
    xcp3 = xc_pre.rearrange("(k p) t -> p k t", p=128)
    slz3 = silu_z.rearrange("(k p) t -> p k t", p=128)

    with ExitStack() as stack:
        wst = sbuf(stack, nc, "f1wst", (128, DI), F32)
        w16 = sbuf(stack, nc, "f1w16", (128, NK * 2 * DI), BF16)
        sx = [sbuf(stack, nc, f"f1sx{i}", (128, NK * CW), F32) for i in range(2)]
        ln16 = [sbuf(stack, nc, f"f1ln{i}", (128, NK * CW), BF16) for i in range(2)]
        ssz = [sbuf(stack, nc, f"f1sz{i}", (128, 512), BF16) for i in range(2)]
        sxc = [sbuf(stack, nc, f"f1xc{i}", (128, 512), F32) for i in range(2)]
        st = alloc_ln_state(stack, nc, "f1")
        p_xz = [psum(stack, nc, f"f1pxz{i}", (128, 512)) for i in range(2)]

        init_ln_state(p, st)
        for k in range(NK):
            for hh in range(2):
                ev = p.dma("sync", wst[:, :],
                           w_in[k * 128:(k + 1) * 128, hh * DI:(hh + 1) * DI],
                           after=[p.val("wcv")], sig="wld")
                p.op("vector", lambda e, _k=k, _h=hh: e.tensor_copy(
                    w16[:, _k * 2 * DI + _h * DI:_k * 2 * DI + (_h + 1) * DI], wst[:, :]),
                    after=[ev], sig="wcv")
        ev_wc_all = p.val("wcv")

        apl_evs = []
        mmz_evs = []
        for c in range(NCH):
            ev_ld = p.dma("sync", sx[c % 2].rearrange("p (k t) -> p k t", k=NK),
                          x3[:, :, c * CW:(c + 1) * CW],
                          after=[apl_evs[c - 2]] if c >= 2 else [],
                          sig="xld")
            xb = sx[c % 2]
            ev_bc = emit_ln_stats(p, st, "f1",
                                  lambda k, _x=xb: _x[:, k * CW:(k + 1) * CW],
                                  NK, D, [ev_ld])
            lnb = ln16[c % 2]
            aft0 = [ev_bc] + ([mmz_evs[c - 2]] if c >= 2 else [])
            ev_apl = None
            for k in range(NK):
                p.op("vector", lambda e, _k=k, _x=xb, _l=lnb: e.tensor_sub(
                    _l[:, _k * CW:(_k + 1) * CW], _x[:, _k * CW:(_k + 1) * CW],
                    st["mbc"][:]),
                    after=aft0 if k == 0 else [])
                ev_apl = p.op("vector", lambda e, _k=k, _l=lnb: e.tensor_mul(
                    _l[:, _k * CW:(_k + 1) * CW], _l[:, _k * CW:(_k + 1) * CW],
                    st["rbc"][:]),
                    sig="apl" if k == NK - 1 else None)
            apl_evs.append(ev_apl)
            zst_evs = []
            xst_evs = []
            ev_mm = None
            for mm in range(32):
                for h in range(2):
                    gidx = (c * 32 + mm) * 2 + h
                    bank = p_xz[gidx % 2]
                    aft = []
                    if mm == 0 and h == 0:
                        aft = [ev_apl, ev_wc_all]
                    if gidx >= 2:
                        pg = gidx - 2
                        if (pg // 2) % 32 < 16:
                            aft.append(p.val("evx"))
                        else:
                            aft.append(p.val("evz"))
                    for k in range(NK):
                        ev_mm = p.op(
                            "tensor",
                            lambda e, _b=bank, _k=k, _m=mm, _h=h, _l=lnb: e.matmul(
                                _b[:],
                                w16[:, _k * 2 * DI + _m * 128:_k * 2 * DI + (_m + 1) * 128],
                                _l[:, _k * CW + _h * 512:_k * CW + (_h + 1) * 512],
                                start=(_k == 0), stop=(_k == NK - 1)),
                            after=aft if k == 0 else [],
                            sig="mmz" if k == NK - 1 else None)
                    if mm < 16:
                        slot = sxc[gidx % 2]
                        aftx = [ev_mm]
                        if len(xst_evs) >= 2:
                            aftx.append(xst_evs[-2])
                        ev_cp = p.op("vector", lambda e, _s=slot, _b=bank:
                                     e.tensor_copy(_s[:], _b[:]),
                                     after=aftx, sig="evx")
                        xev = p.dma("gpsimd",
                                    xcp3[:, mm, c * CW + h * 512:c * CW + (h + 1) * 512],
                                    slot[:], after=[ev_cp], sig="stx")
                        xst_evs.append(xev)
                    else:
                        slot = ssz[gidx % 2]
                        aftz = [ev_mm]
                        if len(zst_evs) >= 2:
                            aftz.append(zst_evs[-2])
                        ev_sl = p.op("scalar", lambda e, _s=slot, _b=bank:
                                     e.activation(_s[:], _b[:], AF.Silu),
                                     after=aftz, sig="evz")
                        zev = p.dma("gpsimd",
                                    slz3[:, mm - 16, c * CW + h * 512:c * CW + (h + 1) * 512],
                                    slot[:], after=[ev_sl], sig="stz")
                        zst_evs.append(zev)
            mmz_evs.append(ev_mm)
        p.op("gpsimd", None, after=[p.val("stx"), p.val("stz")])
    p.barrier()


def host_prep(inputs, T):
    f32 = np.float32
    i = inputs
    out = {}
    xs = []
    for b in range(np.asarray(i["L0_attn_x"]).shape[0]):
        xs.append(np.ascontiguousarray(np.asarray(i["L0_attn_x"])[b, :T, :].T.astype(f32)))
    out["w_in"] = np.ascontiguousarray(np.asarray(i["L0_attn_in_proj_w"]).astype(f32))
    wxp = np.asarray(i["L0_attn_x_proj_w"]).astype(f32)
    wxp_p = np.zeros((DI, 128), f32)
    wxp_p[:, 0:16] = wxp[:, 64:80]
    wxp_p[:, 32:48] = wxp[:, 80:96]
    wxp_p[:, 64:128] = wxp[:, 0:64]
    out["w_xp"] = wxp_p
    out["w_dt"] = np.ascontiguousarray(np.asarray(i["L0_attn_dt_proj_w"]).astype(f32))
    out["w_out"] = np.ascontiguousarray(np.asarray(i["L0_attn_out_proj_w"]).astype(f32))
    out["w_ff1"] = np.ascontiguousarray(np.asarray(i["L0_attn_ff_w1"]).astype(f32))
    out["w_ff2"] = np.ascontiguousarray(np.asarray(i["L0_attn_ff_w2"]).astype(f32))
    out["wr_t"] = np.ascontiguousarray(np.asarray(i["L0_ffn_W_r"]).astype(f32).T)
    out["wk_t"] = np.ascontiguousarray(np.asarray(i["L0_ffn_W_k"]).astype(f32).T)
    out["wv_t"] = np.ascontiguousarray(np.asarray(i["L0_ffn_W_v"]).astype(f32).T)
    out["l1w_t"] = np.ascontiguousarray(np.asarray(i["L1_attn_W"]).astype(f32).T)
    out["wg_t"] = np.ascontiguousarray(np.asarray(i["L1_ffn_w_gate"]).astype(np.int32).T)
    out["wu_t"] = np.ascontiguousarray(np.asarray(i["L1_ffn_w_up"]).astype(np.int32).T)
    out["wd_t"] = np.ascontiguousarray(np.asarray(i["L1_ffn_w_down"]).astype(np.int32).T)

    def col(v):
        v = np.asarray(v, f32)
        n = v.shape[0] // 128
        return v.reshape(n, 128).T

    vecs = np.zeros((128, VC_N), f32)
    cw = np.asarray(i["L0_attn_conv_w"]).astype(f32)
    vecs[:, VC_CONVW:VC_CONVW + 64] = cw.reshape(16, 128, 4).transpose(1, 0, 2).reshape(128, 64)
    vecs[:, VC_CONVB:VC_CONVB + 16] = col(i["L0_attn_conv_b"])
    vecs[:, VC_DTB:VC_DTB + 16] = col(i["L0_attn_dt_proj_b"])
    vecs[:, VC_D:VC_D + 16] = col(i["L0_attn_D"])
    vecs[:, VC_LNG:VC_LNG + 8] = col(i["L0_attn_ln_g"])
    vecs[:, VC_LNB:VC_LNB + 8] = col(i["L0_attn_ln_b"])
    vecs[:, VC_FB1:VC_FB1 + 32] = col(i["L0_attn_ff_b1"])
    vecs[:, VC_FB2:VC_FB2 + 8] = col(i["L0_attn_ff_b2"])
    vecs[:, VC_MUR:VC_MUR + 8] = col(i["L0_ffn_mu_r"])
    vecs[:, VC_MUK:VC_MUK + 8] = col(i["L0_ffn_mu_k"])
    out["vecs"] = vecs
    out["eye16"] = np.eye(128, dtype=ml_dtypes.bfloat16)
    alog = np.asarray(i["L0_attn_A_log"]).astype(f32)
    out["alog_t"] = np.ascontiguousarray(
        alog.reshape(NKI, 128, DS).transpose(1, 0, 2).reshape(128, NKI * DS))
    return xs, out



SCF_IO = dict(reads=["xc_pre", "silu_z", "w_xp", "w_dt", "vecs", "alog_t", "eye16"],
              writes=["yF2"])
SCR_IO = dict(reads=["xc_pre", "silu_z", "yF2", "w_xp", "w_dt", "vecs", "alog_t",
                     "eye16"],
              writes=["ysum16"])


def emit_scan(p, m, T, rev):
    nc = p.nc
    p.new_scope()
    tcs = min(TCS, T)
    assert tcs >= DI
    NTC = T // tcs
    NS = tcs // 512
    pref = "r" if rev else "f"
    xc_pre = m.d("xc_pre", (DI, T), F32)
    silu_z = m.d("silu_z", (DI, T), BF16)
    w_xp = m.d("w_xp", (DI, 128), F32)
    w_dt = m.d("w_dt", (DTR, DI), F32)
    vecs = m.d("vecs", (128, VC_N), F32)
    alog_t = m.d("alog_t", (128, NKI * DS), F32)
    eye16d = m.d("eye16", (128, 128), BF16)
    yF2 = m.d("yF2", (DI, T), BF16)
    xcf32 = m.d(pref + "xc32", (DI, T), F32)
    xcf16 = m.d(pref + "xc16", (DI, T), BF16)
    Brow = m.d(pref + "Brow", (DS, T), BF16)
    Crow = m.d(pref + "Crow", (DS, T), BF16)
    if rev:
        ysum16 = m.d("ysum16", (DI, T), BF16)
        ys3 = ysum16.rearrange("(k p) t -> p k t", p=128)
    DBG = bool(os.environ.get("ATH_SCAN_DEBUG"))
    if DBG:
        ddump = m.d(pref + "ddump", (DI, T), F32)
        dd3 = ddump.rearrange("(k p) t -> p k t", p=128)
        ydump = m.d(pref + "ydump", (DI, T), F32)
        yd3 = ydump.rearrange("(k p) t -> p k t", p=128)

    xcp3 = xc_pre.rearrange("(k p) t -> p k t", p=128)
    slz3 = silu_z.rearrange("(k p) t -> p k t", p=128)
    yF23 = yF2.rearrange("(k p) t -> p k t", p=128)
    xc323 = xcf32.rearrange("(k p) t -> p k t", p=128)
    xc163 = xcf16.rearrange("(k p) t -> p k t", p=128)

    with ExitStack() as stack:
        S = [sbuf(stack, nc, f"{pref}S{i}", (128, tcs + 8), F32) for i in range(6)]
        xcpA, xcpB, acc, xcfA, xcfB, dblS = S
        delta, sv, xc_m, _unused1, _unused2, y_t = S
        gz = sbuf(stack, nc, pref + "gz6", (128, tcs), BF16)
        gy = sbuf(stack, nc, pref + "gy6", (128, tcs), BF16)
        B0 = sbuf(stack, nc, pref + "B0", (128, tcs), BF16)
        B1 = sbuf(stack, nc, pref + "B1", (128, tcs), BF16)
        du16 = sbuf(stack, nc, pref + "du", (128, tcs), BF16)
        a16_one = sbuf(stack, nc, pref + "a6", (128, tcs), BF16)
        a16_2 = [a16_one, a16_one]
        b16 = sbuf(stack, nc, pref + "b6", (128, tcs), BF16)
        h16 = sbuf(stack, nc, pref + "h6", (128, tcs), BF16)
        hc16_one = sbuf(stack, nc, pref + "hc6", (128, tcs), BF16)
        hc16_2 = [hc16_one, hc16_one]
        bcB = [sbuf(stack, nc, pref + f"bB{i}", (128, tcs), BF16) for i in range(2)]
        bcC = sbuf(stack, nc, pref + "bC", (128, tcs), BF16)
        wxp16 = sbuf(stack, nc, pref + "wxp", (128, NKI * 128), BF16)
        wdt16 = sbuf(stack, nc, pref + "wdt", (64, DI), BF16)
        aneg = sbuf(stack, nc, pref + "ang", (128, NKI * DS), F32)
        cvw = sbuf(stack, nc, pref + "cvw", (128, VC_N), F32)
        hcar = sbuf(stack, nc, pref + "hc", (128, NKI * DS), F32)
        cst = sbuf(stack, nc, pref + "cs", (128, 2), F32)
        eye = sbuf(stack, nc, pref + "eye", (128, 128), BF16)
        p_all = psum(stack, nc, pref + "pall", (128, NS * 512))

        p.op("vector", lambda e: e.memset(cst[:, 0:1], 1.0))
        p.op("vector", lambda e: e.memset(hcar[:], 0.0), sig="hc0")
        ev_v = p.dma("sync", cvw[:], vecs, sig="wld")
        ev_ey = p.dma("sync", eye[:], eye16d, sig="wld")
        ev_a0 = p.dma("sync", aneg[:], alog_t, sig="wld")
        ev_e = p.op("scalar", lambda e: e.activation(aneg[:], aneg[:], AF.Exp),
                    after=[ev_a0], sig="ae")
        ev_an = p.op("vector", lambda e: e.tensor_scalar_mul(aneg[:], aneg[:], -1.0),
                     after=[ev_e], sig="ang")
        ev_l = p.dma("sync", dblS[:, 0:NKI * 128].rearrange("p (k c) -> p k c", c=128),
                     w_xp.rearrange("(k p) c -> p k c", p=128), sig="wld")
        ev_wx = p.op("vector", lambda e: e.tensor_copy(
            wxp16[:], dblS[:, 0:NKI * 128]), after=[ev_l], sig="wxc")
        ev_l2 = p.dma("sync", dblS[0:DTR, 0:DI], w_dt, after=[ev_wx], sig="wld")
        ev_wd = p.op("vector", lambda e: e.tensor_copy(wdt16[:], dblS[0:DTR, 0:DI]),
                     after=[ev_l2], sig="wxc")

        dte_evs = []
        for tc in range(NTC):
            if rev:
                lo, hi = T - (tc + 1) * tcs, T - tc * tcs
            else:
                lo, hi = tc * tcs, (tc + 1) * tcs
            scol = tc * tcs
            cva_evs = []
            for mi in range(NKI):
                tile = (xcpA, xcpB)[mi % 2]
                llo, lhi = max(lo - 4, 0), min(hi + 4, T)
                if tc > 0 and mi < 2:
                    aft_ld = [p.val("aev"), p.val("spr"), p.val("dte")]
                elif mi >= 2:
                    aft_ld = [cva_evs[mi - 2]]
                else:
                    aft_ld = []
                ev_ld = p.dma("sync", tile[:, (llo - (lo - 4)):(llo - (lo - 4)) + (lhi - llo)],
                              xcp3[:, mi, llo:lhi], after=aft_ld, sig="cld")
                ev_ms = None
                if (not rev) and lo == 0:
                    ev_ms = p.op("vector", lambda e, _t=tile: e.memset(_t[:, 0:4], 0.0),
                                 sig="cms")
                if rev and hi == T:
                    ev_ms = p.op("vector", lambda e, _t=tile: e.memset(
                        _t[:, tcs + 4:tcs + 8], 0.0), sig="cms")
                aft_t0 = [ev_ld, ev_ms, ev_v]
                if tc > 0 and mi == 0:
                    aft_t0.append(p.val("gat"))
                for i in range(4):
                    if rev:
                        vw = tile[:, tcs + 6 - i: tcs + 6 - i - tcs: -1]
                    else:
                        vw = tile[:, 1 + i: 1 + i + tcs]
                    wcol = cvw[:, VC_CONVW + mi * 4 + i:VC_CONVW + mi * 4 + i + 1]
                    if i == 0:
                        p.op("vector", lambda e, _v=vw, _w=wcol: e.tensor_scalar_mul(
                            acc[:, 0:tcs], _v, _w), after=aft_t0)
                    else:
                        p.op("vector", lambda e, _v=vw, _w=wcol: e.scalar_tensor_tensor(
                            acc[:, 0:tcs], _v, _w, acc[:, 0:tcs], ALU.mult, ALU.add),
                            sig="cva" if i == 3 else None)
                ev_cv = p.val("cva")
                cva_evs.append(ev_cv)
                xcf = (xcfA, xcfB)[mi % 2]
                aft_si = [ev_cv]
                if mi >= 2:
                    aft_si.append(p.val("sxs"))
                ev_si = p.op("scalar", lambda e, _x=xcf, _b=cvw[:, VC_CONVB + mi:VC_CONVB + mi + 1]:
                             e.activation(_x[:, 0:tcs], acc[:, 0:tcs], AF.Silu, bias=_b),
                             after=aft_si, sig="csi")
                b6 = (B0, B1)[mi % 2]
                aft_cv16 = [ev_si]
                if tc > 0 and mi < 2:
                    aft_cv16.append(p.val("dtm"))
                    aft_cv16.append(p.val("yst"))
                if mi >= 2:
                    aft_cv16.append(p.val("sxs"))
                ev_c16 = p.op("vector", lambda e, _b=b6, _x=xcf: e.tensor_copy(
                    _b[:], _x[:, 0:tcs]), after=aft_cv16, sig="c16")
                p.dma("sync", xc323[:, mi, scol:scol + tcs], xcf[:, 0:tcs],
                      after=[ev_si], sig="sxs")
                p.dma("sync", xc163[:, mi, scol:scol + tcs], b6[:],
                      after=[ev_c16], sig="sxs")
            pjk_evs = []
            for k in range(NKI):
                x16 = (B0, B1)[k % 2]
                if k < 2:
                    aft = [p.val("sxs"), p.val("dtm"), p.val("yst"), p.val("bcs")]
                else:
                    aft = [pjk_evs[k - 2]]
                ev_x = p.dma("sync", x16[:], xc163[:, k, scol:scol + tcs],
                             after=aft, sig="xl6")
                for s in range(NS):
                    aftm = [ev_x, ev_wx]
                    if k == 0:
                        aftm += [p.val("yst"), p.val("dte"), p.val("gat")]
                    ev_pk = p.op("tensor", lambda e, _k=k, _s=s, _x=x16: e.matmul(
                        p_all[:, _s * 512:(_s + 1) * 512],
                        wxp16[:, _k * 128:(_k + 1) * 128],
                        _x[:, _s * 512:(_s + 1) * 512],
                        start=(_k == 0), stop=(_k == NKI - 1)),
                        after=aftm,
                        sig=("pj" if k == NKI - 1 else ("pjk" if s == NS - 1 else None)))
                    if s == NS - 1:
                        pjk_evs.append(ev_pk)
            ev_pj = p.val("pj")
            for s in range(NS):
                p.op("scalar", lambda e, _s=s: e.activation(
                    dblS[:, _s * 512:(_s + 1) * 512], p_all[:, _s * 512:(_s + 1) * 512],
                    AF.Identity), after=[ev_pj], sig="dbe")
            ev_dbe = p.val("dbe")
            ev_b48 = p.op("vector", lambda e: e.tensor_copy(
                B1[0:48, 0:tcs], dblS[0:48, 0:tcs]), after=[ev_dbe], sig="b48")
            p.dma("sync", Brow[:, scol:scol + tcs], B1[0:DS, 0:tcs],
                  after=[ev_b48], sig="bcs")
            p.dma("sync", Crow[:, scol:scol + tcs], B1[32:32 + DS, 0:tcs],
                  after=[ev_b48], sig="bcs")
            ev_bcs = p.val("bcs")
            ev_dt16 = p.op("vector", lambda e: e.tensor_copy(
                B0[0:DTR, 0:tcs], dblS[64:128, 0:tcs]),
                after=[ev_dbe, ev_pj], sig="dt6")
            for mi in range(NKI):
                for s2p in range(NS // 2):
                    gb = len(dte_evs)
                    aftm = [ev_dt16, ev_wd, p.val("gat")]
                    if gb >= 1:
                        aftm.append(dte_evs[-1])
                    else:
                        aftm.append(ev_dbe)
                    ev_dm = None
                    for s in (2 * s2p, 2 * s2p + 1):
                        ev_dm = p.op("tensor", lambda e, _mi=mi, _s=s: e.matmul(
                            p_all[:, (_s % 2) * 512:(_s % 2 + 1) * 512],
                            wdt16[:, _mi * 128:(_mi + 1) * 128],
                            B0[0:DTR, _s * 512:(_s + 1) * 512],
                            start=True, stop=True),
                            after=aftm if s % 2 == 0 else [], sig="dtm")
                    afte = [ev_dm, p.val("spr")]
                    if mi == 0 and s2p == 0:
                        afte.append(p.val("cva"))
                    ev_de = p.op("scalar", lambda e, _s=s2p,
                                 _b=cvw[:, VC_DTB + mi:VC_DTB + mi + 1]:
                                 e.activation(sv[:, _s * 1024:(_s + 1) * 1024],
                                              p_all[:, 0:1024],
                                              AF.Identity, bias=_b),
                                 after=afte, sig="dte")
                    dte_evs.append(ev_de)
                ev_v_ready = dte_evs[-1]
                aft_sp = [ev_v_ready]
                if mi == 0:
                    aft_sp.append(p.val("cva"))
                if not (tc == 0 and mi == 0):
                    aft_sp.append(p.val("aev"))
                ev_r = p.op("scalar", lambda e: e.activation(
                    delta[:, 0:tcs], sv[:, 0:tcs], AF.Relu), after=aft_sp, sig="spr")
                aft_ab = [ev_r, p.val("yst")]
                ev_abs = p.op("vector", lambda e: e.scalar_tensor_tensor(
                    y_t[:, 0:tcs], delta[:, 0:tcs], 2.0, sv[:, 0:tcs],
                    ALU.mult, ALU.subtract), after=aft_ab, sig="spa")
                p.op("scalar", lambda e: e.activation(
                    y_t[:, 0:tcs], y_t[:, 0:tcs], AF.Exp, scale=-1.0),
                    after=[ev_abs], sig="spe")
                ev_ln = p.op("scalar", lambda e: e.activation(
                    y_t[:, 0:tcs], y_t[:, 0:tcs], AF.Ln, bias=cst[:, 0:1]), sig="spe")
                ev_dl = p.op("vector", lambda e: e.tensor_add(
                    delta[:, 0:tcs], delta[:, 0:tcs], y_t[:, 0:tcs]),
                    after=[ev_ln], sig="dlt")
                if DBG:
                    p.dma("sync", dd3[:, mi, scol:scol + tcs], delta[:, 0:tcs],
                          after=[ev_dl], sig="dbgs")
                aft_xl = [p.val("csi"), p.val("gat")]
                ev_xm = p.dma("sync", xc_m[:, 0:tcs], xc323[:, mi, scol:scol + tcs],
                              after=aft_xl, sig="xml")
                ev_du = p.op("vector", lambda e: e.tensor_mul(
                    du16[:], delta[:, 0:tcs], xc_m[:, 0:tcs]),
                    after=[ev_xm] + ([p.val("yst")] if not rev else []), sig="duu")
                ev_gz = p.dma("sync", gz[:], slz3[:, mi, lo:hi],
                              after=[p.val("gml")], sig="gzl")
                if rev:
                    ev_gy = p.dma("sync", gy[:], yF23[:, mi, lo:hi],
                                  after=[p.val("ost")], sig="gyl")
                yad_evs = []
                bld_evs = []
                scn_evs = []
                for n in range(DS):
                    bb = bcB[n % 2]
                    if n < 2:
                        aftB = [p.val("bld")]
                    else:
                        aftB = [bld_evs[-2]]
                    ev_bb = p.dma("sync", bb[:],
                                  Brow[n:n + 1, scol:scol + tcs]
                                  .partition_broadcast(128).squeeze(1),
                                  after=aftB + [ev_bcs], sig="bbl")
                    aftC = [p.val("hcm")]
                    ev_cc = p.dma("scalar", bcC[:],
                                  Crow[n:n + 1, scol:scol + tcs]
                                  .partition_broadcast(128).squeeze(1),
                                  after=aftC + [ev_bcs], sig="ccl")
                    a16 = a16_2[n % 2]
                    aft_a = [ev_dl, ev_an]
                    if scn_evs:
                        aft_a.append(scn_evs[-1])
                    elif not (tc == 0 and mi == 0):
                        aft_a.append(p.val("scn"))
                    ev_a = p.op("scalar", lambda e, _mi=mi, _n=n, _a=a16: e.activation(
                        _a[:], delta[:, 0:tcs], AF.Exp,
                        scale=aneg[:, _mi * DS + _n:_mi * DS + _n + 1]),
                        after=aft_a, sig="aev")
                    ev_b = p.op("vector", lambda e, _b=bb: e.tensor_tensor(
                        b16[:], du16[:], _b[:], ALU.mult),
                        after=[ev_bb, ev_du], sig="bld")
                    bld_evs.append(ev_b)
                    col = mi * DS + n
                    ev_h = p.op("vector", lambda e, _c=col, _a=a16: e.tensor_tensor_scan(
                        h16[:], _a[:], b16[:],
                        hcar[:, _c:_c + 1], ALU.mult, ALU.add),
                        after=[ev_a], sig="scn")
                    scn_evs.append(ev_h)
                    hc16 = hc16_2[n % 2]
                    aft_hc = [ev_cc]
                    if yad_evs:
                        aft_hc.append(yad_evs[-1])
                    elif not (tc == 0 and mi == 0):
                        aft_hc.append(p.val("yad"))
                    ev_hc = p.op("vector", lambda e, _o=hc16: e.tensor_tensor(
                        _o[:], h16[:], bcC[:], ALU.mult),
                        after=aft_hc, sig="hcm")
                    aft_ya = [ev_hc, ev_ey]
                    if n == 0:
                        aft_ya += [ev_v_ready, p.val("gat")]
                    ev_ya = None
                    for s2 in range(NS):
                        ev_ya = p.op("tensor", lambda e, _s=s2, _i=hc16,
                                     _st=(n == 0), _sp=(n == DS - 1): e.matmul(
                            p_all[:, _s * 512:(_s + 1) * 512], eye[:],
                            _i[:, _s * 512:(_s + 1) * 512],
                            start=_st, stop=_sp),
                            after=aft_ya if s2 == 0 else [],
                            sig="yad" if s2 == NS - 1 else None)
                    yad_evs.append(ev_ya)
                    p.op("vector", lambda e, _c=col: e.tensor_copy(
                        hcar[:, _c:_c + 1], h16[:, tcs - 1:tcs]))

                aft_g = [yad_evs[-1], p.val("ost" if rev else "yst")]
                ev_g = None
                for s2 in range(NS):
                    ev_g = p.op("vector", lambda e, _mi=mi, _s=s2: e.scalar_tensor_tensor(
                        y_t[:, _s * 512:(_s + 1) * 512],
                        xc_m[:, _s * 512:(_s + 1) * 512],
                        cvw[:, VC_D + _mi:VC_D + _mi + 1],
                        p_all[:, _s * 512:(_s + 1) * 512],
                        ALU.mult, ALU.add),
                        after=aft_g if s2 == 0 else [], sig="gat")
                if DBG:
                    p.dma("sync", yd3[:, mi, scol:scol + tcs], y_t[:, 0:tcs],
                          after=[ev_g], sig="dbgs")
                if rev:
                    ev_gm = p.op("vector", lambda e: e.tensor_tensor(
                        y_t[:, 0:tcs], y_t[:, 0:tcs],
                        gz[:, tcs - 1::-1], ALU.mult),
                        after=[ev_gz], force_after=[ev_g], sig="gml")
                    ev_o = p.op("vector", lambda e: e.tensor_add(
                        B1[:], y_t[:, tcs - 1::-1], gy[:]),
                        after=[ev_gy, p.val("yst"), p.val("bcs")],
                        force_after=[ev_gm], sig="ost")
                    p.dma("sync", ys3[:, mi, lo:hi], B1[:], after=[ev_o], sig="yst")
                else:
                    ev_o = p.op("vector", lambda e: e.tensor_tensor(
                        du16[:], y_t[:, 0:tcs], gz[:], ALU.mult),
                        after=[ev_gz, p.val("yst")], sig="gml")
                    p.dma("sync", yF23[:, mi, lo:hi], du16[:],
                          after=[ev_o], sig="yst")
        p.op("sync", None, after=[p.val("yst"), p.val("sxs"), p.val("bcs")])
    p.barrier()


def load_w16(p, stg, dram, Mcols, kchunks, dest, scale=None, tag=None):
    evs = []
    for k in range(kchunks):
        ev = p.dma("sync", stg[:, 0:Mcols], dram[k * 128:(k + 1) * 128, :],
                   after=[p.val("lwc")], sig="lwl")
        if scale is None:
            ev2 = p.op("vector", lambda e, _k=k: e.tensor_copy(
                dest[:, _k * Mcols:(_k + 1) * Mcols], stg[:, 0:Mcols]),
                after=[ev], sig="lwc")
        else:
            ev2 = p.op("vector", lambda e, _k=k: e.tensor_scalar_mul(
                dest[:, _k * Mcols:(_k + 1) * Mcols], stg[:, 0:Mcols], scale),
                after=[ev], sig="lwc")
        evs.append(ev2)
    return evs[-1]


PO1_IO = dict(reads=["ysum16", "w_out", "w_ff1", "vecs"], writes=["g16"])


def emit_po1(p, m, T):
    nc = p.nc
    p.new_scope()
    NCH = T // CW
    ysum16 = m.d("ysum16", (DI, T), BF16)
    w_out = m.d("w_out", (DI, D), F32)
    w_ff1 = m.d("w_ff1", (D, DFF), F32)
    vecs = m.d("vecs", (128, VC_N), F32)
    g16d = m.d("g16", (DFF, T), BF16)
    ys3 = ysum16.rearrange("(k p) t -> p k t", p=128)
    g3 = g16d.rearrange("(k p) t -> p k t", p=128)

    with ExitStack() as stack:
        wo16 = sbuf(stack, nc, "o1wo", (128, NKI * D), BF16)
        wf16 = sbuf(stack, nc, "o1wf", (128, NK * DFF), BF16)
        ysl2 = [sbuf(stack, nc, f"o1ys{i}", (128, (NKI // 2) * CW), BF16) for i in range(2)]
        mot = sbuf(stack, nc, "o1mo", (128, NK * CW), F32)
        aps = sbuf(stack, nc, "o1ap", (128, CW), F32)
        h16 = sbuf(stack, nc, "o1h", (128, NK * CW), BF16)
        gsl = [sbuf(stack, nc, f"o1g{i}", (128, 512), BF16) for i in range(2)]
        cvw = sbuf(stack, nc, "o1cv", (128, VC_N), F32)
        st = alloc_ln_state(stack, nc, "o1")
        p_o = [psum(stack, nc, f"o1p{i}", (128, 512)) for i in range(2)]

        init_ln_state(p, st)
        p.dma("sync", cvw[:], vecs, sig="cvl")
        ev_cv = p.val("cvl")
        ev_w1 = load_w16(p, mot, w_out, D, NKI, wo16)
        ev_w2 = load_w16(p, mot, w_ff1, DFF, NK, wf16)

        HY = NKI // 2
        yld_evs = []
        for c in range(NCH):
            for q in range(2):
                qi = c * 2 + q
                aftq = [yld_evs[qi - 2][1]] if qi >= 2 else []
                ev_ld = p.dma("sync", ysl2[qi % 2].rearrange("p (k t) -> p k t", k=HY),
                              ys3[:, q * HY:(q + 1) * HY, c * CW:(c + 1) * CW],
                              after=aftq, sig="yld")
                yld_evs.append((ev_ld, None))
            ev_ev = None
            reg_ev = [None] * NK
            for mo in range(NK):
                for h in range(2):
                    g = (c * NK + mo) * 2 + h
                    bank = p_o[g % 2]
                    aft0 = [ev_w1, ev_w2, ev_cv]
                    if g >= 2:
                        aft0.append(p.val("oev"))
                    ev_mm = None
                    for k in range(NKI):
                        q = k // HY
                        qi = c * 2 + q
                        kq = k % HY
                        aft = (aft0 + [yld_evs[qi][0]]) if kq == 0 else []
                        ev_mm = p.op("tensor", lambda e, _b=bank, _q=qi % 2, _k=kq,
                                     _kk=k, _m=mo, _h=h:
                                     e.matmul(_b[:],
                                              wo16[:, _kk * D + _m * 128:_kk * D + (_m + 1) * 128],
                                              ysl2[_q][:, _k * CW + _h * 512:_k * CW + (_h + 1) * 512],
                                              start=(_kk == 0), stop=(_kk == NKI - 1)),
                                     after=aft,
                                     sig="mm1" if k == NKI - 1 else None)
                    for q in range(2):
                        yld_evs[c * 2 + q] = (yld_evs[c * 2 + q][0], ev_mm)
                    ev_ev = p.op("vector", lambda e, _b=bank, _m=mo, _h=h: e.tensor_copy(
                        mot[:, _m * CW + _h * 512:_m * CW + (_h + 1) * 512], _b[:]),
                        after=[ev_mm], sig="oev")
                reg_ev[mo] = ev_ev
            ev_bc = emit_ln_stats(p, st, "o1",
                                  lambda k: mot[:, k * CW:(k + 1) * CW],
                                  NK, D, lambda k, _r=reg_ev: [_r[k]])
            ev_ap = None
            for k in range(NK):
                p.op("vector", lambda e, _k=k: e.tensor_sub(
                    aps[:], mot[:, _k * CW:(_k + 1) * CW], st["mbc"][:]),
                    after=[ev_bc] if k == 0 else [])
                p.op("vector", lambda e: e.tensor_mul(aps[:], aps[:], st["rbc"][:]))
                ev_ap = p.op("vector", lambda e, _k=k: e.tensor_scalar(
                    h16[:, _k * CW:(_k + 1) * CW], aps[:],
                    cvw[:, VC_LNG + _k:VC_LNG + _k + 1],
                    cvw[:, VC_LNB + _k:VC_LNB + _k + 1],
                    ALU.mult, ALU.add), sig="apl" if k == NK - 1 else None)
            gst_evs = []
            for mf in range(NKF):
                for h in range(2):
                    g = (c * NKF + mf) * 2 + h
                    bank = p_o[g % 2]
                    aft = [ev_ap]
                    if g >= 2:
                        aft.append(p.val("gev"))
                    for k in range(NK):
                        ev_mm = p.op("tensor", lambda e, _b=bank, _k=k, _m=mf, _h=h:
                                     e.matmul(_b[:],
                                              wf16[:, _k * DFF + _m * 128:_k * DFF + (_m + 1) * 128],
                                              h16[:, _k * CW + _h * 512:_k * CW + (_h + 1) * 512],
                                              start=(_k == 0), stop=(_k == NK - 1)),
                                     after=aft if k == 0 else [],
                                     sig="mm2" if k == NK - 1 else None)
                    slot = gsl[g % 2]
                    aftg = [ev_mm]
                    if len(gst_evs) >= 2:
                        aftg.append(gst_evs[-2])
                    ev_g = p.op("scalar", lambda e, _s=slot, _b=bank, _m=mf: e.activation(
                        _s[:], _b[:], AF.Gelu,
                        bias=cvw[:, VC_FB1 + _m:VC_FB1 + _m + 1]),
                        after=aftg, sig="gev")
                    gev = p.dma("gpsimd", g3[:, mf, c * CW + h * 512:c * CW + (h + 1) * 512],
                                slot[:], after=[ev_g], sig="gst")
                    gst_evs.append(gev)
        p.op("gpsimd", None, after=[p.val("gst")])
    p.barrier()


PO2_IO = dict(reads=["g16", "w_ff2", "x_fm", "vecs"], writes=["x1"])


def emit_po2(p, m, T):
    nc = p.nc
    p.new_scope()
    NCH = T // CW
    g16d = m.d("g16", (DFF, T), BF16)
    w_ff2 = m.d("w_ff2", (DFF, D), F32)
    x_fm = m.d("x_fm", (D, T), F32)
    vecs = m.d("vecs", (128, VC_N), F32)
    x1d = m.d("x1", (D, T), F32)
    g3 = g16d.rearrange("(k p) t -> p k t", p=128)
    x3 = x_fm.rearrange("(k p) t -> p k t", p=128)
    x13 = x1d.rearrange("(k p) t -> p k t", p=128)

    with ExitStack() as stack:
        wf16 = sbuf(stack, nc, "o2wf", (128, NKF * D), BF16)
        gl2 = [sbuf(stack, nc, f"o2g{i}", (128, (NKF // 4) * CW), BF16) for i in range(3)]
        xl = sbuf(stack, nc, "o2x", (128, NK * CW), F32)
        ot = sbuf(stack, nc, "o2o", (128, NK * CW), F32)
        aps = sbuf(stack, nc, "o2ap", (128, CW), F32)
        cvw = sbuf(stack, nc, "o2cv", (128, VC_N), F32)
        st = alloc_ln_state(stack, nc, "o2")
        p_o = [psum(stack, nc, f"o2p{i}", (128, 512)) for i in range(2)]

        init_ln_state(p, st)
        p.dma("sync", cvw[:], vecs, sig="cvl")
        ev_cv = p.val("cvl")
        ev_w1 = load_w16(p, ot, w_ff2, D, NKF, wf16)

        QK = NKF // 4
        gld_evs = []
        for c in range(NCH):
            ev_lx = p.dma("sync", xl.rearrange("p (k t) -> p k t", k=NK),
                          x3[:, :, c * CW:(c + 1) * CW],
                          after=[p.val("stx")], sig="xld")
            ev_ev = None
            reg_ev = [None] * NK
            for half in range(2):
                for q in range(2):
                    qi = c * 4 + half * 2 + q
                    aftq = [gld_evs[qi - 3][1]] if qi >= 3 else []
                    ev_ld = p.dma("sync", gl2[qi % 3].rearrange("p (k t) -> p k t", k=QK),
                                  g3[:, (half * 2 + q) * QK:(half * 2 + q + 1) * QK,
                                     c * CW:(c + 1) * CW],
                                  after=aftq, sig="gld")
                    gld_evs.append((ev_ld, None))
                for mo in range(NK):
                    for h in range(2):
                        g = ((c * 2 + half) * NK + mo) * 2 + h
                        bank = p_o[g % 2]
                        aft0 = [ev_w1, ev_cv]
                        if g >= 2:
                            aft0.append(p.val("oev"))
                        ev_mm = None
                        for k in range(2 * QK):
                            q = k // QK
                            qi = c * 4 + half * 2 + q
                            kk = half * 2 * QK + k
                            kq = k % QK
                            aft = (aft0 + [gld_evs[qi][0]]) if kq == 0 else []
                            ev_mm = p.op("tensor", lambda e, _b=bank, _q=qi % 3, _k=kq,
                                         _kk=kk, _m=mo, _h=h:
                                         e.matmul(_b[:],
                                                  wf16[:, _kk * D + _m * 128:_kk * D + (_m + 1) * 128],
                                                  gl2[_q][:, _k * CW + _h * 512:_k * CW + (_h + 1) * 512],
                                                  start=(_kk % (2 * QK) == 0),
                                                  stop=(_kk % (2 * QK) == 2 * QK - 1)),
                                         after=aft,
                                         sig="mm1" if k == 2 * QK - 1 else None)
                        for q in range(2):
                            qi = c * 4 + half * 2 + q
                            gld_evs[qi] = (gld_evs[qi][0], ev_mm)
                        if half == 0:
                            ev_ev = p.op("vector", lambda e, _b=bank, _m=mo, _h=h:
                                         e.tensor_copy(
                                             ot[:, _m * CW + _h * 512:_m * CW + (_h + 1) * 512],
                                             _b[:]),
                                         after=[ev_mm], sig="oev")
                        else:
                            sl = slice(mo * CW + h * 512, mo * CW + (h + 1) * 512)
                            ev_ev = p.op("vector", lambda e, _b=bank, _s=sl: e.scalar_tensor_tensor(
                                ot[:, _s], _b[:],
                                cvw[:, VC_FB2 + (_s.start // CW):VC_FB2 + (_s.start // CW) + 1],
                                ot[:, _s], ALU.add, ALU.add),
                                after=[ev_mm], sig="oev")
                    if half == 1:
                        reg_ev[mo] = ev_ev
            ev_bc = emit_ln_stats(p, st, "o2",
                                  lambda k: ot[:, k * CW:(k + 1) * CW],
                                  NK, D, lambda k, _r=reg_ev: [_r[k]])
            for k in range(NK):
                p.op("vector", lambda e, _k=k: e.tensor_sub(
                    aps[:], ot[:, _k * CW:(_k + 1) * CW], st["mbc"][:]),
                    after=[ev_bc, ev_lx] if k == 0 else [])
                p.op("vector", lambda e: e.tensor_mul(aps[:], aps[:], st["rbc"][:]))
                ev_x1 = p.op("vector", lambda e, _k=k: e.tensor_add(
                    ot[:, _k * CW:(_k + 1) * CW], aps[:], xl[:, _k * CW:(_k + 1) * CW]),
                    sig="res")
                p.dma("gpsimd", x13[:, k, c * CW:(c + 1) * CW],
                      ot[:, k * CW:(k + 1) * CW], after=[ev_x1], sig="stx")
        p.op("gpsimd", None, after=[p.val("stx")])
    p.barrier()


SB_IO = dict(reads=["x1", "wr_t", "wk_t", "wv_t", "vecs"], writes=["x2"])


def emit_sb(p, m, T):
    nc = p.nc
    p.new_scope()
    NCH = T // CW
    x1d = m.d("x1", (D, T), F32)
    wr_t = m.d("wr_t", (D, D), F32)
    wk_t = m.d("wk_t", (D, D), F32)
    wv_t = m.d("wv_t", (D, D), F32)
    vecs = m.d("vecs", (128, VC_N), F32)
    x2d = m.d("x2", (D, T), F32)
    x13 = x1d.rearrange("(k p) t -> p k t", p=128)
    x23 = x2d.rearrange("(k p) t -> p k t", p=128)
    DBG = bool(os.environ.get("ATH_SB_DEBUG"))
    if DBG:
        dln = m.d("sb_ln", (D, T), BF16)
        dxr = m.d("sb_xr", (D, T), BF16)
        dr = m.d("sb_r", (D, T), F32)
        dka = m.d("sb_ka", (D, T), BF16)
        do = m.d("sb_o", (D, T), F32)
        dln3 = dln.rearrange("(k p) t -> p k t", p=128)
        dxr3 = dxr.rearrange("(k p) t -> p k t", p=128)
        dr3 = dr.rearrange("(k p) t -> p k t", p=128)
        dka3 = dka.rearrange("(k p) t -> p k t", p=128)
        do3 = do.rearrange("(k p) t -> p k t", p=128)

    with ExitStack() as stack:
        wr16 = sbuf(stack, nc, "sbwr", (128, NK * D), BF16)
        wk16 = sbuf(stack, nc, "sbwk", (128, NK * D), BF16)
        wv16 = sbuf(stack, nc, "sbwv", (128, NK * D), BF16)
        xl = sbuf(stack, nc, "sbx", (128, NK * CW), F32)
        lnf = sbuf(stack, nc, "sbln", (128, NK * (CW + 1)), BF16)
        xr16 = sbuf(stack, nc, "sbxr", (128, NK * CW), BF16)
        xk16 = sbuf(stack, nc, "sbxk", (128, NK * CW), BF16)
        rt = sbuf(stack, nc, "sbr", (128, NK * CW), F32)
        ka16 = sbuf(stack, nc, "sbka", (128, NK * CW), BF16)
        aps = sbuf(stack, nc, "sbap", (128, CW), F32)
        cvw = sbuf(stack, nc, "sbcv", (128, VC_N), F32)
        st = alloc_ln_state(stack, nc, "sb")
        p_o = [psum(stack, nc, f"sbp{i}", (128, 512)) for i in range(2)]

        init_ln_state(p, st)
        p.dma("sync", cvw[:], vecs, sig="cvl")
        ev_cv = p.val("cvl")
        load_w16(p, rt, wr_t, D, NK, wr16)
        load_w16(p, rt, wk_t, D, NK, wk16)
        ev_w = load_w16(p, rt, wv_t, D, NK, wv16)
        ln3 = lnf.rearrange("p (k t) -> p k t", t=CW + 1)
        p.op("vector", lambda e: e.memset(ln3[:, :, 0:1], 0.0), sig="car")

        for c in range(NCH):
            ev_lx = p.dma("sync", xl.rearrange("p (k t) -> p k t", k=NK),
                          x13[:, :, c * CW:(c + 1) * CW],
                          after=[p.val("res")], sig="xld")
            ev_bc = emit_ln_stats(p, st, "sb",
                                  lambda k: xl[:, k * CW:(k + 1) * CW],
                                  NK, D, [ev_lx])
            if c > 0:
                p.op("vector", lambda e: e.tensor_copy(
                    ln3[:, :, 0:1], ln3[:, :, CW:CW + 1]))
            ev_ap = None
            for k in range(NK):
                p.op("vector", lambda e, _k=k: e.tensor_sub(
                    aps[:], xl[:, _k * CW:(_k + 1) * CW], st["mbc"][:]),
                    after=[ev_bc, p.val("mm1")] if k == 0 else [])
                ev_ap = p.op("vector", lambda e, _k=k: e.tensor_mul(
                    ln3[:, _k, 1:CW + 1], aps[:], st["rbc"][:]), sig="apl")
            ev_mx = None
            for k in range(NK):
                p.op("vector", lambda e, _k=k: e.tensor_sub(
                    aps[:], ln3[:, _k, 1:CW + 1], ln3[:, _k, 0:CW]))
                p.op("vector", lambda e, _k=k: e.scalar_tensor_tensor(
                    xr16[:, _k * CW:(_k + 1) * CW], aps[:],
                    cvw[:, VC_MUR + _k:VC_MUR + _k + 1],
                    ln3[:, _k, 0:CW], ALU.mult, ALU.add))
                ev_mx = p.op("vector", lambda e, _k=k: e.scalar_tensor_tensor(
                    xk16[:, _k * CW:(_k + 1) * CW], aps[:],
                    cvw[:, VC_MUK + _k:VC_MUK + _k + 1],
                    ln3[:, _k, 0:CW], ALU.mult, ALU.add),
                    sig="mix" if k == NK - 1 else None)
            for mo in range(NK):
                for h in range(2):
                    g = (c * NK + mo) * 2 + h
                    bank = p_o[g % 2]
                    aft = [ev_mx, ev_w]
                    if g >= 2:
                        aft.append(p.val("rev"))
                    for k in range(NK):
                        ev_mm = p.op("tensor", lambda e, _b=bank, _k=k, _m=mo, _h=h:
                                     e.matmul(_b[:],
                                              wr16[:, _k * D + _m * 128:_k * D + (_m + 1) * 128],
                                              xr16[:, _k * CW + _h * 512:_k * CW + (_h + 1) * 512],
                                              start=(_k == 0), stop=(_k == NK - 1)),
                                     after=aft if k == 0 else [],
                                     sig="mmr" if k == NK - 1 else None)
                    p.op("scalar", lambda e, _b=bank, _m=mo, _h=h: e.activation(
                        rt[:, _m * CW + _h * 512:_m * CW + (_h + 1) * 512], _b[:],
                        AF.Sigmoid), after=[ev_mm], sig="rev")
            ev_r = p.val("rev")
            for mo in range(NK):
                for h in range(2):
                    g = (c * NK + mo) * 2 + h
                    bank = p_o[g % 2]
                    aft = [ev_r]
                    if g >= 2:
                        aft.append(p.val("kev"))
                    for k in range(NK):
                        ev_mm = p.op("tensor", lambda e, _b=bank, _k=k, _m=mo, _h=h:
                                     e.matmul(_b[:],
                                              wk16[:, _k * D + _m * 128:_k * D + (_m + 1) * 128],
                                              xk16[:, _k * CW + _h * 512:_k * CW + (_h + 1) * 512],
                                              start=(_k == 0), stop=(_k == NK - 1)),
                                     after=aft if k == 0 else [],
                                     sig="mmkk" if k == NK - 1 else None)
                    ev_rl = p.op("scalar", lambda e, _b=bank: e.activation(
                        aps[:, 0:512], _b[:], AF.Relu),
                        after=[ev_mm, p.val("kev")], sig="krl")
                    p.op("scalar", lambda e, _m=mo, _h=h: e.activation(
                        ka16[:, _m * CW + _h * 512:_m * CW + (_h + 1) * 512],
                        aps[:, 0:512], AF.Square), after=[ev_rl], sig="kev")
            ev_k = p.val("kev")
            for mo in range(NK):
                for h in range(2):
                    g = (c * NK + mo) * 2 + h
                    bank = p_o[g % 2]
                    aft = [ev_k]
                    if g >= 2:
                        aft.append(p.val("vev"))
                    for k in range(NK):
                        ev_mm = p.op("tensor", lambda e, _b=bank, _k=k, _m=mo, _h=h:
                                     e.matmul(_b[:],
                                              wv16[:, _k * D + _m * 128:_k * D + (_m + 1) * 128],
                                              ka16[:, _k * CW + _h * 512:_k * CW + (_h + 1) * 512],
                                              start=(_k == 0), stop=(_k == NK - 1)),
                                     after=aft if k == 0 else [],
                                     sig="mmk" if k == NK - 1 else None)
                    sl = lambda _m=mo, _h=h: slice(_m * CW + _h * 512, _m * CW + (_h + 1) * 512)
                    ev_vv = p.op("vector", lambda e, _b=bank, _s=sl(): e.tensor_tensor(
                        rt[:, _s], rt[:, _s], _b[:], ALU.mult),
                        after=[ev_mm], sig="vev")
            ev_o = p.val("vev")
            if DBG:
                for k in range(NK):
                    p.dma("sync", dln3[:, k, c * CW:(c + 1) * CW],
                          ln3[:, k, 1:CW + 1], after=[p.val("mix")], sig="dbg")
                    p.dma("sync", dxr3[:, k, c * CW:(c + 1) * CW],
                          xr16[:, k * CW:(k + 1) * CW], after=[p.val("mix")], sig="dbg")
                    p.dma("sync", dr3[:, k, c * CW:(c + 1) * CW],
                          rt[:, k * CW:(k + 1) * CW], after=[p.val("rev")], sig="dbg")
                    p.dma("sync", dka3[:, k, c * CW:(c + 1) * CW],
                          ka16[:, k * CW:(k + 1) * CW], after=[p.val("kev")], sig="dbg")
                    p.dma("sync", do3[:, k, c * CW:(c + 1) * CW],
                          rt[:, k * CW:(k + 1) * CW], after=[ev_o], sig="dbg")
            ev_bc2 = emit_ln_stats(p, st, "sc2" if False else "sbl",
                                   lambda k: rt[:, k * CW:(k + 1) * CW],
                                   NK, D, [ev_o])
            for k in range(NK):
                p.op("vector", lambda e, _k=k: e.tensor_sub(
                    aps[:], rt[:, _k * CW:(_k + 1) * CW], st["mbc"][:]),
                    after=[ev_bc2] if k == 0 else [])
                p.op("vector", lambda e: e.tensor_mul(aps[:], aps[:], st["rbc"][:]))
                ev_x2 = p.op("vector", lambda e, _k=k: e.tensor_add(
                    rt[:, _k * CW:(_k + 1) * CW], aps[:], xl[:, _k * CW:(_k + 1) * CW]),
                    sig="res")
                p.dma("gpsimd", x23[:, k, c * CW:(c + 1) * CW],
                      rt[:, k * CW:(k + 1) * CW], after=[ev_x2], sig="stx")
            p.op("tensor", lambda e: e.nop(), after=[p.val("vev")], sig="mm1")
        p.op("gpsimd", None, after=[p.val("stx")])
    p.barrier()



SC_IO = dict(reads=["x2", "l1w_t"], writes=["x3"])


def emit_sc(p, m, T):
    nc = p.nc
    p.new_scope()
    NCH = T // CW
    x2d = m.d("x2", (D, T), F32)
    l1w_t = m.d("l1w_t", (D, H3), F32)
    x3d = m.d("x3", (D, T), F32)
    x23 = x2d.rearrange("(k p) t -> p k t", p=128)
    x33 = x3d.rearrange("(k p) t -> p k t", p=128)

    with ExitStack() as stack:
        w16 = sbuf(stack, nc, "scw", (128, NK * H3), BF16)
        xl = sbuf(stack, nc, "scx", (128, NK * CW), F32)
        ln16 = sbuf(stack, nc, "scl", (128, NK * CW), BF16)
        hall = sbuf(stack, nc, "sch", (128, NK * CW), F32)
        fk2 = [sbuf(stack, nc, f"scf{i}", (128, CW), F32) for i in range(2)]
        ik2 = [sbuf(stack, nc, f"sci{i}", (128, CW), F32) for i in range(2)]
        hk2 = [sbuf(stack, nc, f"schd{i}", (128, CW), F32) for i in range(2)]
        t1_2 = [sbuf(stack, nc, f"sct1{i}", (128, CW), F32) for i in range(2)]
        t2_2 = [sbuf(stack, nc, f"sct2{i}", (128, CW), F32) for i in range(2)]
        ak2 = [sbuf(stack, nc, f"scak{i}", (128, CW), F32) for i in range(2)]
        bk2 = [sbuf(stack, nc, f"scbk{i}", (128, CW), F32) for i in range(2)]
        aps = bk2[1]
        ccar = sbuf(stack, nc, "sccc", (128, NK), F32)
        stg = hall[:, 0:H3]
        cvw = sbuf(stack, nc, "sccv", (128, 4), F32)
        st = alloc_ln_state(stack, nc, "sc")
        p_o = [psum(stack, nc, f"scp{i}", (128, 512)) for i in range(2)]

        init_ln_state(p, st)
        p.op("vector", lambda e: e.memset(ccar[:], 0.0))
        p.op("vector", lambda e: e.memset(cvw[:, 0:1], 1.0), sig="cv1")
        ev_w = load_w16(p, stg, l1w_t, H3, NK, w16)

        for c in range(NCH):
            ev_lx = p.dma("sync", xl.rearrange("p (k t) -> p k t", k=NK),
                          x23[:, :, c * CW:(c + 1) * CW],
                          after=[p.val("res")], sig="xld")
            ev_bc = emit_ln_stats(p, st, "sc",
                                  lambda k: xl[:, k * CW:(k + 1) * CW],
                                  NK, D, [ev_lx])
            ev_ap = None
            for k in range(NK):
                p.op("vector", lambda e, _k=k: e.tensor_sub(
                    aps[:], xl[:, _k * CW:(_k + 1) * CW], st["mbc"][:]),
                    after=[ev_bc, p.val("mml")] if k == 0 else [])
                ev_ap = p.op("vector", lambda e, _k=k: e.tensor_mul(
                    ln16[:, _k * CW:(_k + 1) * CW], ln16[:, _k * CW:(_k + 1) * CW] if False else aps[:],
                    st["rbc"][:]), sig="apl" if k == NK - 1 else None)
            def emit_gates(k):
                outs = []
                for role, mo in ((fk2[k % 2], k), (ik2[k % 2], NK + k),
                                 (hk2[k % 2], 2 * NK + k)):
                    ev_e = None
                    for h in range(2):
                        bank = p_o[(mo * 2 + h) % 2]
                        aft = [ev_ap, ev_w, p.val("cv1"), p.val("gev")]
                        for kk in range(NK):
                            ev_mm = p.op("tensor", lambda e, _b=bank, _k=kk, _m=mo, _h=h:
                                         e.matmul(_b[:],
                                                  w16[:, _k * H3 + _m * 128:_k * H3 + (_m + 1) * 128],
                                                  ln16[:, _k * CW + _h * 512:_k * CW + (_h + 1) * 512],
                                                  start=(_k == 0), stop=(_k == NK - 1)),
                                         after=aft if kk == 0 else [],
                                         sig="mml" if kk == NK - 1 else None)
                        ev_e = p.op("scalar", lambda e, _b=bank, _r=role, _h=h: e.activation(
                            _r[:, _h * 512:(_h + 1) * 512], _b[:], AF.Identity),
                            after=[ev_mm, p.val("sgv")], sig="gev")
                    outs.append(ev_e)
                return outs

            def chain_steps(k, outs):
                fk = fk2[k % 2]; ik = ik2[k % 2]; hk = hk2[k % 2]
                t1 = t1_2[k % 2]; t2 = t2_2[k % 2]
                ak = ak2[k % 2]; bk = bk2[k % 2]
                ev_rf = p.op("scalar", lambda e: e.activation(t1[:], fk[:], AF.Relu,
                                                              scale=-1.0),
                             after=[outs[0]], sig="c" + str(k % 2))
                yield
                ev_af = p.op("vector", lambda e: e.scalar_tensor_tensor(
                    t2[:], fk[:], -1.0, fk[:], ALU.mult, ALU.max),
                    after=[outs[0]], sig="cv" + str(k % 2))
                yield
                ev_ef = p.op("scalar", lambda e: e.activation(t2[:], t2[:], AF.Exp,
                                                              scale=-1.0),
                             after=[ev_af], sig="c" + str(k % 2))
                ev_lf = p.op("scalar", lambda e: e.activation(t2[:], t2[:], AF.Ln,
                                                              bias=cvw[:, 0:1]),
                             sig="c" + str(k % 2))
                yield
                ev_sf = p.op("vector", lambda e: e.tensor_add(t1[:], t1[:], t2[:]),
                             after=[ev_lf], sig="cv" + str(k % 2))
                yield
                ev_ri = p.op("scalar", lambda e: e.activation(bk[:], ik[:], AF.Relu,
                                                              scale=-1.0),
                             after=[outs[1]], sig="c" + str(k % 2))
                yield
                ev_ai = p.op("vector", lambda e: e.scalar_tensor_tensor(
                    t2[:], ik[:], -1.0, ik[:], ALU.mult, ALU.max),
                    after=[outs[1], ev_sf], sig="cv" + str(k % 2))
                yield
                ev_ei = p.op("scalar", lambda e: e.activation(t2[:], t2[:], AF.Exp,
                                                              scale=-1.0),
                             after=[ev_ai], sig="c" + str(k % 2))
                ev_li = p.op("scalar", lambda e: e.activation(t2[:], t2[:], AF.Ln,
                                                              bias=cvw[:, 0:1]),
                             sig="c" + str(k % 2))
                yield
                ev_si = p.op("vector", lambda e: e.tensor_add(bk[:], bk[:], t2[:]),
                             after=[ev_li, ev_ri], sig="cv" + str(k % 2))
                yield
                ev_df = p.op("vector", lambda e: e.tensor_sub(fk[:], t1[:], bk[:]),
                             sig="dif")
                yield
                ev_a = p.op("scalar", lambda e, _a=ak: e.activation(_a[:], fk[:],
                                                                    AF.Sigmoid,
                                                                    scale=-1.0),
                            after=[ev_df], sig="aev")
                ev_sg = p.op("scalar", lambda e, _b=bk: e.activation(_b[:], hk[:],
                                                                     AF.Sigmoid),
                             after=[ev_df, outs[2]], sig="sgv")
                yield
                ev_s = p.op("vector", lambda e, _a=ak: e.tensor_scalar(
                    t1[:], _a[:], -1.0, 1.0, ALU.mult, ALU.add),
                    after=[ev_a], sig="sev")
                ev_sel = p.op("vector", lambda e, _b=bk: e.scalar_tensor_tensor(
                    t2[:], hk[:], 0.5, _b[:], ALU.add, ALU.max),
                    after=[ev_sg], sig="sel")
                ev_b = p.op("vector", lambda e, _b=bk: e.tensor_mul(_b[:], t1[:], t2[:]),
                            after=[ev_sel, ev_s], sig="bev")
                yield
                p.op("vector", lambda e, _k=k, _a=ak, _b=bk: e.tensor_tensor_scan(
                    hall[:, _k * CW:(_k + 1) * CW], _a[:], _b[:],
                    ccar[:, _k:_k + 1], ALU.mult, ALU.add),
                    after=[ev_a, ev_b], sig="scn")
                yield

            outs_k = emit_gates(0)
            gens = {}
            for k in range(NK):
                gens[k] = chain_steps(k, outs_k)
                if k + 1 < NK:
                    outs_k = emit_gates(k + 1)
                gprev = gens.get(k - 1)
                gcur = gens[k]
                done_p = gprev is None
                done_c = False
                while not (done_p and done_c):
                    if not done_p:
                        done_p = next(gprev, "END") == "END"
                    if not done_c:
                        done_c = next(gcur, "END") == "END"
            for k in range(NK):
                p.op("vector", lambda e, _k=k: e.tensor_copy(
                    ccar[:, _k:_k + 1], hall[:, (_k + 1) * CW - 1:(_k + 1) * CW]),
                    sig="car" if k == NK - 1 else None)
            ev_bc2 = emit_ln_stats(p, st, "scl",
                                   lambda k: hall[:, k * CW:(k + 1) * CW],
                                   NK, D, [p.val("scn")])
            for k in range(NK):
                p.op("vector", lambda e, _k=k: e.tensor_sub(
                    aps[:], hall[:, _k * CW:(_k + 1) * CW], st["mbc"][:]),
                    after=[ev_bc2] if k == 0 else [])
                p.op("vector", lambda e: e.tensor_mul(aps[:], aps[:], st["rbc"][:]))
                ev_x3 = p.op("vector", lambda e, _k=k: e.tensor_add(
                    hall[:, _k * CW:(_k + 1) * CW], aps[:],
                    xl[:, _k * CW:(_k + 1) * CW]), sig="res")
                p.dma("gpsimd", x33[:, k, c * CW:(c + 1) * CW],
                      hall[:, k * CW:(k + 1) * CW], after=[ev_x3], sig="stx")
        p.op("gpsimd", None, after=[p.val("stx")])
    p.barrier()


SD1_IO = dict(reads=["x3", "wg_t", "wu_t"], writes=["hh16"])


def emit_sd1(p, m, T):
    nc = p.nc
    p.new_scope()
    NCH = T // CW
    x3d = m.d("x3", (D, T), F32)
    wg_t = m.d("wg_t", (D, H3), I32)
    wu_t = m.d("wu_t", (D, H3), I32)
    hh16d = m.d("hh16", (H3, T), BF16)
    x33 = x3d.rearrange("(k p) t -> p k t", p=128)
    hh3 = hh16d.rearrange("(k p) t -> p k t", p=128)

    with ExitStack() as stack:
        wg16 = sbuf(stack, nc, "d1wg", (128, NK * H3), BF16)
        wu16 = sbuf(stack, nc, "d1wu", (128, NK * H3), BF16)
        stg = sbuf(stack, nc, "d1st", (128, H3), I32)
        xl = sbuf(stack, nc, "d1x", (128, NK * CW), F32)
        ln16 = sbuf(stack, nc, "d1l", (128, NK * CW), BF16)
        sg = [sbuf(stack, nc, f"d1s{i}", (128, 512), F32) for i in range(2)]
        hhs = [sbuf(stack, nc, f"d1h{i}", (128, 512), BF16) for i in range(2)]
        aps = sbuf(stack, nc, "d1ap", (128, CW), F32)
        st = alloc_ln_state(stack, nc, "d1")
        p_g = psum(stack, nc, "d1pg", (128, 512))
        p_u = psum(stack, nc, "d1pu", (128, 512))

        init_ln_state(p, st)
        load_w16(p, stg, wg_t, H3, NK, wg16, scale=1.0 / 127.0)
        ev_w = load_w16(p, stg, wu_t, H3, NK, wu16, scale=1.0 / 127.0)

        hst_evs = []
        for c in range(NCH):
            ev_lx = p.dma("sync", xl.rearrange("p (k t) -> p k t", k=NK),
                          x33[:, :, c * CW:(c + 1) * CW],
                          after=[p.val("mml")], sig="xld")
            ev_bc = emit_ln_stats(p, st, "d1",
                                  lambda k: xl[:, k * CW:(k + 1) * CW],
                                  NK, D, [ev_lx])
            ev_ap = None
            for k in range(NK):
                p.op("vector", lambda e, _k=k: e.tensor_sub(
                    aps[:], xl[:, _k * CW:(_k + 1) * CW], st["mbc"][:]),
                    after=[ev_bc, p.val("mml")] if k == 0 else [])
                ev_ap = p.op("vector", lambda e, _k=k: e.tensor_mul(
                    ln16[:, _k * CW:(_k + 1) * CW], aps[:], st["rbc"][:]),
                    sig="apl" if k == NK - 1 else None)
            for mo in range(NKH):
                for h in range(2):
                    g = (c * NKH + mo) * 2 + h
                    aft = [ev_ap, ev_w]
                    if g >= 1:
                        aft.append(p.val("sev"))
                    ev_mg = None
                    for kk in range(NK):
                        ev_mg = p.op("tensor", lambda e, _k=kk, _m=mo, _h=h: e.matmul(
                            p_g[:],
                            wg16[:, _k * H3 + _m * 128:_k * H3 + (_m + 1) * 128],
                            ln16[:, _k * CW + _h * 512:_k * CW + (_h + 1) * 512],
                            start=(_k == 0), stop=(_k == NK - 1)),
                            after=aft if kk == 0 else [],
                            sig="mmg" if kk == NK - 1 else None)
                    aft2 = [p.val("hev")] if g >= 1 else []
                    for kk in range(NK):
                        ev_mu = p.op("tensor", lambda e, _k=kk, _m=mo, _h=h: e.matmul(
                            p_u[:],
                            wu16[:, _k * H3 + _m * 128:_k * H3 + (_m + 1) * 128],
                            ln16[:, _k * CW + _h * 512:_k * CW + (_h + 1) * 512],
                            start=(_k == 0), stop=(_k == NK - 1)),
                            after=aft2 if kk == 0 else [],
                            sig="mml" if kk == NK - 1 else None)
                    slot = sg[g % 2]
                    aft3 = [ev_mg] + ([p.val("hev")] if g >= 2 else [])
                    ev_sl = p.op("scalar", lambda e, _s=slot: e.activation(
                        _s[:], p_g[:], AF.Silu), after=aft3, sig="sev")
                    hslot = hhs[g % 2]
                    afth = [ev_sl, ev_mu]
                    if len(hst_evs) >= 2:
                        afth.append(hst_evs[-2])
                    ev_hh = p.op("vector", lambda e, _s=slot, _o=hslot: e.tensor_tensor(
                        _o[:], _s[:], p_u[:], ALU.mult), after=afth, sig="hev")
                    hev = p.dma("gpsimd", hh3[:, mo, c * CW + h * 512:c * CW + (h + 1) * 512],
                                hslot[:], after=[ev_hh], sig="hst")
                    hst_evs.append(hev)
        p.op("gpsimd", None, after=[p.val("hst")])
    p.barrier()


SD2_IO = dict(reads=["hh16", "wd_t", "x3"], writes=["y_out"])


def emit_sd2(p, m, T):
    nc = p.nc
    p.new_scope()
    NCH = T // CW
    hh16d = m.d("hh16", (H3, T), BF16)
    wd_t = m.d("wd_t", (H3, D), I32)
    x3d = m.d("x3", (D, T), F32)
    yod = m.d("y_out", (D, T), F32)
    hh3 = hh16d.rearrange("(k p) t -> p k t", p=128)
    x33 = x3d.rearrange("(k p) t -> p k t", p=128)
    yo3 = yod.rearrange("(k p) t -> p k t", p=128)

    with ExitStack() as stack:
        wd16 = sbuf(stack, nc, "d2wd", (128, NKH * D), BF16)
        stg = sbuf(stack, nc, "d2st", (128, D), I32)
        hl2 = [sbuf(stack, nc, f"d2h{i}", (128, (NKH // 3) * CW), BF16) for i in range(3)]
        xl = sbuf(stack, nc, "d2x", (128, NK * CW), F32)
        ot = sbuf(stack, nc, "d2o", (128, NK * CW), F32)
        aps = sbuf(stack, nc, "d2ap", (128, CW), F32)
        st = alloc_ln_state(stack, nc, "d2")
        p_o = [psum(stack, nc, f"d2p{i}", (128, 512)) for i in range(2)]

        init_ln_state(p, st)
        ev_w = load_w16(p, stg, wd_t, D, NKH, wd16, scale=1.0 / 127.0)

        QH = NKH // 3
        hld_evs = []
        for c in range(NCH):
            for q in range(3):
                qi = c * 3 + q
                aftq = [hld_evs[qi - 3][1]] if qi >= 3 else []
                ev_ld = p.dma("sync", hl2[qi % 3].rearrange("p (k t) -> p k t", k=QH),
                              hh3[:, q * QH:(q + 1) * QH, c * CW:(c + 1) * CW],
                              after=aftq, sig="hld")
                hld_evs.append((ev_ld, None))
            ev_lx = p.dma("sync", xl.rearrange("p (k t) -> p k t", k=NK),
                          x33[:, :, c * CW:(c + 1) * CW],
                          after=[p.val("sty")], sig="xld")
            ev_ev = None
            reg_ev = [None] * NK
            for mo in range(NK):
                for h in range(2):
                    g = (c * NK + mo) * 2 + h
                    bank = p_o[g % 2]
                    aft0 = [ev_w]
                    if g >= 2:
                        aft0.append(p.val("oev"))
                    ev_mm = None
                    for k in range(NKH):
                        q = k // QH
                        qi = c * 3 + q
                        kq = k % QH
                        aft = (aft0 + [hld_evs[qi][0]]) if kq == 0 else []
                        ev_mm = p.op("tensor", lambda e, _b=bank, _q=qi % 3, _k=kq,
                                     _kk=k, _m=mo, _h=h:
                                     e.matmul(_b[:],
                                              wd16[:, _kk * D + _m * 128:_kk * D + (_m + 1) * 128],
                                              hl2[_q][:, _k * CW + _h * 512:_k * CW + (_h + 1) * 512],
                                              start=(_kk == 0), stop=(_kk == NKH - 1)),
                                     after=aft,
                                     sig="mm1" if k == NKH - 1 else None)
                    for q in range(3):
                        hld_evs[c * 3 + q] = (hld_evs[c * 3 + q][0], ev_mm)
                    ev_ev = p.op("vector", lambda e, _b=bank, _m=mo, _h=h: e.tensor_copy(
                        ot[:, _m * CW + _h * 512:_m * CW + (_h + 1) * 512], _b[:]),
                        after=[ev_mm], sig="oev")
                reg_ev[mo] = ev_ev
            ev_bc = emit_ln_stats(p, st, "d2",
                                  lambda k: ot[:, k * CW:(k + 1) * CW],
                                  NK, D, lambda k, _r=reg_ev: [_r[k]])
            for k in range(NK):
                p.op("vector", lambda e, _k=k: e.tensor_sub(
                    aps[:], ot[:, _k * CW:(_k + 1) * CW], st["mbc"][:]),
                    after=[ev_bc, ev_lx] if k == 0 else [])
                p.op("vector", lambda e: e.tensor_mul(aps[:], aps[:], st["rbc"][:]))
                ev_y = p.op("vector", lambda e, _k=k: e.tensor_add(
                    ot[:, _k * CW:(_k + 1) * CW], aps[:],
                    xl[:, _k * CW:(_k + 1) * CW]), sig="res")
                p.dma("gpsimd", yo3[:, k, c * CW:(c + 1) * CW],
                      ot[:, k * CW:(k + 1) * CW], after=[ev_y], sig="sty")
        p.op("gpsimd", None, after=[p.val("sty")])
    p.barrier()


STAGES = {"pf1": (emit_pf1, PF1_IO),
          "scf": (lambda p, m, T: emit_scan(p, m, T, False), SCF_IO),
          "scr": (lambda p, m, T: emit_scan(p, m, T, True), SCR_IO),
          "po1": (emit_po1, PO1_IO),
          "po2": (emit_po2, PO2_IO),
          "sb": (emit_sb, SB_IO),
          "sc": (emit_sc, SC_IO),
          "sd1": (emit_sd1, SD1_IO),
          "sd2": (emit_sd2, SD2_IO)}


def build_stage(name, T):
    emit, io = STAGES[name]
    io = dict(io)
    if os.environ.get("ATH_SB_DEBUG") and name == "sb":
        io["writes"] = list(io["writes"]) + ["sb_ln", "sb_xr", "sb_r", "sb_ka", "sb_o"]
    if os.environ.get("ATH_SCAN_DEBUG") and name in ("scf", "scr"):
        pref = "r" if name == "scr" else "f"
        io["writes"] = list(io["writes"]) + [pref + "ddump", pref + "ydump",
                                             pref + "xc32", pref + "Brow", pref + "Crow"]
    nc = bass.Bass("TRN2", target_bir_lowering=False)
    m = Mem(nc, fused=False, reads=io["reads"], writes=io["writes"])
    p = Prog(nc)
    emit(p, m, T)
    p.emit()
    return nc


def run_stage(name, T, in_maps, core_ids=None):
    nc = build_stage(name, T)
    if core_ids is None:
        core_ids = list(range(len(in_maps)))
    res = run_bass_kernel_spmd(nc, in_maps, core_ids=core_ids)
    return res.results


CHAIN = [
    ("pf1", ["x_fm", "w_in"]),
    ("scf", ["xc_pre", "silu_z", "w_xp", "w_dt", "vecs", "alog_t", "eye16"]),
    ("scr", ["xc_pre", "silu_z", "yF2", "w_xp", "w_dt", "vecs", "alog_t", "eye16"]),
    ("po1", ["ysum16", "w_out", "w_ff1", "vecs"]),
    ("po2", ["g16", "w_ff2", "x_fm", "vecs"]),
    ("sb", ["x1", "wr_t", "wk_t", "wv_t", "vecs"]),
    ("sc", ["x2", "l1w_t"]),
    ("sd1", ["x3", "wg_t", "wu_t"]),
    ("sd2", ["hh16", "wd_t", "x3"]),
]


MODULE_INPUTS = ["x_fm", "w_in", "w_xp", "w_dt", "vecs", "alog_t", "eye16",
                 "w_out", "w_ff1", "w_ff2", "wr_t", "wk_t", "wv_t", "l1w_t",
                 "wg_t", "wu_t", "wd_t"]


def build_fused(T):
    nc = bass.Bass("TRN2", target_bir_lowering=False)
    m = Mem(nc, fused=True, reads=MODULE_INPUTS, writes=["y_out"])
    p = Prog(nc)
    for name, _ in CHAIN:
        STAGES[name][0](p, m, T)
    p.emit()
    return nc


def kernel(**inputs):
    x = np.asarray(inputs["L0_attn_x"])
    B, T, _ = x.shape
    xs, shared = host_prep(inputs, T)
    if os.environ.get("ATH_UNFUSED"):
        state = [dict(shared, x_fm=xs[b]) for b in range(B)]
        for name, reads in CHAIN:
            in_maps = [{r: state[b][r] for r in reads} for b in range(B)]
            res = run_stage(name, T, in_maps, core_ids=list(range(B)))
            for b in range(B):
                for k2, v in res[b].items():
                    state[b][k2] = np.asarray(v)
        return np.stack([state[b]["y_out"].T for b in range(B)]).astype(np.float32)
    nc = build_fused(T)
    in_maps = [dict(shared, x_fm=xs[b]) for b in range(B)]
    res = run_bass_kernel_spmd(nc, in_maps, core_ids=list(range(B)))
    return np.stack([np.asarray(r["y_out"]).T for r in res.results]).astype(np.float32)
